# version 02
# baseline (speedup 1.0000x reference)
import contextlib
import numpy as np
import concourse.bass as bass
import concourse.mybir as mybir

F32 = mybir.dt.float32
BF16 = mybir.dt.bfloat16
I8 = mybir.dt.int8
U8 = mybir.dt.uint8
I32 = mybir.dt.int32
F16 = mybir.dt.float16
AF = mybir.ActivationFunctionType
ALU = mybir.AluOpType
AX = mybir.AxisListType

DMA_ENGINES = ("sp", "pool")


import os
HEAVY_SYNC = os.environ.get("FW_HEAVY_SYNC", "0") == "1"
HEAVY_SRC_DEFAULT = "pe"


class Prog:
    def __init__(self, nc):
        self.nc = nc
        self.ops = []

    def add(self, engine, emit, reads=(), writes=()):
        self.ops.append([engine, emit, tuple(reads), tuple(writes), False])

    def dma(self, queue, out, in_, reads=(), writes=()):
        self.add(queue, lambda e: e.dma_start(out=out, in_=in_), reads, writes)

    def build_and_emit(self):
        nc = self.nc
        n = len(self.ops)
        deps = [set() for _ in range(n)]
        last_writer = {}
        readers = {}
        for i, (eng, emit, rd, wr, _) in enumerate(self.ops):
            for r in rd:
                if r in last_writer:
                    deps[i].add(last_writer[r])
            for w in wr:
                if w in last_writer:
                    deps[i].add(last_writer[w])
                for j in readers.get(w, ()):
                    if j != i:
                        deps[i].add(j)
            for r in rd:
                readers.setdefault(r, []).append(i)
            for w in wr:
                last_writer[w] = i
                readers[w] = []
        same_eng_dep = [set() for _ in range(n)]
        cross_dep = [set() for _ in range(n)]
        for op in self.ops:
            if op[0] in DMA_ENGINES:
                op[4] = True
        for i, (eng, *_rest) in enumerate(self.ops):
            for d in deps[i]:
                deng = self.ops[d][0]
                if deng == eng and eng not in DMA_ENGINES:
                    if eng in ("dve", "act"):
                        same_eng_dep[i].add(d)
                else:
                    if deng == eng:
                        cross_dep[i].add(d)
                    else:
                        cross_dep[i].add(d)
                    self.ops[d][4] = True
        if HEAVY_SYNC:
            for op in self.ops:
                op[4] = True
            last_by_eng = {}
            for i, op in enumerate(self.ops):
                for e, j in last_by_eng.items():
                    if e != op[0]:
                        cross_dep[i].add(j)
                last_by_eng[op[0]] = i
        elif os.environ.get("FW_HEAVY_SRC", HEAVY_SRC_DEFAULT):
            srcs = os.environ.get("FW_HEAVY_SRC", HEAVY_SRC_DEFAULT).split(",")
            last_by_eng = {}
            for i, op in enumerate(self.ops):
                for e in srcs:
                    j = last_by_eng.get(e)
                    if j is not None and e != op[0]:
                        cross_dep[i].add(j)
                        self.ops[j][4] = True
                last_by_eng[op[0]] = i
        inc_rank = {}
        cnt = {}
        for i, op in enumerate(self.ops):
            if op[4]:
                c = cnt.get(op[0], 0) + 1
                cnt[op[0]] = c
                inc_rank[i] = c
        with contextlib.ExitStack() as es:
            sems = {}
            for e in ("pe", "dve", "act", "sp", "pool"):
                sems[e] = es.enter_context(nc.semaphore(f"sem_{e}"))
            block = es.enter_context(nc.Block())
            eng_objs = {}

            def make_section(eng_name, eng_ops):
                def section(eng):
                    clock = {}
                    last_drained = -1
                    for i in eng_ops:
                        _, emit, rd, wr, needs_inc = self.ops[i]
                        if any(d >= last_drained for d in same_eng_dep[i]):
                            eng.drain()
                            last_drained = i
                        want = {}
                        for d in cross_dep[i]:
                            deng = self.ops[d][0]
                            step = 16 if deng in DMA_ENGINES else 1
                            v = inc_rank[d] * step
                            if v > want.get(deng, 0):
                                want[deng] = v
                        for deng, v in want.items():
                            if clock.get(deng, 0) < v:
                                eng.wait_ge(sems[deng], v)
                                clock[deng] = v
                        inst = emit(eng)
                        if needs_inc:
                            step = 16 if self.ops[i][0] in DMA_ENGINES else 1
                            if eng_name in ("dve", "act"):
                                eng.drain().then_inc(sems[eng_name], step)
                                last_drained = i
                            else:
                                inst.then_inc(sems[self.ops[i][0]], step)
                    return
                return section

            by_eng = {}
            for i, op in enumerate(self.ops):
                by_eng.setdefault(op[0], []).append(i)

            finals = {e: cnt.get(e, 0) * (16 if e in DMA_ENGINES else 1)
                      for e in ("pe", "dve", "act", "sp", "pool")}

            if "sp" in by_eng:
                @block.sync
                def _(sync):
                    make_section("sp", by_eng["sp"])(sync)

            if "pool" in by_eng or True:
                @block.gpsimd
                def _(gpsimd):
                    make_section("pool", by_eng.get("pool", []))(gpsimd)
                    for e, v in finals.items():
                        if e != "pool" and v > 0:
                            gpsimd.wait_ge(sems[e], v)
                    if finals["pool"] > 0:
                        gpsimd.wait_ge(sems["pool"], finals["pool"])

            if "act" in by_eng:
                @block.scalar
                def _(scalar):
                    make_section("act", by_eng["act"])(scalar)

            if "dve" in by_eng:
                @block.vector
                def _(vector):
                    make_section("dve", by_eng["dve"])(vector)

            if "pe" in by_eng:
                @block.tensor
                def _(tensor):
                    make_section("pe", by_eng["pe"])(tensor)
        return


def mark_output_dmas(prog):
    pass





def build_l0ffn(pr, nc, h, name="f0", eps=1e-5, B0_D=2048, B0_I=8192, B0_T=4096):
    es = h["es"]
    sb = lambda nm, shp, dt: es.enter_context(nc.sbuf_tensor(f"{name}_{nm}", shp, dt))
    ps = lambda nm, shp, dt: es.enter_context(nc.psum_tensor(f"{name}_{nm}", shp, dt))
    NT = B0_I // 128
    KT = B0_D // 128
    OT = B0_D // 128
    TC = 512
    NCH = B0_T // TC

    x1, wg_t, wu_t, wd_t, sg_c, su_c, sd_c, ident, x2 = (
        h["x1"], h["wg_t"], h["wu_t"], h["wd_t"], h["sg_c"], h["su_c"],
        h["sd_c"], h["identf"], h["x2"])

    xa = sb("xa", [128, B0_D], F32)
    xb = sb("xb", [128, B0_D], F32)
    xTc = sb("xTc", [128, KT * TC], BF16)
    wg_sb = [sb(f"wg{i}", [128, KT * 128], BF16) for i in range(2)]
    wu_sb = [sb(f"wu{i}", [128, KT * 128], BF16) for i in range(2)]
    wd_sb = [sb(f"wd{i}", [128, NT * 128], BF16) for i in range(2)]
    tg = sb("tg", [128, TC], F32)
    tu = sb("tu", [128, TC], F32)
    fuse = sb("fuse", [128, NT * TC], BF16)
    yt = sb("yt", [128, TC], F32)
    ytok = [sb(f"ytok{i}", [128, B0_D], F32) for i in range(4)]
    acc = sb("acc", [128, B0_D], F32)
    sq = sb("sq", [128, B0_D], F32)
    st1 = sb("st1", [128, 8], F32)
    sgc = sb("sgc", [128, NT], F32)
    suc = sb("suc", [128, NT], F32)
    sdc = sb("sdc", [128, OT], F32)
    idf = sb("idf", [128, 128], F32)
    otile = sb("otile", [128, B0_D], F32)

    pg = ps("pg", [128, 512], F32)
    pu = ps("pu", [128, 512], F32)
    pd = ps("pd", [128, 512], F32)
    pt = [ps(f"pt{i}", [128, 128], F32) for i in range(4)]

    pr.dma("sp", sgc[:, :], sg_c[:, :], writes=("sgc",))
    pr.dma("sp", suc[:, :], su_c[:, :], writes=("suc",))
    pr.dma("sp", sdc[:, :], sd_c[:, :], writes=("sdc",))
    pr.dma("sp", idf[:, :], ident[:, :], writes=("idf",))

    for c in range(NCH):
        for tt in range(4):
            trow = c * TC + tt * 128
            pr.dma("sp", xa[:, :], x1[trow:trow + 128, :], writes=("xa",))
            for kt in range(KT):
                b = kt % 4
                pr.add("pe", (lambda xa=xa, kt=kt, b=b:
                              lambda e: e.transpose(pt[b][:, :], xa[:, kt * 128:(kt + 1) * 128], idf[:, :]))(),
                       reads=("xa", "idf"), writes=(f"pt{b}",))
                dst = xTc[:, kt * TC + tt * 128: kt * TC + (tt + 1) * 128]
                pr.add("act", (lambda dst=dst, b=b:
                               lambda e: e.activation(out=dst, in_=pt[b][:, :], func=AF.Copy))(),
                       reads=(f"pt{b}",), writes=("xTc",))
        for nt in range(NT):
            bw = nt % 2
            pr.dma("pool", wg_sb[bw][:, :], wg_t[nt], writes=(f"wg{bw}",))
            pr.dma("pool", wu_sb[bw][:, :], wu_t[nt], writes=(f"wu{bw}",))
            for kt in range(KT):
                pr.add("pe", (lambda bw=bw, kt=kt:
                              lambda e: e.matmul(pg[:, :], wg_sb[bw][:, kt * 128:(kt + 1) * 128],
                                                 xTc[:, kt * TC:(kt + 1) * TC],
                                                 start=(kt == 0), stop=(kt == KT - 1)))(),
                       reads=(f"wg{bw}", "xTc"), writes=("pg",))
            for kt in range(KT):
                pr.add("pe", (lambda bw=bw, kt=kt:
                              lambda e: e.matmul(pu[:, :], wu_sb[bw][:, kt * 128:(kt + 1) * 128],
                                                 xTc[:, kt * TC:(kt + 1) * TC],
                                                 start=(kt == 0), stop=(kt == KT - 1)))(),
                       reads=(f"wu{bw}", "xTc"), writes=("pu",))
            pr.add("act", (lambda nt=nt:
                           lambda e: e.activation(out=tg[:, :], in_=pg[:, :], func=AF.Silu,
                                                  scale=sgc[:, nt:nt + 1]))(),
                   reads=("pg", "sgc"), writes=("tg",))
            pr.add("act", (lambda nt=nt:
                           lambda e: e.activation(out=tu[:, :], in_=pu[:, :], func=AF.Copy,
                                                  scale=suc[:, nt:nt + 1]))(),
                   reads=("pu", "suc"), writes=("tu",))
            pr.add("dve", (lambda nt=nt:
                           lambda e: e.tensor_mul(fuse[:, nt * TC:(nt + 1) * TC], tg[:, :], tu[:, :]))(),
                   reads=("tg", "tu"), writes=("fuse",))
        for ot in range(OT):
            bw = ot % 2
            pr.dma("pool", wd_sb[bw][:, :], wd_t[ot], writes=(f"wd{bw}",))
            for k2 in range(NT):
                pr.add("pe", (lambda bw=bw, k2=k2:
                              lambda e: e.matmul(pd[:, :], wd_sb[bw][:, k2 * 128:(k2 + 1) * 128],
                                                 fuse[:, k2 * TC:(k2 + 1) * TC],
                                                 start=(k2 == 0), stop=(k2 == NT - 1)))(),
                       reads=(f"wd{bw}", "fuse"), writes=("pd",))
            pr.add("act", (lambda ot=ot:
                           lambda e: e.activation(out=yt[:, :], in_=pd[:, :], func=AF.Copy,
                                                  scale=sdc[:, ot:ot + 1]))(),
                   reads=("pd", "sdc"), writes=("yt",))
            for tt in range(4):
                b = tt % 4
                pr.add("pe", (lambda tt=tt, b=b:
                              lambda e: e.transpose(pt[b][:, :], yt[:, tt * 128:(tt + 1) * 128], idf[:, :]))(),
                       reads=("yt", "idf"), writes=(f"pt{b}",))
                dst = ytok[tt][:, ot * 128:(ot + 1) * 128]
                pr.add("act", (lambda dst=dst, b=b:
                               lambda e: e.activation(out=dst, in_=pt[b][:, :], func=AF.Copy))(),
                       reads=(f"pt{b}",), writes=(f"ytok{tt}",))
        for tt in range(4):
            trow = c * TC + tt * 128
            pr.dma("sp", xb[:, :], x1[trow:trow + 128, :], writes=("xb",))
            pr.add("dve", (lambda tt=tt: lambda e: e.tensor_add(acc[:, :], ytok[tt][:, :], xb[:, :]))(),
                   reads=(f"ytok{tt}", "xb"), writes=("acc",))
            emit_ln(pr, nc, acc, sq, st1, otile, eps, B0_D)
            pr.dma("pool", x2[trow:trow + 128, :], otile[:, :], reads=("otile",))
    return


def emit_ln(pr, nc, acc, sq, st1, otile, eps, B0_D):
    pr.add("act", lambda e: e.activation(out=sq[:, :], in_=acc[:, :], func=AF.Square),
           reads=("acc",), writes=("sq",))
    pr.add("dve", lambda e: e.reduce_sum(st1[:, 0:1], acc[:, :], axis=AX.X),
           reads=("acc",), writes=("st1",))
    pr.add("dve", lambda e: e.reduce_sum(st1[:, 1:2], sq[:, :], axis=AX.X),
           reads=("sq",), writes=("st1",))
    pr.add("dve", lambda e: e.tensor_scalar_mul(st1[:, 2:3], st1[:, 0:1], 1.0 / B0_D),
           reads=("st1",), writes=("st1",))
    pr.add("dve", lambda e: e.tensor_scalar_mul(st1[:, 3:4], st1[:, 1:2], 1.0 / B0_D),
           reads=("st1",), writes=("st1",))
    pr.add("dve", lambda e: e.scalar_tensor_tensor(st1[:, 4:5], st1[:, 2:3], -1.0,
                                                   st1[:, 2:3], ALU.mult, ALU.mult),
           reads=("st1",), writes=("st1",))
    pr.add("dve", lambda e: e.tensor_add(st1[:, 4:5], st1[:, 4:5], st1[:, 3:4]),
           reads=("st1",), writes=("st1",))
    pr.add("dve", lambda e: e.tensor_scalar_add(st1[:, 4:5], st1[:, 4:5], float(eps)),
           reads=("st1",), writes=("st1",))
    pr.add("act", lambda e: e.activation(out=st1[:, 5:6], in_=st1[:, 4:5], func=AF.Ln),
           reads=("st1",), writes=("st1",))
    pr.add("dve", lambda e: e.tensor_scalar_mul(st1[:, 5:6], st1[:, 5:6], -0.5),
           reads=("st1",), writes=("st1",))
    pr.add("act", lambda e: e.activation(out=st1[:, 6:7], in_=st1[:, 5:6], func=AF.Exp),
           reads=("st1",), writes=("st1",))
    pr.add("dve", lambda e: e.scalar_tensor_tensor(st1[:, 7:8], st1[:, 2:3], -1.0,
                                                   st1[:, 6:7], ALU.mult, ALU.mult),
           reads=("st1",), writes=("st1",))
    pr.add("act", lambda e: e.activation(out=otile[:, :], in_=acc[:, :], func=AF.Identity,
                                         scale=st1[:, 6:7], bias=st1[:, 7:8]),
           reads=("acc", "st1"), writes=("otile",))


def emit_ln2x(pr, nc, acc, acc_res, sq, st1, otile, ot_res, eps, B0_D, sq_res="sq"):
    return _emit_ln_core(pr, nc, acc, acc_res, sq, st1, otile, ot_res, eps, B0_D, sq_res)


def emit_ln2(pr, nc, acc, sq, st1, otile, ot_res, eps, B0_D):
    return _emit_ln_core(pr, nc, acc, "acc", sq, st1, otile, ot_res, eps, B0_D, "sq")


def _emit_ln_core(pr, nc, acc, acc_res, sq, st1, otile, ot_res, eps, B0_D, sq_res="sq"):
    pr.add("act", lambda e: e.activation(out=sq[:, :], in_=acc[:, :], func=AF.Square),
           reads=(acc_res,), writes=(sq_res,))
    pr.add("dve", lambda e: e.reduce_sum(st1[:, 0:1], acc[:, :], axis=AX.X),
           reads=(acc_res,), writes=("st1",))
    pr.add("dve", lambda e: e.reduce_sum(st1[:, 1:2], sq[:, :], axis=AX.X),
           reads=(sq_res,), writes=("st1",))
    pr.add("dve", lambda e: e.tensor_scalar_mul(st1[:, 2:3], st1[:, 0:1], 1.0 / B0_D),
           reads=("st1",), writes=("st1",))
    pr.add("dve", lambda e: e.tensor_scalar_mul(st1[:, 3:4], st1[:, 1:2], 1.0 / B0_D),
           reads=("st1",), writes=("st1",))
    pr.add("dve", lambda e: e.scalar_tensor_tensor(st1[:, 4:5], st1[:, 2:3], -1.0,
                                                   st1[:, 2:3], ALU.mult, ALU.mult),
           reads=("st1",), writes=("st1",))
    pr.add("dve", lambda e: e.tensor_add(st1[:, 4:5], st1[:, 4:5], st1[:, 3:4]),
           reads=("st1",), writes=("st1",))
    pr.add("dve", lambda e: e.tensor_scalar_add(st1[:, 4:5], st1[:, 4:5], float(eps)),
           reads=("st1",), writes=("st1",))
    pr.add("act", lambda e: e.activation(out=st1[:, 5:6], in_=st1[:, 4:5], func=AF.Ln),
           reads=("st1",), writes=("st1",))
    pr.add("dve", lambda e: e.tensor_scalar_mul(st1[:, 5:6], st1[:, 5:6], -0.5),
           reads=("st1",), writes=("st1",))
    pr.add("act", lambda e: e.activation(out=st1[:, 6:7], in_=st1[:, 5:6], func=AF.Exp),
           reads=("st1",), writes=("st1",))
    pr.add("dve", lambda e: e.scalar_tensor_tensor(st1[:, 7:8], st1[:, 2:3], -1.0,
                                                   st1[:, 6:7], ALU.mult, ALU.mult),
           reads=("st1",), writes=("st1",))
    pr.add("act", lambda e: e.activation(out=otile[:, :], in_=acc[:, :], func=AF.Identity,
                                         scale=st1[:, 6:7], bias=st1[:, 7:8]),
           reads=(acc_res, "st1"), writes=(ot_res,))




A0_H, A0_G, A0_DH, A0_D, A0_S = 16, 4, 128, 2048, 2048
A0_EPSR = 1e-6


def build_l0attn(pr, nc, h, name="a0"):
    es = h["es"]
    sb = lambda nm, shp, dt: es.enter_context(nc.sbuf_tensor(f"{name}_{nm}", shp, dt))
    ps = lambda nm, shp, dt: es.enter_context(nc.psum_tensor(f"{name}_{nm}", shp, dt))

    x0, x1 = h["x0"], h["x1"]
    KT = A0_D // 128

    xTb = sb("xTb", [128, KT * A0_S], BF16)
    attnT = sb("attnT", [128, A0_H * A0_S], BF16)
    wslab = [sb("wslab0", [128, A0_D], BF16)]
    xa = sb("xa", [128, A0_D], F32)
    qraw = sb("qraw", [128, A0_S], F32)
    rows = sb("rows", [1, A0_S], F32)
    b2hi = sb("b2hi", [1, A0_S], BF16)
    b2lo = sb("b2lo", [1, A0_S], BF16)
    qn = sb("qn", [128, A0_S], BF16)
    ktb = [sb(f"ktb{g}", [128, A0_S], BF16) for g in range(1)]
    vrow = sb("vrow", [128, A0_S], BF16)
    t1 = sb("t1", [128, 512], BF16)
    qnr = sb("qnr", [128, A0_S], BF16)
    probsT = sb("probsT", [128, A0_S], BF16)
    colv = sb("colv", [128, 8], F32)
    ysb = qraw
    st1 = sb("st1", [128, 8], F32)
    bohi = sb("bohi", [1, A0_D], BF16)
    bolo = sb("bolo", [1, A0_D], BF16)
    bqc = sb("bqc", [128, A0_H], F32)
    bkc = sb("bkc", [128, A0_G], F32)
    bvc = sb("bvc", [128, A0_G], F32)
    qnsc = sb("qnsc", [128, 1], F32)
    knsc = sb("knsc", [128, 1], F32)
    cost = sb("cost", [128, A0_S], BF16)
    sint = sb("sint", [128, A0_S], BF16)
    pswap = sb("pswap", [128, 128], BF16)
    cmsk = sb("cmsk", [128, 128], BF16)
    cmskT = sb("cmskT", [128, 4 * 512], BF16)
    idf = sb("idf", [128, 128], F32)
    onec = sb("onec", [128, 1], BF16)
    oner = sb("oner", [1, 128], F32)
    onerb = sb("onerb", [1, 128], BF16)

    pqkv = ps("pqkv", [128, 512], F32)
    pstat = ps("pstat", [128, 512], F32)
    pT = ps("pT", [128, 512], F32)
    pv = [ps(f"pv{i}", [128, 512], F32) for i in range(4)]
    psm = ps("psm", [128, 512], F32)

    for nm, src in [("bqc", h["bq_c"]), ("bkc", h["bk_c"]), ("bvc", h["bv_c"]),
                    ("qnsc", h["qns_c"]), ("knsc", h["kns_c"]),
                    ("cost", h["costab"]), ("sint", h["sintab"]), ("pswap", h["pswap"]),
                    ("cmsk", h["cmask"]), ("idf", h["identf"]),
                    ("onec", h["ones_col_b"]), ("oner", h["ones_row_f"]),
                    ("onerb", h["ones_row_b"])]:
        dst = locals()[nm]
        pr.dma("sp", dst[:, :], src[:, :], writes=(nm,))
    pr.dma("sp", cmskT[:, :], h["cmaskT"][:, :], writes=("cmskT",))

    pr.dma("sp", rows[0:1, :], h["bo_r"][0:1, :], writes=("rows",))
    pr.add("dve", lambda e: e.tensor_copy(bohi[0:1, :], rows[0:1, :]),
           reads=("rows",), writes=("bohi",))
    pr.add("dve", lambda e: e.tensor_sub(bolo[0:1, :], rows[0:1, :], bohi[0:1, :]),
           reads=("rows", "bohi"), writes=("bolo",))

    def transpose_to(dst_ap, src_sb_ap, src_res, dst_res, pbank, pres):
        pr.add("pe", lambda e: e.transpose(pbank[:, 0:128], src_sb_ap, idf[:, :]),
               reads=(src_res, "idf"), writes=(pres,))
        pr.add("act", lambda e: e.activation(out=dst_ap, in_=pbank[:, 0:128], func=AF.Copy),
               reads=(pres,), writes=(dst_res,))

    def project(dst_sb, dst_res, wt_ap, wt_res_idx, bias_col, bias_res, out_bf=False):
        pass

    for b in range(2):
        base = b * A0_S
        for tt in range(16):
            pr.dma("sp", xa[:, :], x0[base + tt * 128: base + (tt + 1) * 128, :], writes=("xa",))
            for kt in range(KT):
                transpose_to(xTb[:, kt * A0_S + tt * 128: kt * A0_S + (tt + 1) * 128],
                             xa[:, kt * 128:(kt + 1) * 128], "xa", "xTb", pstat, "pstat")

        for g in range(A0_G):
            gbuf = 0
            pr.dma("pool", wslab[gbuf][:, :], h["wk_t"][g], writes=(f"wslab{gbuf}",))
            _proj_norm_rope(pr, nc, locals(), gbuf, bkc[:, g:g + 1], knsc, ktb[0], "ktb0",
                            fold_hs=False, name=f"k{b}{g}")
            vbuf = 0
            pr.dma("pool", wslab[vbuf][:, :], h["wv_t"][g], writes=(f"wslab{vbuf}",))
            for ch in range(4):
                for kt in range(KT):
                    pr.add("pe", (lambda vbuf=vbuf, ch=ch, kt=kt:
                                  lambda e: e.matmul(pqkv[:, :], wslab[vbuf][:, kt * 128:(kt + 1) * 128],
                                                     xTb[:, kt * A0_S + ch * 512: kt * A0_S + (ch + 1) * 512],
                                                     start=(kt == 0), stop=(kt == KT - 1)))(),
                           reads=(f"wslab{vbuf}", "xTb"), writes=("pqkv",))
                pr.add("act", (lambda ch=ch, g=g:
                               lambda e: e.activation(out=qraw[:, ch * 512:(ch + 1) * 512], in_=pqkv[:, :],
                                                      func=AF.Identity, bias=bvc[:, g:g + 1]))(),
                       reads=("pqkv", "bvc"), writes=("qraw",))
            for st in range(16):
                transpose_to(vrow[:, st * 128:(st + 1) * 128], qraw[:, st * 128:(st + 1) * 128],
                             "qraw", "vrow", pstat, "pstat")

            for hh in range(4):
                head = g * 4 + hh
                qbuf = 0
                pr.dma("pool", wslab[qbuf][:, :], h["wq_t"][head], writes=(f"wslab{qbuf}",))
                _proj_norm_rope(pr, nc, locals(), qbuf, bqc[:, head:head + 1], qnsc, qn, "qn",
                                fold_hs=True, name=f"q{b}{head}")
                _attention(pr, nc, locals(), head)

        for kt in range(KT):
            pr.dma("pool", xTb[:, kt * A0_S:(kt + 1) * A0_S], h["wo_t"][kt], writes=("xTb",))
        for tt in range(16):
            pr.dma("sp", xa[:, :], x0[base + tt * 128: base + (tt + 1) * 128, :], writes=("xa",))
            for och in range(4):
                for kt in range(KT):
                    pr.add("pe", (lambda tt=tt, och=och, kt=kt:
                                  lambda e: e.matmul(pqkv[:, :],
                                                     attnT[:, kt * A0_S + tt * 128: kt * A0_S + (tt + 1) * 128],
                                                     xTb[:, kt * A0_S + och * 512: kt * A0_S + (och + 1) * 512],
                                                     start=(kt == 0), stop=False))(),
                           reads=("attnT", "xTb"), writes=("pqkv",))
                pr.add("pe", (lambda och=och:
                              lambda e: e.matmul(pqkv[:, :], onerb[0:1, :],
                                                 bohi[0:1, och * 512:(och + 1) * 512],
                                                 start=False, stop=False))(),
                       reads=("onerb", "bohi"), writes=("pqkv",))
                pr.add("pe", (lambda och=och:
                              lambda e: e.matmul(pqkv[:, :], onerb[0:1, :],
                                                 bolo[0:1, och * 512:(och + 1) * 512],
                                                 start=False, stop=True))(),
                       reads=("onerb", "bolo"), writes=("pqkv",))
                pr.add("act", (lambda och=och:
                               lambda e: e.activation(out=ysb[:, och * 512:(och + 1) * 512],
                                                      in_=pqkv[:, :], func=AF.Copy))(),
                       reads=("pqkv",), writes=("qraw",))
            pr.add("dve", lambda e: e.tensor_add(xa[:, :], ysb[:, :], xa[:, :]),
                   reads=("qraw", "xa"), writes=("xa",))
            emit_ln2x(pr, nc, xa, "xa", probsT, st1, ysb, "qraw", 1e-5, A0_D, sq_res="probsT")
            pr.dma("pool", x1[base + tt * 128: base + (tt + 1) * 128, :], ysb[:, :], reads=("qraw",))
    return


def _proj_norm_rope(pr, nc, L, wbuf, bias_ap, ns_col, dst, dst_res, fold_hs, name):
    pqkv, psm, wslab, xTb, qraw, rows, onec, oner, idf = (
        L["pqkv"], L["psm"], L["wslab"], L["xTb"], L["qraw"], L["rows"],
        L["onec"], L["oner"], L["idf"])
    cost, sint, t1 = L["cost"], L["sint"], L["t1"]
    qnr = L["qnr"]
    KT = 16
    S_ = A0_S
    nsres = "qnsc" if ns_col is L["qnsc"] else "knsc"
    bres = "bqc" if fold_hs else "bkc"
    for ch in range(4):
        for kt in range(KT):
            pr.add("pe", (lambda wbuf=wbuf, ch=ch, kt=kt:
                          lambda e: e.matmul(pqkv[:, :], wslab[wbuf][:, kt * 128:(kt + 1) * 128],
                                             xTb[:, kt * S_ + ch * 512: kt * S_ + (ch + 1) * 512],
                                             start=(kt == 0), stop=(kt == KT - 1)))(),
                   reads=(f"wslab{wbuf}", "xTb"), writes=("pqkv",))
        pr.add("act", (lambda ch=ch:
                       lambda e: e.activation(out=qraw[:, ch * 512:(ch + 1) * 512], in_=pqkv[:, :],
                                              func=AF.Identity, bias=bias_ap))(),
               reads=("pqkv", bres), writes=("qraw",))
    sqt = L["probsT"]
    pr.add("act", lambda e: e.activation(out=sqt[:, :], in_=qraw[:, :], func=AF.Square),
           reads=("qraw",), writes=("probsT",))
    for ch in range(4):
        pr.add("pe", (lambda ch=ch: lambda e: e.matmul(psm[0:1, :], onec[:, 0:1],
                                                       sqt[:, ch * 512:(ch + 1) * 512],
                                                       start=True, stop=True))(),
               reads=("onec", "probsT"), writes=("psm",))
        pr.add("act", (lambda ch=ch: lambda e: e.activation(out=rows[0:1, ch * 512:(ch + 1) * 512],
                                                            in_=psm[0:1, :], func=AF.Copy))(),
               reads=("psm",), writes=("rows",))
    pr.add("dve", lambda e: e.tensor_scalar(rows[0:1, 0:S_], rows[0:1, 0:S_],
                                            1.0 / 128, float(A0_EPSR), ALU.mult, ALU.add),
           reads=("rows",), writes=("rows",))
    pr.add("act", lambda e: e.activation(out=rows[0:1, 0:S_], in_=rows[0:1, 0:S_], func=AF.Ln),
           reads=("rows",), writes=("rows",))
    pr.add("dve", lambda e: e.tensor_scalar_mul(rows[0:1, 0:S_], rows[0:1, 0:S_], -0.5),
           reads=("rows",), writes=("rows",))
    pr.add("act", lambda e: e.activation(out=rows[0:1, 0:S_], in_=rows[0:1, 0:S_], func=AF.Exp),
           reads=("rows",), writes=("rows",))
    if fold_hs:
        pr.add("dve", lambda e: e.tensor_scalar_mul(rows[0:1, 0:S_], rows[0:1, 0:S_], 1.0 / 128),
               reads=("rows",), writes=("rows",))
    for ch in range(4):
        pr.add("pe", (lambda ch=ch: lambda e: e.matmul(psm[:, :], oner[0:1, :],
                                                       rows[0:1, ch * 512:(ch + 1) * 512],
                                                       start=True, stop=True))(),
               reads=("oner", "rows"), writes=("psm",))
        pr.add("dve", (lambda ch=ch: lambda e: e.scalar_tensor_tensor(
            qnr[:, ch * 512:(ch + 1) * 512], qraw[:, ch * 512:(ch + 1) * 512], ns_col[:, 0:1],
            psm[:, :], ALU.mult, ALU.mult))(),
               reads=("qraw", nsres, "psm"), writes=("qnr",))
    pswap_sb, qnr_l = L["pswap"], L["qnr"]
    for ch in range(4):
        sl = slice(ch * 512, (ch + 1) * 512)
        pr.add("pe", (lambda sl=sl: lambda e: e.matmul(psm[:, :], pswap_sb[:, :], qnr_l[:, sl],
                                                       start=True, stop=True))(),
               reads=("pswap", "qnr"), writes=("psm",))
        pr.add("dve", (lambda sl=sl: lambda e: e.tensor_mul(t1[:, :], psm[:, :], sint[:, sl]))(),
               reads=("psm", "sint"), writes=("t1",))
        pr.add("dve", (lambda sl=sl: lambda e: e.tensor_mul(dst[:, sl], qnr_l[:, sl], cost[:, sl]))(),
               reads=("qnr", "cost"), writes=(dst_res,))
        pr.add("dve", (lambda sl=sl: lambda e: e.tensor_add(dst[:, sl], dst[:, sl], t1[:, :]))(),
               reads=(dst_res, "t1"), writes=(dst_res,))


def _attention(pr, nc, L, head):
    (qn, ktb, vrow, lsb, probsT, expsc, colv, rows, b2hi, b2lo, cmsk, cmskT,
     pstat, pT, pv, psm, attnT, idf, onerb) = (
        L["qn"], L["ktb"][0], L["vrow"], L["qraw"], L["probsT"], L["probsT"],
        L["colv"], L["rows"], L["b2hi"], L["b2lo"], L["cmsk"], L["cmskT"],
        L["pstat"], L["pT"], L["pv"], L["psm"], L["attnT"], L["idf"], L["onerb"])
    S_ = A0_S
    for qt in range(16):
        cols = (qt + 1) * 128
        nch = (cols + 511) // 512
        for ch in range(nch):
            n = min(512, cols - ch * 512)
            pr.add("pe", (lambda qt=qt, ch=ch, n=n:
                          lambda e: e.matmul(pstat[:, 0:n], qn[:, qt * 128:(qt + 1) * 128],
                                             ktb[:, ch * 512: ch * 512 + n], start=True, stop=True))(),
                   reads=("qn", "ktb0"), writes=("pstat",))
            pr.add("act", (lambda qt=qt, ch=ch, n=n:
                           lambda e: e.activation(out=lsb[:, ch * 512: ch * 512 + n],
                                                  in_=pstat[:, 0:n], func=AF.Copy))(),
                   reads=("pstat",), writes=("qraw",))
        pr.add("dve", (lambda qt=qt: lambda e: e.tensor_add(lsb[:, qt * 128:(qt + 1) * 128],
                                                            lsb[:, qt * 128:(qt + 1) * 128], cmsk[:, :]))(),
               reads=("qraw", "cmsk"), writes=("qraw",))
        pr.add("dve", (lambda cols=cols: lambda e: e.reduce_max(colv[:, 0:1], lsb[:, 0:cols],
                                                                axis=AX.X, negate=True))(),
               reads=("qraw",), writes=("colv",))
        pr.add("act", (lambda cols=cols: lambda e: e.activation(out=expsc[:, 0:cols], in_=lsb[:, 0:cols],
                                                                func=AF.Exp, bias=colv[:, 0:1],
                                                                accum_out=colv[:, 1:2]))(),
               reads=("qraw", "colv"), writes=("probsT", "colv"))
        pr.add("act", lambda e: e.activation(out=colv[:, 2:3], in_=colv[:, 1:2], func=AF.Ln),
               reads=("colv",), writes=("colv",))
        pr.add("dve", lambda e: e.tensor_sub(colv[:, 3:4], colv[:, 0:1], colv[:, 2:3]),
               reads=("colv",), writes=("colv",))
        pr.add("pe", lambda e: e.transpose(pstat[0:1, 0:128], colv[:, 3:4], idf[:, :]),
               reads=("colv", "idf"), writes=("pstat",))
        pr.add("act", (lambda qt=qt: lambda e: e.activation(out=b2hi[0:1, qt * 128:(qt + 1) * 128],
                                                            in_=pstat[0:1, 0:128], func=AF.Copy))(),
               reads=("pstat",), writes=("b2hi",))
        pr.add("dve", (lambda qt=qt: lambda e: e.tensor_sub(b2lo[0:1, qt * 128:(qt + 1) * 128],
                                                            pstat[0:1, 0:128],
                                                            b2hi[0:1, qt * 128:(qt + 1) * 128]))(),
               reads=("pstat", "b2hi"), writes=("b2lo",))
    for st in range(16):
        qch0 = st // 4
        for qch in range(qch0, 4):
            pr.add("pe", (lambda st=st, qch=qch:
                          lambda e: e.matmul(pT[:, :], ktb[:, st * 128:(st + 1) * 128],
                                             qn[:, qch * 512:(qch + 1) * 512], start=True, stop=False))(),
                   reads=("ktb0", "qn"), writes=("pT",))
            pr.add("pe", (lambda qch=qch:
                          lambda e: e.matmul(pT[:, :], onerb[0:1, :], b2hi[0:1, qch * 512:(qch + 1) * 512],
                                             start=False, stop=False))(),
                   reads=("onerb", "b2hi"), writes=("pT",))
            pr.add("pe", (lambda qch=qch:
                          lambda e: e.matmul(pT[:, :], onerb[0:1, :], b2lo[0:1, qch * 512:(qch + 1) * 512],
                                             start=False, stop=True))(),
                   reads=("onerb", "b2lo"), writes=("pT",))
            if qch == qch0:
                k = st - 4 * qch
                pr.add("dve", (lambda k=k: lambda e: e.tensor_add(pT[:, :], pT[:, :],
                                                                  cmskT[:, k * 512:(k + 1) * 512]))(),
                       reads=("pT", "cmskT"), writes=("pT",))
            pr.add("act", (lambda st=st, qch=qch:
                           lambda e: e.activation(out=probsT[:, qch * 512:(qch + 1) * 512],
                                                  in_=pT[:, :], func=AF.Exp))(),
                   reads=("pT",), writes=("probsT",))
        for qch in range(qch0, 4):
            pr.add("pe", (lambda st=st, qch=qch:
                          lambda e: e.matmul(pv[qch][:, :], vrow[:, st * 128:(st + 1) * 128],
                                             probsT[:, qch * 512:(qch + 1) * 512],
                                             start=(st == 0), stop=(st == 4 * qch + 3)))(),
                   reads=("vrow", "probsT"), writes=(f"pv{qch}",))
    for qch in range(4):
        pr.add("act", (lambda qch=qch, head=head:
                       lambda e: e.activation(out=attnT[:, head * S_ + qch * 512: head * S_ + (qch + 1) * 512],
                                              in_=pv[qch][:, :], func=AF.Copy))(),
               reads=(f"pv{qch}",), writes=("attnT",))




D1_D = 2048
D1_H = 8
D1_DH = 128
D1_CT = 64
D1_C = 128
D1_NCHK = 32
D1_EPS_O = 1e-5


def build_l1attn(pr, nc, h, name="d1"):
    es = h["es"]
    sb = lambda nm, shp, dt: es.enter_context(nc.sbuf_tensor(f"{name}_{nm}", shp, dt))
    ps = lambda nm, shp, dt: es.enter_context(nc.psum_tensor(f"{name}_{nm}", shp, dt))
    S = 2048
    KT = 16

    x2, x3 = h["x2"], h["x3"]

    xTb = sb("xTb", [128, KT * S], BF16)
    wsl = sb("wsl", [128, D1_D], BF16)
    qraw = sb("qraw", [128, S], F32)
    cacc = sb("cacc", [128, S], F32)
    rowm = sb("rowm", [1, S], F32)
    qtf = sb("qtf", [128, S], BF16)
    ktf = sb("ktf", [128, 2 * S], BF16)
    vtf = sb("vtf", [128, 2 * S], BF16)
    wbs = sb("wbs", [128, 16 * 16], BF16)
    was = sb("was", [128, 16 * 8], BF16)
    btT = sb("btT", [16, S], F32)
    lbT = sb("lbT", [16, S], F32)
    DgT = sb("DgT", [8, S], F32)
    alog = sb("alog", [8, 1], F32)
    dtb = sb("dtb", [8, 1], F32)
    eA = sb("eA", [8, 1], F32)
    drow = sb("drow", [1, 2 * S], F32)
    brow = sb("brow", [1, 2 * S], BF16)
    lbrow = sb("lbrow", [1, 2 * S], BF16)
    dqrow = sb("dqrow", [1, D1_CT], F32)
    nd = sb("nd", [1, D1_C], F32)
    dlb = sb("dlb", [1, D1_C], F32)
    tiny = sb("tiny", [1, 8], F32)
    AAn = sb("AAn", [128, D1_C], F32)
    Dm2 = sb("Dm2", [128, D1_C], F32)
    XT_sb = sb("XT_sb", [128, D1_C], BF16)
    X_sb = sb("X_sb", [128, D1_C], BF16)
    X2a = sb("X2a", [128, D1_C], BF16)
    X2b = sb("X2b", [128, D1_C], BF16)
    Kc = sb("Kc", [128, D1_DH], BF16)
    khat = sb("khat", [128, D1_DH], BF16)
    rhs = sb("rhs", [128, 256], BF16)
    uf = sb("uf", [128, 256], F32)
    ub = sb("ub", [128, 256], BF16)
    bcol = sb("bcol", [128, 4], F32)
    btcol = sb("btcol", [128, 1], F32)
    GTs = sb("GTs", [128, D1_CT], F32)
    R2s = sb("R2s", [128, D1_CT], F32)
    WTs = sb("WTs", [128, D1_CT], BF16)
    ZTs = sb("ZTs", [128, D1_CT], BF16)
    tzs = sb("tzs", [128, D1_CT], BF16)
    bqr = sb("bqr", [1, D1_CT], F32)
    Msb = sb("Msb", [128, D1_DH], BF16)
    Psb = sb("Psb", [128, D1_DH], F32)
    Sf = sb("Sf", [128, D1_DH], F32)
    Sbf = sb("Sbf", [128, D1_DH], BF16)
    Icon = sb("Icon", [128, 128], BF16)
    otile = sb("otile", [128, D1_H * D1_DH], BF16)
    gtile = sb("gtile", [128, D1_H * D1_DH], BF16)
    oT_t = sb("oT_t", [128, D1_H * D1_DH], BF16)
    xr = sb("xr", [128, D1_D], F32)
    ysb = None
    sqb = sb("sqb", [128, D1_D], BF16)
    st1 = sb("st1", [128, 8], F32)
    rq = sb("rq", [128, 8], F32)
    nwc = sb("nwc", [128, 1], F32)
    onefb = sb("onefb", [1, 1], BF16)
    idf = sb("idf", [128, 128], F32)
    oner = sb("oner", [1, 128], F32)
    onef = sb("onef", [1, 1], F32)
    mx = sb("mx", [128, 128], F32)
    mq = sb("mq", [128, D1_CT], F32)
    cqc = sb("cqc", [128, 8 * 4], F32)
    ckc = sb("ckc", [128, 16 * 4], F32)
    cvc = sb("cvc", [128, 16 * 4], F32)

    pA = ps("pA", [128, 512], F32)
    pB = ps("pB", [128, 512], F32)
    pU = ps("pU", [128, 512], F32)
    pS = ps("pS", [128, 512], F32)
    pW = ps("pW", [128, 512], F32)
    pX2 = ps("pX2", [128, 256], F32)
    pXb = ps("pXb", [128, 128], BF16)

    for nm, src in [("idf", h["identf"]), ("oner", h["ones_row_f"]), ("onef", h["one_f"]),
                    ("mx", h["mx_c"]), ("mq", h["mq_c"]), ("alog", h["alog_c"]), ("dtb", h["dtb_c"]),
                    ("cqc", h["cq_c"]), ("ckc", h["ck_c"]), ("cvc", h["cv_c"]),
                    ("nwc", h["nw_c"]), ("Icon", h["ident_bf"]), ("onefb", h["one_b"])]:
        dst = locals()[nm]
        pr.dma("sp", dst[:, :], src[:, :], writes=(nm,))
    pr.dma("pool", wbs[:, :], h["wb_c"][:, :], writes=("wbs",))
    pr.dma("pool", was[:, :], h["wa_c"][:, :], writes=("was",))

    def proj_slab(wslab_src, wres, out_ps=True):
        pr.dma("pool", wsl[:, :], wslab_src, writes=("wsl",))
        for ch in range(4):
            for kt in range(KT):
                pr.add("pe", (lambda ch=ch, kt=kt: lambda e: e.matmul(
                    pA[:, :], wsl[:, kt * 128:(kt + 1) * 128],
                    xTb[:, kt * S + ch * 512: kt * S + (ch + 1) * 512],
                    start=(kt == 0), stop=(kt == KT - 1)))(),
                       reads=("wsl", "xTb"), writes=("pA",))
            pr.add("act", (lambda ch=ch: lambda e: e.activation(
                out=qraw[:, ch * 512:(ch + 1) * 512], in_=pA[:, :], func=AF.Copy))(),
                   reads=("pA",), writes=("qraw",))

    def conv_silu(cw, cw_res, idx):
        pr.add("dve", (lambda: lambda e: e.tensor_scalar_mul(
            cacc[:, :], qraw[:, :], cw[:, idx * 4 + 3: idx * 4 + 4]))(),
               reads=("qraw", cw_res), writes=("cacc",))
        for i in range(3):
            sh = 3 - i
            pr.add("dve", (lambda i=i, sh=sh: lambda e: e.scalar_tensor_tensor(
                cacc[:, sh:S], qraw[:, 0:S - sh], cw[:, idx * 4 + i: idx * 4 + i + 1],
                cacc[:, sh:S], ALU.mult, ALU.add))(),
                   reads=("qraw", cw_res, "cacc"), writes=("cacc",))
        pr.add("act", lambda e: e.activation(out=cacc[:, :], in_=cacc[:, :], func=AF.Silu),
               reads=("cacc",), writes=("cacc",))

    def l2norm_to(dst_ap, dst_res, scale):
        pr.add("act", lambda e: e.activation(out=sqb_[:, :], in_=cacc[:, :], func=AF.Square),
               reads=("cacc",), writes=("sqb",))
        for ch in range(4):
            pr.add("pe", (lambda ch=ch: lambda e: e.matmul(
                pW[0:1, :], onecb[:, 0:1], sqb_[:, ch * 512:(ch + 1) * 512],
                start=True, stop=True))(),
                   reads=("onecb", "sqb"), writes=("pW",))
            pr.add("act", (lambda ch=ch: lambda e: e.activation(
                out=rowm[0:1, ch * 512:(ch + 1) * 512], in_=pW[0:1, :], func=AF.Copy))(),
                   reads=("pW",), writes=("rowm",))
        pr.add("act", lambda e: e.activation(out=rowm[0:1, :], in_=rowm[0:1, :], func=AF.Sqrt),
               reads=("rowm",), writes=("rowm",))
        pr.add("dve", lambda e: e.tensor_scalar_max(rowm[0:1, :], rowm[0:1, :], 1e-12),
               reads=("rowm",), writes=("rowm",))
        pr.add("dve", lambda e: e.reciprocal(rowm[0:1, :], rowm[0:1, :]),
               reads=("rowm",), writes=("rowm",))
        if scale != 1.0:
            pr.add("dve", (lambda: lambda e: e.tensor_scalar_mul(rowm[0:1, :], rowm[0:1, :], scale))(),
                   reads=("rowm",), writes=("rowm",))
        for ch in range(4):
            pr.add("pe", (lambda ch=ch: lambda e: e.matmul(
                pW[:, :], oner[0:1, :], rowm[0:1, ch * 512:(ch + 1) * 512],
                start=True, stop=True))(),
                   reads=("oner", "rowm"), writes=("pW",))
            pr.add("dve", (lambda ch=ch, dst_ap=dst_ap: lambda e: e.tensor_mul(
                dst_ap[:, ch * 512:(ch + 1) * 512], cacc[:, ch * 512:(ch + 1) * 512], pW[:, :]))(),
                   reads=("cacc", "pW"), writes=(dst_res,))

    sqb_ = sqb
    onecb = sb("onecb", [128, 1], BF16)
    pr.dma("sp", onecb[:, :], h["ones_col_b"][:, :], writes=("onecb",))
    ktmp = sb("ktmp", [128, S], BF16)
    zrow = cacc[0:8, :]
    ysb = qraw
    dtmp = rowm
    wosl = xTb

    for b in range(2):
        base = b * S
        for tt in range(16):
            pr.dma("sp", xr[:, :], x2[base + tt * 128: base + (tt + 1) * 128, :], writes=("xr",))
            for kt in range(KT):
                pr.add("pe", (lambda kt=kt: lambda e: e.transpose(
                    pA[:, 0:128], xr[:, kt * 128:(kt + 1) * 128], idf[:, :]))(),
                       reads=("xr", "idf"), writes=("pA",))
                pr.add("act", (lambda kt=kt, tt=tt: lambda e: e.activation(
                    out=xTb[:, kt * S + tt * 128: kt * S + (tt + 1) * 128],
                    in_=pA[:, 0:128], func=AF.Copy))(),
                       reads=("pA",), writes=("xTb",))
        for sig, wofs, M in [("bt", 0, 16), ("ga", 0, 8)]:
            pass
        for ch in range(4):
            for kt in range(KT):
                pr.add("pe", (lambda ch=ch, kt=kt: lambda e: e.matmul(
                    pB[0:16, :], wbs[:, kt * 16:(kt + 1) * 16],
                    xTb[:, kt * S + ch * 512: kt * S + (ch + 1) * 512],
                    start=(kt == 0), stop=(kt == KT - 1)))(),
                       reads=("wbs", "xTb"), writes=("pB",))
            pr.add("act", (lambda ch=ch: lambda e: e.activation(
                out=btT[:, ch * 512:(ch + 1) * 512], in_=pB[0:16, :], func=AF.Sigmoid))(),
                   reads=("pB",), writes=("btT",))
        pr.add("dve", lambda e: e.tensor_scalar_mul(btT[:, :], btT[:, :], 2.0),
               reads=("btT",), writes=("btT",))
        pr.add("act", lambda e: e.activation(out=lbT[:, :], in_=btT[:, :], func=AF.Ln),
               reads=("btT",), writes=("lbT",))
        for ch in range(4):
            for kt in range(KT):
                pr.add("pe", (lambda ch=ch, kt=kt: lambda e: e.matmul(
                    pB[0:8, :], was[:, kt * 8:(kt + 1) * 8],
                    xTb[:, kt * S + ch * 512: kt * S + (ch + 1) * 512],
                    start=(kt == 0), stop=(kt == KT - 1)))(),
                       reads=("was", "xTb"), writes=("pB",))
            pr.add("act", (lambda ch=ch: lambda e: e.activation(
                out=DgT[:, ch * 512:(ch + 1) * 512], in_=pB[0:8, :], func=AF.Identity,
                bias=dtb[:, 0:1]))(),
                   reads=("pB", "dtb"), writes=("DgT",))
        pr.add("dve", lambda e: e.tensor_scalar_mul(zrow[:, :], DgT[:, :], -1.0),
               reads=("DgT",), writes=("cacc",))
        pr.add("dve", lambda e: e.tensor_tensor(zrow[:, :], zrow[:, :], DgT[:, :], ALU.min),
               reads=("cacc", "DgT"), writes=("cacc",))
        pr.add("act", lambda e: e.activation(out=zrow[:, :], in_=zrow[:, :], func=AF.Exp),
               reads=("cacc",), writes=("cacc",))
        pr.add("dve", lambda e: e.tensor_scalar_add(zrow[:, :], zrow[:, :], 1.0),
               reads=("cacc",), writes=("cacc",))
        pr.add("act", lambda e: e.activation(out=zrow[:, :], in_=zrow[:, :], func=AF.Ln),
               reads=("cacc",), writes=("cacc",))
        pr.add("dve", lambda e: e.tensor_scalar_max(DgT[:, :], DgT[:, :], 0.0),
               reads=("DgT",), writes=("DgT",))
        pr.add("dve", lambda e: e.tensor_add(DgT[:, :], DgT[:, :], zrow[:, :]),
               reads=("DgT", "cacc"), writes=("DgT",))
        pr.add("act", lambda e: e.activation(out=eA[:, 0:1], in_=alog[:, 0:1], func=AF.Exp),
               reads=("alog",), writes=("eA",))
        pr.add("dve", lambda e: e.tensor_scalar(DgT[:, :], DgT[:, :], eA[:, 0:1], -1.0,
                                                ALU.mult, ALU.mult),
               reads=("DgT", "eA"), writes=("DgT",))
        pr.add("dve", lambda e: e.memset(zrow[:, :], 0.0), reads=(), writes=("cacc",))
        pr.add("dve", lambda e: e.tensor_tensor_scan(DgT[:, :], DgT[:, :], zrow[:, :], 0.0,
                                                     ALU.add, ALU.add),
               reads=("DgT", "cacc"), writes=("DgT",))

        for hd in range(D1_H):
            proj_slab(h["wq_t"][hd], "wq")
            conv_silu(cqc, "cqc", hd)
            l2norm_to(qtf, "qtf", D1_DH ** -0.5)
            for jj in range(2):
                proj_slab(h["wk_t"][jj * 8 + hd], "wk")
                conv_silu(ckc, "ckc", jj * 8 + hd)
                l2norm_to(ktmp, "ktmp", 1.0)
                pr.add("dve", (lambda jj=jj: lambda e: e.tensor_copy(
                    ktf[:, jj:2 * S:2], ktmp[:, :]))(),
                       reads=("ktmp",), writes=("ktf",))
            for jj in range(2):
                proj_slab(h["wv_t"][jj * 8 + hd], "wv")
                conv_silu(cvc, "cvc", jj * 8 + hd)
                pr.add("dve", (lambda jj=jj: lambda e: e.tensor_copy(
                    vtf[:, jj:2 * S:2], cacc[:, :]))(),
                       reads=("cacc",), writes=("vtf",))
            proj_slab(h["wg_t"][hd], "wg")
            pr.add("act", lambda e: e.activation(out=cacc[:, :], in_=qraw[:, :], func=AF.Silu),
                   reads=("qraw",), writes=("cacc",))
            for tt in range(16):
                pr.add("pe", (lambda tt=tt: lambda e: e.transpose(
                    pA[:, 0:128], cacc[:, tt * 128:(tt + 1) * 128], idf[:, :]))(),
                       reads=("cacc", "idf"), writes=("pA",))
                pr.add("act", (lambda tt=tt: lambda e: e.activation(
                    out=gtile[:, 0:128], in_=pA[:, 0:128], func=AF.Copy))(),
                       reads=("pA",), writes=("gtile",))
                pr.dma("pool", h["g_s"][b, tt * 128:(tt + 1) * 128, hd * 128:(hd + 1) * 128],
                       gtile[:, 0:128], reads=("gtile",), writes=(f"gs{b}",))
            pr.dma("pool", dtmp[0:1, :], DgT[hd:hd + 1, :], reads=("DgT",), writes=("rowm",))
            pr.add("dve", lambda e: e.tensor_copy(drow[0:1, 0:2 * S:2], dtmp[0:1, :]),
                   reads=("rowm",), writes=("drow",))
            pr.add("dve", lambda e: e.tensor_copy(drow[0:1, 1:2 * S:2], dtmp[0:1, :]),
                   reads=("rowm",), writes=("drow",))
            for jj in range(2):
                pr.dma("pool", dtmp[0:1, :], btT[jj * 8 + hd: jj * 8 + hd + 1, :],
                       reads=("btT",), writes=("rowm",))
                pr.add("dve", (lambda jj=jj: lambda e: e.tensor_copy(
                    brow[0:1, jj:2 * S:2], dtmp[0:1, :]))(),
                       reads=("rowm",), writes=("brow",))
                pr.dma("pool", dtmp[0:1, :], lbT[jj * 8 + hd: jj * 8 + hd + 1, :],
                       reads=("lbT",), writes=("rowm",))
                pr.add("dve", (lambda jj=jj: lambda e: e.tensor_copy(
                    lbrow[0:1, jj:2 * S:2], dtmp[0:1, :]))(),
                       reads=("rowm",), writes=("lbrow",))
            pr.add("dve", lambda e: e.memset(Sf[:, :], 0.0), reads=(), writes=("Sf",))
            pr.add("dve", lambda e: e.memset(Sbf[:, :], 0.0), reads=(), writes=("Sbf",))
            for c in range(D1_NCHK):
                _chunk(pr, nc, locals(), b, hd, c)
        _endphase(pr, nc, locals(), b)
    return


def _chunk(pr, nc, L, b, hd, c):
    (ktf, vtf, qtf, drow, brow, lbrow, nd, dlb, tiny, AAn, Dm2, XT_sb, X_sb, X2a, X2b,
     Kc, khat, rhs, uf, ub, bcol, GTs, R2s, WTs, ZTs, tzs, bqr, Msb, Psb, Sf, Sbf,
     dqrow, mx, mq, oner, onef, Icon, idf, otile) = (
        L["ktf"], L["vtf"], L["qtf"], L["drow"], L["brow"], L["lbrow"], L["nd"], L["dlb"],
        L["tiny"], L["AAn"], L["Dm2"], L["XT_sb"], L["X_sb"], L["X2a"], L["X2b"],
        L["Kc"], L["khat"], L["rhs"], L["uf"], L["ub"], L["bcol"], L["GTs"], L["R2s"],
        L["WTs"], L["ZTs"], L["tzs"], L["bqr"], L["Msb"], L["Psb"], L["Sf"], L["Sbf"],
        L["dqrow"], L["mx"], L["mq"], L["oner"], L["onef"], L["Icon"], L["idf"], L["otile"])
    pA, pB, pU, pS, pW, pX2, pXb = (L["pA"], L["pB"], L["pU"], L["pS"], L["pW"],
                                    L["pX2"], L["pXb"])
    h = L["h"]
    c0 = c * D1_C
    KTc = ktf[:, c0:c0 + D1_C]
    VTc = vtf[:, c0:c0 + D1_C]
    QTc = qtf[:, c * D1_CT:(c + 1) * D1_CT]
    pr.add("act", (lambda: lambda e: e.activation(out=nd[0:1, :], in_=drow[0:1, c0:c0 + D1_C],
                                                  func=AF.Copy, scale=-1))(),
           reads=("drow",), writes=("nd",))
    pr.add("dve", (lambda: lambda e: e.tensor_add(dlb[0:1, :], drow[0:1, c0:c0 + D1_C],
                                                  lbrow[0:1, c0:c0 + D1_C]))(),
           reads=("drow", "lbrow"), writes=("dlb",))
    pr.add("dve", (lambda: lambda e: e.tensor_copy(dqrow[0:1, :], drow[0:1, c0 + 1:c0 + D1_C:2]))(),
           reads=("drow",), writes=("dqrow",))
    if c > 0:
        pr.add("act", (lambda: lambda e: e.activation(out=tiny[0:1, 0:1],
                                                      in_=drow[0:1, c0 - 1:c0],
                                                      func=AF.Copy, scale=-1))(),
               reads=("drow",), writes=("tiny",))
    pr.add("pe", lambda e: e.matmul(pA[:, 0:D1_C], KTc, KTc, start=True, stop=True),
           reads=("ktf",), writes=("pA",))
    pr.add("act", lambda e: e.activation(out=AAn[:, :], in_=pA[:, 0:D1_C], func=AF.Copy, scale=-1),
           reads=("pA",), writes=("AAn",))
    pr.add("pe", lambda e: e.matmul(pB[:, 0:D1_C], nd[0:1, :], oner[0:1, :], start=True, stop=False),
           reads=("nd", "oner"), writes=("pB",))
    pr.add("pe", lambda e: e.matmul(pB[:, 0:D1_C], oner[0:1, :], dlb[0:1, :], start=False, stop=True),
           reads=("oner", "dlb"), writes=("pB",))
    pr.add("dve", lambda e: e.tensor_add(Dm2[:, :], pB[:, 0:D1_C], mx[:, :]),
           reads=("pB", "mx"), writes=("Dm2",))
    pr.add("act", lambda e: e.activation(out=Dm2[:, :], in_=Dm2[:, :], func=AF.Exp),
           reads=("Dm2",), writes=("Dm2",))
    pr.add("dve", lambda e: e.tensor_mul(XT_sb[:, :], Dm2[:, :], AAn[:, :]),
           reads=("Dm2", "AAn"), writes=("XT_sb",))
    pr.add("pe", lambda e: e.transpose(pXb[:, :], XT_sb[:, :], Icon[:, :]),
           reads=("XT_sb", "Icon"), writes=("pXb",))
    pr.add("act", lambda e: e.activation(out=X_sb[:, :], in_=pXb[:, :], func=AF.Copy),
           reads=("pXb",), writes=("X_sb",))
    pr.add("pe", (lambda: lambda e: e.matmul(pW[:, 0:1], drow[0:1, c0:c0 + D1_C], onef[0:1, 0:1],
                                             start=True, stop=(c == 0)))(),
           reads=("drow", "onef"), writes=("pW",))
    if c > 0:
        pr.add("pe", lambda e: e.matmul(pW[:, 0:1], oner[0:1, :], tiny[0:1, 0:1],
                                        start=False, stop=True),
               reads=("oner", "tiny"), writes=("pW",))
    pr.add("act", lambda e: e.activation(out=bcol[:, 0:1], in_=pW[:, 0:1], func=AF.Exp),
           reads=("pW",), writes=("bcol",))
    pr.add("pe", lambda e: e.matmul(pW[:, 1:2], nd[0:1, :], onef[0:1, 0:1], start=True, stop=False),
           reads=("nd", "onef"), writes=("pW",))
    pr.add("pe", (lambda: lambda e: e.matmul(pW[:, 1:2], oner[0:1, :],
                                             drow[0:1, c0 + D1_C - 1:c0 + D1_C], start=False, stop=True))(),
           reads=("oner", "drow"), writes=("pW",))
    pr.add("act", lambda e: e.activation(out=bcol[:, 1:2], in_=pW[:, 1:2], func=AF.Exp),
           reads=("pW",), writes=("bcol",))
    onefb_l, btcol_l = L["onefb"], L["btcol"]
    pr.add("pe", (lambda: lambda e: e.matmul(pW[:, 2:3], brow[0:1, c0:c0 + D1_C],
                                             onefb_l[0:1, 0:1], start=True, stop=True))(),
           reads=("brow", "onefb"), writes=("pW",))
    pr.add("act", lambda e: e.activation(out=btcol_l[:, 0:1], in_=pW[:, 2:3], func=AF.Copy),
           reads=("pW",), writes=("btcol",))
    pr.add("pe", (lambda: lambda e: e.matmul(pW[:, 3:4], oner[0:1, :],
                                             drow[0:1, c0 + D1_C - 1:c0 + D1_C],
                                             start=True, stop=(c == 0)))(),
           reads=("oner", "drow"), writes=("pW",))
    if c > 0:
        pr.add("pe", lambda e: e.matmul(pW[:, 3:4], oner[0:1, :], tiny[0:1, 0:1],
                                        start=False, stop=True),
               reads=("oner", "tiny"), writes=("pW",))
    pr.add("act", lambda e: e.activation(out=bcol[:, 2:3], in_=pW[:, 3:4], func=AF.Exp),
           reads=("pW",), writes=("bcol",))
    pr.add("pe", lambda e: e.transpose(pXb[:, :], KTc, Icon[:, :]),
           reads=("ktf", "Icon"), writes=("pXb",))
    pr.add("act", lambda e: e.activation(out=Kc[:, :], in_=pXb[:, :], func=AF.Copy),
           reads=("pXb",), writes=("Kc",))
    pr.add("pe", lambda e: e.transpose(pXb[:, :], VTc, Icon[:, :]),
           reads=("vtf", "Icon"), writes=("pXb",))
    pr.add("dve", lambda e: e.tensor_scalar_mul(rhs[:, 0:128], pXb[:, :], btcol_l[:, 0:1]),
           reads=("pXb", "btcol"), writes=("rhs",))
    pr.add("dve", lambda e: e.tensor_mul(bcol[:, 3:4], btcol_l[:, 0:1], bcol[:, 0:1]),
           reads=("btcol", "bcol"), writes=("bcol",))
    pr.add("dve", lambda e: e.tensor_scalar_mul(rhs[:, 128:256], Kc[:, :], bcol[:, 3:4]),
           reads=("Kc", "bcol"), writes=("rhs",))
    pr.add("dve", lambda e: e.tensor_scalar_mul(khat[:, :], Kc[:, :], bcol[:, 1:2]),
           reads=("Kc", "bcol"), writes=("khat",))
    pr.add("dve", lambda e: e.tensor_copy(uf[:, :], rhs[:, :]),
           reads=("rhs",), writes=("uf",))
    pr.add("dve", lambda e: e.tensor_copy(ub[:, :], rhs[:, :]),
           reads=("rhs",), writes=("ub",))
    Xc, XcT, Xn, XnT = X_sb, XT_sb, X2a, X2b
    xres = {id(X_sb): "X_sb", id(XT_sb): "XT_sb", id(X2a): "X2a", id(X2b): "X2b"}
    for lev in range(7):
        pr.add("pe", (lambda XcT=XcT: lambda e: e.matmul(pU[:, 0:256], XcT[:, :], ub[:, :],
                                                         start=True, stop=True))(),
               reads=(xres[id(XcT)], "ub"), writes=("pU",))
        pr.add("dve", lambda e: e.tensor_add(uf[:, :], uf[:, :], pU[:, 0:256]),
               reads=("uf", "pU"), writes=("uf",))
        pr.add("act", lambda e: e.activation(out=ub[:, :], in_=uf[:, :], func=AF.Copy),
               reads=("uf",), writes=("ub",))
        if lev < 6:
            pr.add("pe", (lambda Xc=Xc, XcT=XcT: lambda e: e.matmul(
                pX2[:, 0:128], XcT[:, :], Xc[:, :], start=True, stop=True))(),
                   reads=(xres[id(XcT)], xres[id(Xc)]), writes=("pX2",))
            pr.add("pe", (lambda Xc=Xc, XcT=XcT: lambda e: e.matmul(
                pX2[:, 128:256], Xc[:, :], XcT[:, :], start=True, stop=True))(),
                   reads=(xres[id(Xc)], xres[id(XcT)]), writes=("pX2",))
            pr.add("act", (lambda Xn=Xn: lambda e: e.activation(
                out=Xn[:, :], in_=pX2[:, 0:128], func=AF.Copy))(),
                   reads=("pX2",), writes=(xres[id(Xn)],))
            pr.add("act", (lambda XnT=XnT: lambda e: e.activation(
                out=XnT[:, :], in_=pX2[:, 128:256], func=AF.Copy))(),
                   reads=("pX2",), writes=(xres[id(XnT)],))
            Xc, XcT, Xn, XnT = Xn, XnT, Xc, XcT
    pr.add("pe", lambda e: e.matmul(pA[:, 0:D1_CT], KTc, QTc, start=True, stop=True),
           reads=("ktf", "qtf"), writes=("pA",))
    pr.add("act", lambda e: e.activation(out=GTs[:, :], in_=pA[:, 0:D1_CT], func=AF.Copy),
           reads=("pA",), writes=("GTs",))
    pr.add("pe", lambda e: e.matmul(pB[:, 0:D1_CT], nd[0:1, :], oner[0:1, 0:D1_CT], start=True, stop=False),
           reads=("nd", "oner"), writes=("pB",))
    pr.add("pe", lambda e: e.matmul(pB[:, 0:D1_CT], oner[0:1, :], dqrow[0:1, :], start=False, stop=True),
           reads=("oner", "dqrow"), writes=("pB",))
    pr.add("dve", lambda e: e.tensor_add(R2s[:, :], pB[:, 0:D1_CT], mq[:, :]),
           reads=("pB", "mq"), writes=("R2s",))
    pr.add("act", lambda e: e.activation(out=R2s[:, :], in_=R2s[:, :], func=AF.Exp),
           reads=("R2s",), writes=("R2s",))
    pr.add("dve", lambda e: e.tensor_mul(WTs[:, :], R2s[:, :], GTs[:, :]),
           reads=("R2s", "GTs"), writes=("WTs",))
    if c > 0:
        pr.add("dve", (lambda: lambda e: e.tensor_single_scalar(
            bqr[0:1, :], dqrow[0:1, :], tiny[0:1, 0:1], ALU.add))(),
               reads=("dqrow", "tiny"), writes=("bqr",))
    else:
        pr.add("dve", lambda e: e.tensor_copy(bqr[0:1, :], dqrow[0:1, :]),
               reads=("dqrow",), writes=("bqr",))
    pr.add("act", lambda e: e.activation(out=bqr[0:1, :], in_=bqr[0:1, :], func=AF.Exp),
           reads=("bqr",), writes=("bqr",))
    pr.add("pe", lambda e: e.matmul(pB[:, 0:D1_CT], oner[0:1, :], bqr[0:1, :], start=True, stop=True),
           reads=("oner", "bqr"), writes=("pB",))
    pr.add("dve", lambda e: e.tensor_mul(tzs[:, :], QTc, pB[:, 0:D1_CT]),
           reads=("qtf", "pB"), writes=("tzs",))
    pr.add("pe", lambda e: e.matmul(pS[:, 0:D1_CT], ub[:, 128:256], WTs[:, :], start=True, stop=True),
           reads=("ub", "WTs"), writes=("pS",))
    pr.add("dve", lambda e: e.tensor_sub(ZTs[:, :], tzs[:, :], pS[:, 0:D1_CT]),
           reads=("tzs", "pS"), writes=("ZTs",))
    pr.add("pe", lambda e: e.matmul(pU[0:64, 0:128], WTs[:, :], ub[:, 0:128], start=True, stop=False),
           reads=("WTs", "ub"), writes=("pU",))
    pr.add("pe", lambda e: e.matmul(pU[0:64, 0:128], ZTs[:, :], Sbf[:, :], start=False, stop=True),
           reads=("ZTs", "Sbf"), writes=("pU",))
    tile = c // 2
    off = (c % 2) * 64
    pr.add("act", (lambda tile=tile, off=off: lambda e: e.activation(
        out=otile[off:off + 64, hd * 128:(hd + 1) * 128], in_=pU[0:64, 0:128], func=AF.Copy))(),
           reads=("pU",), writes=("otile",))
    if c % 2 == 1:
        pr.dma("pool", h["o_s"][b, tile * 128:(tile + 1) * 128, hd * 128:(hd + 1) * 128],
               otile[:, hd * 128:(hd + 1) * 128], reads=("otile",), writes=(f"os{b}",))
    pr.add("pe", lambda e: e.matmul(pS[:, 0:128], khat[:, :], ub[:, 0:128], start=True, stop=True),
           reads=("khat", "ub"), writes=("pS",))
    pr.add("act", lambda e: e.activation(out=Psb[:, :], in_=pS[:, 0:128], func=AF.Copy),
           reads=("pS",), writes=("Psb",))
    pr.add("pe", lambda e: e.matmul(pS[:, 0:128], ub[:, 128:256], khat[:, :], start=True, stop=True),
           reads=("ub", "khat"), writes=("pS",))
    pr.add("dve", lambda e: e.scalar_tensor_tensor(Msb[:, :], Icon[:, :], bcol[:, 2:3],
                                                   pS[:, 0:128], ALU.mult, ALU.subtract),
           reads=("Icon", "bcol", "pS"), writes=("Msb",))
    pr.add("pe", lambda e: e.matmul(pS[:, 0:128], Msb[:, :], Sbf[:, :], start=True, stop=True),
           reads=("Msb", "Sbf"), writes=("pS",))
    pr.add("dve", lambda e: e.tensor_add(Sf[:, :], pS[:, 0:128], Psb[:, :]),
           reads=("pS", "Psb"), writes=("Sf",))
    pr.add("act", lambda e: e.activation(out=Sbf[:, :], in_=Sf[:, :], func=AF.Copy),
           reads=("Sf",), writes=("Sbf",))


def _endphase(pr, nc, L, b):
    (otile, gtile, oT_t, xr, ysb, sqb, st1, rq, nwc, idf, oner, Icon, wsl) = (
        L["otile"], L["gtile"], L["oT_t"], L["xr"], L["ysb"], L["sqb"], L["st1"],
        L["rq"], L["nwc"], L["idf"], L["oner"], L["Icon"], L["wsl"])
    mybir_local = None
    pA, pB = L["pA"], L["pB"]
    h = L["h"]
    x2, x3 = h["x2"], h["x3"]
    base = b * 2048
    wosl = L["wosl"]
    for kt in range(8):
        pr.dma("pool", wosl[:, kt * 2048:(kt + 1) * 2048], h["wo_t"][kt], writes=("xTb",))
        pr.add("dve", (lambda kt=kt: lambda e: e.tensor_scalar_mul(
            wosl[:, kt * 2048:(kt + 1) * 2048], wosl[:, kt * 2048:(kt + 1) * 2048],
            nwc[:, 0:1]))(),
               reads=("xTb", "nwc"), writes=("xTb",))
    for tt in range(16):
        pr.dma("sp", otile[:, :], h["o_s"][b, tt * 128:(tt + 1) * 128, :],
               reads=(f"os{b}",), writes=("otile",))
        pr.dma("sp", gtile[:, :], h["g_s"][b, tt * 128:(tt + 1) * 128, :],
               reads=(f"gs{b}",), writes=("gtile",))
        pr.dma("sp", xr[:, :], x2[base + tt * 128: base + (tt + 1) * 128, :], writes=("xr",))
        pr.add("act", lambda e: e.activation(out=sqb[:, 0:1024], in_=otile[:, :], func=AF.Square),
               reads=("otile",), writes=("sqb",))
        for hd in range(8):
            pr.add("dve", (lambda hd=hd: lambda e: e.reduce_sum(
                rq[:, hd:hd + 1], sqb[:, hd * 128:(hd + 1) * 128], axis=AX.X))(),
                   reads=("sqb",), writes=("rq",))
        pr.add("dve", lambda e: e.tensor_scalar(rq[:, :], rq[:, :], 1.0 / 128, float(D1_EPS_O),
                                                ALU.mult, ALU.add),
               reads=("rq",), writes=("rq",))
        pr.add("act", lambda e: e.activation(out=rq[:, :], in_=rq[:, :], func=AF.Ln),
               reads=("rq",), writes=("rq",))
        pr.add("dve", lambda e: e.tensor_scalar_mul(rq[:, :], rq[:, :], -0.5),
               reads=("rq",), writes=("rq",))
        pr.add("act", lambda e: e.activation(out=rq[:, :], in_=rq[:, :], func=AF.Exp),
               reads=("rq",), writes=("rq",))
        for hd in range(8):
            pr.add("dve", (lambda hd=hd: lambda e: e.tensor_scalar_mul(
                otile[:, hd * 128:(hd + 1) * 128], otile[:, hd * 128:(hd + 1) * 128],
                rq[:, hd:hd + 1]))(),
                   reads=("otile", "rq"), writes=("otile",))
        pr.add("dve", lambda e: e.tensor_mul(otile[:, :], otile[:, :], gtile[:, :]),
               reads=("otile", "gtile"), writes=("otile",))
        for hd in range(8):
            pr.add("pe", (lambda hd=hd: lambda e: e.transpose(
                L["pXb"][:, :], otile[:, hd * 128:(hd + 1) * 128], Icon[:, :]))(),
                   reads=("otile", "Icon"), writes=("pXb",))
            pr.add("act", (lambda hd=hd: lambda e: e.activation(
                out=oT_t[:, hd * 128:(hd + 1) * 128], in_=L["pXb"][:, :], func=AF.Copy))(),
                   reads=("pXb",), writes=("oT_t",))
        for och in range(4):
            for kt in range(8):
                pr.add("pe", (lambda och=och, kt=kt: lambda e: e.matmul(
                    pA[:, :], oT_t[:, kt * 128:(kt + 1) * 128],
                    wosl[:, kt * 2048 + och * 512: kt * 2048 + (och + 1) * 512],
                    start=(kt == 0), stop=(kt == 7)))(),
                       reads=("oT_t", "xTb"), writes=("pA",))
            pr.add("act", (lambda och=och: lambda e: e.activation(
                out=ysb[:, och * 512:(och + 1) * 512], in_=pA[:, :], func=AF.Copy))(),
                   reads=("pA",), writes=("qraw",))
        pr.add("dve", lambda e: e.tensor_add(xr[:, :], ysb[:, :], xr[:, :]),
               reads=("qraw", "xr"), writes=("xr",))
        emit_ln2x(pr, nc, xr, "xr", sqb, st1, ysb, "qraw", 1e-5, 2048, sq_res="sqb")
        pr.dma("pool", x3[base + tt * 128: base + (tt + 1) * 128, :], ysb[:, :], reads=("qraw",))




F1_D = 2048
F1_EPS_RMS = 1e-6
F1_EPS_LN = 1e-5


def build_l1ffn(pr, nc, h, name="f1", T=4096, skip_dequant=False):
    es = h["es"]
    sb = lambda nm, shp, dt: es.enter_context(nc.sbuf_tensor(f"{name}_{nm}", shp, dt))
    ps = lambda nm, shp, dt: es.enter_context(nc.psum_tensor(f"{name}_{nm}", shp, dt))
    TC = 512
    NCH = T // TC
    NGU = 88
    NDT = 44
    KT = 16

    x3, xout = h["x3"], h["xout"]

    pk = sb("pk", [128, 704], I32)
    shi = sb("shi", [128, 704], I32)
    mi = sb("mi", [128, 704], I32)
    d8 = sb("d8", [128, 704], I32)
    mh = sb("mh", [128, 704], F32)
    sgn = sb("sgn", [128, 704], F32)
    r1 = sb("r1", [128, 704], F32)
    r2 = sb("r2", [128, 704], F32)
    tsum = sb("tsum", [128, 704], F32)
    slab = sb("slab", [128, 5632], F32)
    slabT = sb("slabT", [128, F1_D], BF16)
    sgu = sb("sgu", [128, NGU * 16], F32)
    sdw = sb("sdw", [128, 16 * NDT], F32)
    idf = sb("idf", [128, 128], F32)

    xa = [sb(f"xa{i}", [128, F1_D], F32) for i in range(2)]
    xTc = sb("xTc", [128, KT * TC], BF16)
    wg_sb = [sb(f"wg{i}", [128, KT * 128], BF16) for i in range(2)]
    wd_sb = [sb(f"wd{i}", [128, 1024], BF16) for i in range(2)]
    tg = sb("tg", [128, TC], F32)
    tu = sb("tu", [128, TC], F32)
    fuse = sb("fuse", [128, NDT * TC], BF16)
    ysb = [sb(f"ysb{i}", [128, F1_D], F32) for i in range(2)]
    st1 = [sb(f"st1{i}", [128, 8], F32) for i in range(2)]
    sqb = sb("sqb", [128, F1_D], BF16)

    pg = ps("pg", [128, 512], F32)
    pu = ps("pu", [128, 512], F32)
    pdb = [ps(f"pdb{i}", [128, 512], F32) for i in range(4)]
    ptr = ps("ptr", [128, 512], F32)

    pr.dma("sp", sgu[:, :], h["sgu_c"][:, :], writes=("sgu",))
    pr.dma("sp", sdw[:, :], h["sdw_c"][:, :], writes=("sdw",))
    pr.dma("sp", idf[:, :], h["identf"][:, :], writes=("idf",))

    def dequant(W):
        for s in range(8):
            pr.add("dve", (lambda s=s, W=W: lambda e: e.tensor_single_scalar(
                shi[:, 0:W], pk[:, 0:W], s * 4, ALU.logical_shift_right))(),
                   reads=("pk",), writes=("shi",))
            pr.add("dve", (lambda W=W: lambda e: e.tensor_single_scalar(
                mi[:, 0:W], shi[:, 0:W], 7, ALU.bitwise_and))(),
                   reads=("shi",), writes=("mi",))
            pr.add("dve", (lambda W=W: lambda e: e.tensor_single_scalar(
                d8[:, 0:W], shi[:, 0:W], 8, ALU.bitwise_and))(),
                   reads=("shi",), writes=("d8",))
            pr.add("dve", (lambda W=W: lambda e: e.tensor_scalar_mul(
                mh[:, 0:W], mi[:, 0:W], 0.5))(),
                   reads=("mi",), writes=("mh",))
            pr.add("dve", (lambda W=W: lambda e: e.tensor_scalar(
                sgn[:, 0:W], d8[:, 0:W], -0.25, 1.0, ALU.mult, ALU.add))(),
                   reads=("d8",), writes=("sgn",))
            pr.add("dve", (lambda W=W: lambda e: e.tensor_scalar(
                r1[:, 0:W], mh[:, 0:W], -2.0, 0.0, ALU.add, ALU.max))(),
                   reads=("mh",), writes=("r1",))
            pr.add("dve", (lambda W=W: lambda e: e.tensor_scalar(
                r2[:, 0:W], mh[:, 0:W], -3.0, 0.0, ALU.add, ALU.max))(),
                   reads=("mh",), writes=("r2",))
            pr.add("dve", (lambda W=W: lambda e: e.scalar_tensor_tensor(
                tsum[:, 0:W], r2[:, 0:W], 2.0, r1[:, 0:W], ALU.mult, ALU.add))(),
                   reads=("r1", "r2"), writes=("tsum",))
            pr.add("dve", (lambda W=W: lambda e: e.tensor_add(
                tsum[:, 0:W], tsum[:, 0:W], mh[:, 0:W]))(),
                   reads=("tsum", "mh"), writes=("tsum",))
            pr.add("dve", (lambda s=s, W=W: lambda e: e.tensor_mul(
                slab[:, s:8 * W:8], tsum[:, 0:W], sgn[:, 0:W]))(),
                   reads=("tsum", "sgn"), writes=("slab",))

    for ot in range(NGU if not skip_dequant else 0):
        pr.dma("sp", pk[:, 0:256], h["gup_t"][ot], writes=("pk",))
        dequant(256)
        for kt in range(KT):
            pr.add("dve", (lambda ot=ot, kt=kt: lambda e: e.tensor_scalar_mul(
                slab[:, kt * 128:(kt + 1) * 128], slab[:, kt * 128:(kt + 1) * 128],
                sgu[:, ot * 16 + kt: ot * 16 + kt + 1]))(),
                   reads=("slab", "sgu"), writes=("slab",))
        for kt in range(KT):
            pr.add("pe", (lambda kt=kt: lambda e: e.transpose(
                ptr[:, 0:128], slab[:, kt * 128:(kt + 1) * 128], idf[:, :]))(),
                   reads=("slab", "idf"), writes=("ptr",))
            pr.add("act", (lambda kt=kt: lambda e: e.activation(
                out=slabT[:, kt * 128:(kt + 1) * 128], in_=ptr[:, 0:128], func=AF.Copy))(),
                   reads=("ptr",), writes=("slabT",))
        pr.dma("pool", h["wgu_s"][ot], slabT[:, :], reads=("slabT",), writes=(f"wgu{ot}",))

    for ot in range(16 if not skip_dequant else 0):
        pr.dma("sp", pk[:, 0:704], h["wdp_t"][ot], writes=("pk",))
        dequant(704)
        for g in range(NDT):
            pr.add("dve", (lambda ot=ot, g=g: lambda e: e.tensor_scalar_mul(
                slab[:, g * 128:(g + 1) * 128], slab[:, g * 128:(g + 1) * 128],
                sdw[:, ot * NDT + g: ot * NDT + g + 1]))(),
                   reads=("slab", "sdw"), writes=("slab",))
        for g in range(NDT):
            pr.add("pe", (lambda g=g: lambda e: e.transpose(
                ptr[:, 0:128], slab[:, g * 128:(g + 1) * 128], idf[:, :]))(),
                   reads=("slab", "idf"), writes=("ptr",))
            pr.add("act", (lambda g=g: lambda e: e.activation(
                out=slabT[:, 0:128], in_=ptr[:, 0:128], func=AF.Copy))(),
                   reads=("ptr",), writes=("slabT",))
            pr.dma("pool", h["wdt_s"][g][:, ot * 128:(ot + 1) * 128], slabT[:, 0:128],
                   reads=("slabT",), writes=(f"wdt{g}",))

    for c in range(NCH):
        for tt in range(4):
            trow = c * TC + tt * 128
            pr.dma("sp", xa[0][:, :], x3[trow:trow + 128, :], writes=("xa0",))
            for kt in range(KT):
                pr.add("pe", (lambda kt=kt: lambda e: e.transpose(
                    ptr[:, 0:128], xa[0][:, kt * 128:(kt + 1) * 128], idf[:, :]))(),
                       reads=("xa0", "idf"), writes=("ptr",))
                pr.add("act", (lambda kt=kt, tt=tt: lambda e: e.activation(
                    out=xTc[:, kt * TC + tt * 128: kt * TC + (tt + 1) * 128],
                    in_=ptr[:, 0:128], func=AF.Copy))(),
                       reads=("ptr",), writes=("xTc",))
        for j in range(NDT):
            bw = j % 2
            pr.dma("sp", wg_sb[bw][:, :], h["wgu_s"][j], reads=(f"wgu{j}",), writes=(f"wg{bw}",))
            for kt in range(KT):
                pr.add("pe", (lambda bw=bw, kt=kt: lambda e: e.matmul(
                    pg[:, :], wg_sb[bw][:, kt * 128:(kt + 1) * 128],
                    xTc[:, kt * TC:(kt + 1) * TC],
                    start=(kt == 0), stop=(kt == KT - 1)))(),
                       reads=(f"wg{bw}", "xTc"), writes=("pg",))
            pr.dma("sp", wg_sb[bw][:, :], h["wgu_s"][j + NDT], reads=(f"wgu{j + NDT}",), writes=(f"wg{bw}",))
            for kt in range(KT):
                pr.add("pe", (lambda bw=bw, kt=kt: lambda e: e.matmul(
                    pu[:, :], wg_sb[bw][:, kt * 128:(kt + 1) * 128],
                    xTc[:, kt * TC:(kt + 1) * TC],
                    start=(kt == 0), stop=(kt == KT - 1)))(),
                       reads=(f"wg{bw}", "xTc"), writes=("pu",))
            pr.add("act", lambda e: e.activation(out=tg[:, :], in_=pg[:, :], func=AF.Silu),
                   reads=("pg",), writes=("tg",))
            pr.add("act", lambda e: e.activation(out=tu[:, :], in_=pu[:, :], func=AF.Copy),
                   reads=("pu",), writes=("tu",))
            pr.add("dve", (lambda j=j: lambda e: e.tensor_mul(
                fuse[:, j * TC:(j + 1) * TC], tg[:, :], tu[:, :]))(),
                   reads=("tg", "tu"), writes=("fuse",))
        for ttp in range(2):
            xr0 = slab[:, 2048:4096]
            for tti in range(2):
                trow = c * TC + (ttp * 2 + tti) * 128
                if tti == 0:
                    pr.dma("sp", xr0[:, :], x3[trow:trow + 128, :], writes=("slab",))
                else:
                    pr.dma("sp", xa[tti][:, :], x3[trow:trow + 128, :], writes=(f"xa{tti}",))
            for ochp in range(2):
                for it in range(NDT):
                    bw = it % 2
                    pr.dma("pool", wd_sb[bw][:, :],
                           h["wdt_s"][it][:, ochp * 1024:(ochp + 1) * 1024],
                           reads=(f"wdt{it}",), writes=(f"wd{bw}",))
                    for tti in range(2):
                        tt = ttp * 2 + tti
                        for oi in range(2):
                            pr.add("pe", (lambda bw=bw, it=it, tt=tt, tti=tti, oi=oi:
                                          lambda e: e.matmul(
                                pdb[tti * 2 + oi][:, :],
                                fuse[:, it * TC + tt * 128: it * TC + (tt + 1) * 128],
                                wd_sb[bw][:, oi * 512:(oi + 1) * 512],
                                start=(it == 0), stop=(it == NDT - 1)))(),
                                   reads=(f"wd{bw}", "fuse"), writes=(f"pdb{tti * 2 + oi}",))
                for tti in range(2):
                    for oi in range(2):
                        och = ochp * 2 + oi
                        pr.add("act", (lambda tti=tti, oi=oi, och=och: lambda e: e.activation(
                            out=ysb[tti][:, och * 512:(och + 1) * 512],
                            in_=pdb[tti * 2 + oi][:, :], func=AF.Copy))(),
                               reads=(f"pdb{tti * 2 + oi}",), writes=(f"ysb{tti}",))
            for tti in range(2):
                tt = ttp * 2 + tti
                trow = c * TC + tt * 128
                y, xr, stt = ysb[tti], (xr0 if tti == 0 else xa[tti]), st1[tti]
                yres, xres = f"ysb{tti}", ("slab" if tti == 0 else f"xa{tti}")
                pr.add("act", (lambda y=y: lambda e: e.activation(out=sqb[:, :], in_=y[:, :], func=AF.Square))(),
                       reads=(yres,), writes=("sqb",))
                pr.add("dve", (lambda stt=stt: lambda e: e.reduce_sum(stt[:, 0:1], sqb[:, :], axis=AX.X))(),
                       reads=("sqb",), writes=(f"st1{tti}",))
                pr.add("dve", (lambda stt=stt: lambda e: e.tensor_scalar(
                    stt[:, 1:2], stt[:, 0:1], 1.0 / F1_D, float(F1_EPS_RMS), ALU.mult, ALU.add))(),
                       reads=(f"st1{tti}",), writes=(f"st1{tti}",))
                pr.add("act", (lambda stt=stt: lambda e: e.activation(out=stt[:, 2:3], in_=stt[:, 1:2], func=AF.Ln))(),
                       reads=(f"st1{tti}",), writes=(f"st1{tti}",))
                pr.add("dve", (lambda stt=stt: lambda e: e.tensor_scalar_mul(stt[:, 2:3], stt[:, 2:3], -0.5))(),
                       reads=(f"st1{tti}",), writes=(f"st1{tti}",))
                pr.add("act", (lambda stt=stt: lambda e: e.activation(out=stt[:, 3:4], in_=stt[:, 2:3], func=AF.Exp))(),
                       reads=(f"st1{tti}",), writes=(f"st1{tti}",))
                pr.add("dve", (lambda y=y, xr=xr, stt=stt: lambda e: e.scalar_tensor_tensor(
                    xr[:, :], y[:, :], stt[:, 3:4], xr[:, :], ALU.mult, ALU.add))(),
                       reads=(yres, f"st1{tti}", xres), writes=(xres,))
                emit_ln2x(pr, nc, xr, xres, sqb, stt, y, yres, F1_EPS_LN, F1_D, sq_res="sqb")
                pr.dma("pool", xout[trow:trow + 128, :], y[:, :], reads=(yres,))
    return


import numpy as np
import ml_dtypes as _mld
from concourse.bass_utils import run_bass_kernel_spmd

_BF = _mld.bfloat16
_NCORES = 8


def _consts_l0attn():
    DH, S = 128, 2048
    inv_freq = (1.0 / 10000.0 ** (np.arange(0, DH, 2) / DH)).astype(np.float32)
    ang = np.outer(np.arange(S), inv_freq)
    cosT = np.ascontiguousarray(np.cos(ang).T).astype(np.float32)
    sinT = np.ascontiguousarray(np.sin(ang).T).astype(np.float32)
    costab = np.concatenate([cosT, cosT], axis=0).astype(_BF)
    sintab = np.concatenate([-sinT, sinT], axis=0).astype(_BF)
    pswap = np.zeros((128, 128), np.float32)
    for kk in range(128):
        pswap[kk, (kk + 64) % 128] = 1.0
    r = np.arange(128)
    cmask = np.where(r[None, :] <= r[:, None], 0.0, -1e30).astype(np.float32).astype(_BF)
    cmaskT = np.zeros((4, 128, 512), np.float32)
    for kk in range(4):
        cmaskT[kk, :, :kk * 128] = -1e30
        blk = np.where(r[:, None] <= r[None, :], 0.0, -1e30)
        cmaskT[kk, :, kk * 128:(kk + 1) * 128] = blk
    cmaskT = np.ascontiguousarray(cmaskT.transpose(1, 0, 2).reshape(128, 2048)).astype(_BF)
    return dict(costab=costab, sintab=sintab, pswap=pswap.astype(_BF),
                cmask=cmask, cmaskT=cmaskT,
                identf=np.eye(128, dtype=np.float32),
                ones_col_f=np.ones((128, 1), np.float32),
                ones_col_b=np.ones((128, 1), dtype=_BF),
                ones_row_f=np.ones((1, 128), np.float32),
                ones_row_b=np.ones((1, 128), dtype=_BF))


def kernel(**inp):
    x = np.ascontiguousarray(np.asarray(inp["L0_attn_x"], dtype=np.float32))
    B, S, D = x.shape
    shards = [np.ascontiguousarray(x[2 * c:2 * c + 2].reshape(2 * S, D)) for c in range(_NCORES)]
    f32 = np.float32
    asf = lambda a: np.ascontiguousarray(np.asarray(a, dtype=f32))

    cA = _consts_l0attn()
    wq = asf(inp["L0_attn_wq"]); wk = asf(inp["L0_attn_wk"])
    wv = asf(inp["L0_attn_wv"]); wo = asf(inp["L0_attn_wo"])
    packA = dict(
        wq_t=np.ascontiguousarray(wq.reshape(16, 128, 16, 128).transpose(2, 1, 0, 3).reshape(16, 128, 2048)),
        wk_t=np.ascontiguousarray(wk.reshape(16, 128, 4, 128).transpose(2, 1, 0, 3).reshape(4, 128, 2048)),
        wv_t=np.ascontiguousarray(wv.reshape(16, 128, 4, 128).transpose(2, 1, 0, 3).reshape(4, 128, 2048)),
        wo_t=np.ascontiguousarray(wo.reshape(16, 128, 2048)),
        bq_c=np.ascontiguousarray(asf(inp["L0_attn_bq"]).reshape(16, 128).T),
        bk_c=np.ascontiguousarray(asf(inp["L0_attn_bk"]).reshape(4, 128).T),
        bv_c=np.ascontiguousarray(asf(inp["L0_attn_bv"]).reshape(4, 128).T),
        bo_r=asf(inp["L0_attn_bo"]).reshape(1, 2048),
        qns_c=asf(inp["L0_attn_q_norm_scale"]).reshape(128, 1),
        kns_c=asf(inp["L0_attn_k_norm_scale"]).reshape(128, 1),
        **cA)
    nc = bass.Bass("TRN2", target_bir_lowering=False)
    hh = {}
    hh["x0"] = nc.dram_tensor("x0", [2 * S, D], F32, kind="ExternalInput").ap()
    for nm, shp, dt in [("wq_t", [16, 128, 2048], F32), ("wk_t", [4, 128, 2048], F32),
                        ("wv_t", [4, 128, 2048], F32), ("wo_t", [16, 128, 2048], F32),
                        ("bq_c", [128, 16], F32), ("bk_c", [128, 4], F32), ("bv_c", [128, 4], F32),
                        ("bo_r", [1, 2048], F32), ("qns_c", [128, 1], F32), ("kns_c", [128, 1], F32),
                        ("costab", [128, 2048], BF16), ("sintab", [128, 2048], BF16),
                        ("pswap", [128, 128], BF16), ("cmask", [128, 128], BF16),
                        ("cmaskT", [128, 2048], BF16), ("identf", [128, 128], F32),
                        ("ones_col_f", [128, 1], F32), ("ones_col_b", [128, 1], BF16),
                        ("ones_row_f", [1, 128], F32), ("ones_row_b", [1, 128], BF16)]:
        hh[nm] = nc.dram_tensor(nm, shp, dt, kind="ExternalInput").ap()
    hh["x1"] = nc.dram_tensor("x1", [2 * S, D], F32, kind="ExternalOutput").ap()
    import contextlib
    with contextlib.ExitStack() as es:
        hh["es"] = es
        pr = Prog(nc)
        build_l0attn(pr, nc, hh)
        pr.build_and_emit()
    in_maps = [dict(x0=shards[c], **packA) for c in range(_NCORES)]
    res = run_bass_kernel_spmd(nc, in_maps, core_ids=list(range(_NCORES)))
    x1s = [res.results[c]["x1"] for c in range(_NCORES)]

    w_gate = np.asarray(inp["L0_ffn_w_gate"])
    w_up = np.asarray(inp["L0_ffn_w_up"])
    w_down = np.asarray(inp["L0_ffn_w_down"])
    packB = dict(
        wg_t=np.ascontiguousarray(w_gate.reshape(16, 128, 64, 128).transpose(2, 1, 0, 3).reshape(64, 128, 2048)),
        wu_t=np.ascontiguousarray(w_up.reshape(16, 128, 64, 128).transpose(2, 1, 0, 3).reshape(64, 128, 2048)),
        wd_t=np.ascontiguousarray(w_down.reshape(64, 128, 16, 128).transpose(2, 1, 0, 3).reshape(16, 128, 8192)),
        sg_c=np.ascontiguousarray(asf(inp["L0_ffn_s_gate"]).reshape(64, 128).T),
        su_c=np.ascontiguousarray(asf(inp["L0_ffn_s_up"]).reshape(64, 128).T),
        sd_c=np.ascontiguousarray(asf(inp["L0_ffn_s_down"]).reshape(16, 128).T),
        identf=np.eye(128, dtype=np.float32))
    nc = bass.Bass("TRN2", target_bir_lowering=False)
    hh = {}
    hh["x1"] = nc.dram_tensor("x1", [2 * S, D], F32, kind="ExternalInput").ap()
    for nm, shp, dt in [("wg_t", [64, 128, 2048], I8), ("wu_t", [64, 128, 2048], I8),
                        ("wd_t", [16, 128, 8192], I8), ("sg_c", [128, 64], F32),
                        ("su_c", [128, 64], F32), ("sd_c", [128, 16], F32),
                        ("identf", [128, 128], F32)]:
        hh[nm] = nc.dram_tensor(nm, shp, dt, kind="ExternalInput").ap()
    hh["x2"] = nc.dram_tensor("x2", [2 * S, D], F32, kind="ExternalOutput").ap()
    with contextlib.ExitStack() as es:
        hh["es"] = es
        hh["x2_out"] = hh["x2"]
        pr = Prog(nc)
        h2 = dict(hh)
        h2["x2"] = hh["x2"]
        build_l0ffn(pr, nc, dict(hh, x1=hh["x1"], x2=hh["x2"]))
        pr.build_and_emit()
    in_maps = [dict(x1=x1s[c], **packB) for c in range(_NCORES)]
    res = run_bass_kernel_spmd(nc, in_maps, core_ids=list(range(_NCORES)))
    x2s = [res.results[c]["x2"] for c in range(_NCORES)]

    Wq = asf(inp["L1_attn_Wq"]); Wk = asf(inp["L1_attn_Wk"])
    Wv = asf(inp["L1_attn_Wv"]); Wg = asf(inp["L1_attn_Wg"])
    Wb = asf(inp["L1_attn_Wb"]); Wa = asf(inp["L1_attn_Wa"])
    Wo2 = asf(inp["L1_attn_Wo"])
    cq = asf(inp["L1_attn_conv_q_w"]); ck = asf(inp["L1_attn_conv_k_w"]); cv = asf(inp["L1_attn_conv_v_w"])
    r = np.arange(128)
    sarr = np.arange(64)
    packC = dict(
        wq_t=np.ascontiguousarray(Wq.reshape(16, 128, 8, 128).transpose(2, 1, 0, 3).reshape(8, 128, 2048)),
        wk_t=np.ascontiguousarray(Wk.reshape(16, 128, 16, 128).transpose(2, 1, 0, 3).reshape(16, 128, 2048)),
        wv_t=np.ascontiguousarray(Wv.reshape(16, 128, 16, 128).transpose(2, 1, 0, 3).reshape(16, 128, 2048)),
        wg_t=np.ascontiguousarray(Wg.reshape(16, 128, 8, 128).transpose(2, 1, 0, 3).reshape(8, 128, 2048)),
        wb_c=np.ascontiguousarray(Wb.reshape(16, 128, 16).transpose(1, 0, 2).reshape(128, 256)),
        wa_c=np.ascontiguousarray(Wa.reshape(16, 128, 8).transpose(1, 0, 2).reshape(128, 128)),
        wo_t=np.ascontiguousarray(Wo2.reshape(8, 128, 2048)),
        nw_c=asf(inp["L1_attn_norm_w"]).reshape(128, 1),
        alog_c=asf(inp["L1_attn_A_log"]).reshape(8, 1),
        dtb_c=asf(inp["L1_attn_dt_bias"]).reshape(8, 1),
        cq_c=np.ascontiguousarray(cq[:, 0, :].reshape(8, 128, 4).transpose(1, 0, 2).reshape(128, 32)),
        ck_c=np.ascontiguousarray(ck[:, 0, :].reshape(16, 128, 4).transpose(1, 0, 2).reshape(128, 64)),
        cv_c=np.ascontiguousarray(cv[:, 0, :].reshape(16, 128, 4).transpose(1, 0, 2).reshape(128, 64)),
        identf=np.eye(128, dtype=np.float32),
        ones_row_f=np.ones((1, 128), np.float32), one_f=np.ones((1, 1), np.float32),
        mx_c=np.where(r[:, None] < r[None, :], 0.0, -1e30).astype(np.float32),
        mq_c=np.where(r[:, None] <= (2 * sarr + 1)[None, :], 0.0, -1e30).astype(np.float32),
        ident_bf=np.eye(128, dtype=_BF), one_b=np.ones((1, 1), dtype=_BF),
        ones_col_b=np.ones((128, 1), dtype=_BF))
    nc = bass.Bass("TRN2", target_bir_lowering=False)
    hh = {}
    hh["x2"] = nc.dram_tensor("x2", [2 * S, D], F32, kind="ExternalInput").ap()
    for nm, shp, dt in [("wq_t", [8, 128, 2048], F32), ("wk_t", [16, 128, 2048], F32),
                        ("wv_t", [16, 128, 2048], F32), ("wg_t", [8, 128, 2048], F32),
                        ("wb_c", [128, 256], F32), ("wa_c", [128, 128], F32),
                        ("wo_t", [8, 128, 2048], F32), ("nw_c", [128, 1], F32),
                        ("alog_c", [8, 1], F32), ("dtb_c", [8, 1], F32),
                        ("cq_c", [128, 32], F32), ("ck_c", [128, 64], F32), ("cv_c", [128, 64], F32),
                        ("identf", [128, 128], F32), ("ones_row_f", [1, 128], F32),
                        ("one_f", [1, 1], F32), ("mx_c", [128, 128], F32), ("mq_c", [128, 64], F32),
                        ("ident_bf", [128, 128], BF16), ("one_b", [1, 1], BF16),
                        ("ones_col_b", [128, 1], BF16)]:
        hh[nm] = nc.dram_tensor(nm, shp, dt, kind="ExternalInput").ap()
    hh["o_s"] = nc.dram_tensor("o_s", [2, 2048, 1024], BF16, kind="Internal").ap()
    hh["g_s"] = nc.dram_tensor("g_s", [2, 2048, 1024], BF16, kind="Internal").ap()
    hh["x3"] = nc.dram_tensor("x3", [2 * S, D], F32, kind="ExternalOutput").ap()
    with contextlib.ExitStack() as es:
        hh["es"] = es
        pr = Prog(nc)
        build_l1attn(pr, nc, hh)
        pr.build_and_emit()
    in_maps = [dict(x2=x2s[c], **packC) for c in range(_NCORES)]
    res = run_bass_kernel_spmd(nc, in_maps, core_ids=list(range(_NCORES)))
    x3s = [res.results[c]["x3"] for c in range(_NCORES)]

    gup = np.asarray(inp["L1_ffn_gate_up_packed"])
    gus = np.asarray(inp["L1_ffn_gate_up_scales"])
    dnp_ = np.asarray(inp["L1_ffn_down_packed"])
    dns = np.asarray(inp["L1_ffn_down_scales"])
    packD = dict(
        gup_t=np.ascontiguousarray(gup.reshape(88, 128, 256)),
        sgu_c=np.ascontiguousarray(gus.astype(np.float32).T.reshape(88, 128, 16).transpose(1, 0, 2).reshape(128, 88 * 16)),
        wdp_t=np.ascontiguousarray(dnp_.reshape(16, 128, 704)),
        sdw_c=np.ascontiguousarray(dns.astype(np.float32).T.reshape(16, 128, 44).transpose(1, 0, 2).reshape(128, 16 * 44)),
        identf=np.eye(128, dtype=np.float32))
    nc = bass.Bass("TRN2", target_bir_lowering=False)
    hh = {}
    hh["x3"] = nc.dram_tensor("x3", [2 * S, D], F32, kind="ExternalInput").ap()
    for nm, shp, dt in [("gup_t", [88, 128, 256], I32), ("sgu_c", [128, 88 * 16], F32),
                        ("wdp_t", [16, 128, 704], I32), ("sdw_c", [128, 16 * 44], F32),
                        ("identf", [128, 128], F32)]:
        hh[nm] = nc.dram_tensor(nm, shp, dt, kind="ExternalInput").ap()
    hh["wgu_s"] = nc.dram_tensor("wgu_s", [88, 128, 2048], BF16, kind="Internal").ap()
    hh["wdt_s"] = nc.dram_tensor("wdt_s", [44, 128, 2048], BF16, kind="Internal").ap()
    hh["xout"] = nc.dram_tensor("xout", [2 * S, D], F32, kind="ExternalOutput").ap()
    with contextlib.ExitStack() as es:
        hh["es"] = es
        pr = Prog(nc)
        build_l1ffn(pr, nc, hh)
        pr.build_and_emit()
    in_maps = [dict(x3=x3s[c], **packD) for c in range(_NCORES)]
    res = run_bass_kernel_spmd(nc, in_maps, core_ids=list(range(_NCORES)))
    outs = [res.results[c]["xout"] for c in range(_NCORES)]

    out = np.concatenate([o.reshape(2, S, D) for o in outs], axis=0)
    return np.ascontiguousarray(out.astype(np.float32))


# version 04 (final)
# speedup vs baseline: 1.1621x; 1.1621x over previous
import contextlib
import numpy as np
import concourse.bass as bass
import concourse.mybir as mybir

F32 = mybir.dt.float32
BF16 = mybir.dt.bfloat16
I8 = mybir.dt.int8
U8 = mybir.dt.uint8
I32 = mybir.dt.int32
F16 = mybir.dt.float16
AF = mybir.ActivationFunctionType
ALU = mybir.AluOpType
AX = mybir.AxisListType

DMA_ENGINES = ("sp", "pool")


import os
HEAVY_SYNC = os.environ.get("FW_HEAVY_SYNC", "0") == "1"
HEAVY_SRC_DEFAULT = "pe"
HEAVY_DST_DEFAULT = "sp,pool"


class Prog:
    def __init__(self, nc):
        self.nc = nc
        self.ops = []

    def add(self, engine, emit, reads=(), writes=()):
        self.ops.append([engine, emit, tuple(reads), tuple(writes), False])

    def dma(self, queue, out, in_, reads=(), writes=()):
        self.add(queue, lambda e: e.dma_start(out=out, in_=in_), reads, writes)

    def build_and_emit(self):
        nc = self.nc
        n = len(self.ops)
        deps = [set() for _ in range(n)]
        last_writer = {}
        readers = {}
        for i, (eng, emit, rd, wr, _) in enumerate(self.ops):
            for r in rd:
                if r in last_writer:
                    deps[i].add(last_writer[r])
            for w in wr:
                if w in last_writer:
                    deps[i].add(last_writer[w])
                for j in readers.get(w, ()):
                    if j != i:
                        deps[i].add(j)
            for r in rd:
                readers.setdefault(r, []).append(i)
            for w in wr:
                last_writer[w] = i
                readers[w] = []
        same_eng_dep = [set() for _ in range(n)]
        cross_dep = [set() for _ in range(n)]
        for op in self.ops:
            if op[0] in DMA_ENGINES:
                op[4] = True
        for i, (eng, *_rest) in enumerate(self.ops):
            for d in deps[i]:
                deng = self.ops[d][0]
                if deng == eng and eng not in DMA_ENGINES:
                    if eng in ("dve", "act"):
                        same_eng_dep[i].add(d)
                else:
                    if deng == eng:
                        cross_dep[i].add(d)
                    else:
                        cross_dep[i].add(d)
                    self.ops[d][4] = True
        if HEAVY_SYNC:
            for op in self.ops:
                op[4] = True
            last_by_eng = {}
            for i, op in enumerate(self.ops):
                for e, j in last_by_eng.items():
                    if e != op[0]:
                        cross_dep[i].add(j)
                last_by_eng[op[0]] = i
        elif os.environ.get("FW_HEAVY_SRC", HEAVY_SRC_DEFAULT):
            srcs = os.environ.get("FW_HEAVY_SRC", HEAVY_SRC_DEFAULT).split(",")
            dsts = os.environ.get("FW_HEAVY_DST", HEAVY_DST_DEFAULT)
            dsts = set(dsts.split(",")) if dsts else None
            last_by_eng = {}
            for i, op in enumerate(self.ops):
                for e in srcs:
                    j = last_by_eng.get(e)
                    if (j is not None and e != op[0]
                            and (dsts is None or op[0] in dsts)):
                        cross_dep[i].add(j)
                        self.ops[j][4] = True
                last_by_eng[op[0]] = i
        inc_rank = {}
        cnt = {}
        for i, op in enumerate(self.ops):
            if op[4]:
                c = cnt.get(op[0], 0) + 1
                cnt[op[0]] = c
                inc_rank[i] = c
        with contextlib.ExitStack() as es:
            sems = {}
            for e in ("pe", "dve", "act", "sp", "pool"):
                sems[e] = es.enter_context(nc.semaphore(f"sem_{e}"))
            block = es.enter_context(nc.Block())
            eng_objs = {}

            def make_section(eng_name, eng_ops):
                def section(eng):
                    clock = {}
                    last_drained = -1
                    for i in eng_ops:
                        _, emit, rd, wr, needs_inc = self.ops[i]
                        if eng_name in DMA_ENGINES:
                            pv = (inc_rank[i] - 1) * 16
                            if pv > 0 and clock.get(eng_name, 0) < pv:
                                eng.wait_ge(sems[eng_name], pv)
                                clock[eng_name] = pv
                        if any(d >= last_drained for d in same_eng_dep[i]):
                            eng.drain()
                            last_drained = i
                        want = {}
                        for d in cross_dep[i]:
                            deng = self.ops[d][0]
                            step = 16 if deng in DMA_ENGINES else 1
                            v = inc_rank[d] * step
                            if v > want.get(deng, 0):
                                want[deng] = v
                        for deng, v in want.items():
                            if clock.get(deng, 0) < v:
                                eng.wait_ge(sems[deng], v)
                                clock[deng] = v
                        inst = emit(eng)
                        if needs_inc:
                            step = 16 if self.ops[i][0] in DMA_ENGINES else 1
                            if eng_name in ("dve", "act"):
                                eng.drain().then_inc(sems[eng_name], step)
                                last_drained = i
                            else:
                                inst.then_inc(sems[self.ops[i][0]], step)
                    return
                return section

            by_eng = {}
            for i, op in enumerate(self.ops):
                by_eng.setdefault(op[0], []).append(i)

            finals = {e: cnt.get(e, 0) * (16 if e in DMA_ENGINES else 1)
                      for e in ("pe", "dve", "act", "sp", "pool")}

            if "sp" in by_eng:
                @block.sync
                def _(sync):
                    make_section("sp", by_eng["sp"])(sync)

            if "pool" in by_eng or True:
                @block.gpsimd
                def _(gpsimd):
                    make_section("pool", by_eng.get("pool", []))(gpsimd)
                    for e, v in finals.items():
                        if e != "pool" and v > 0:
                            gpsimd.wait_ge(sems[e], v)
                    if finals["pool"] > 0:
                        gpsimd.wait_ge(sems["pool"], finals["pool"])

            if "act" in by_eng:
                @block.scalar
                def _(scalar):
                    make_section("act", by_eng["act"])(scalar)

            if "dve" in by_eng:
                @block.vector
                def _(vector):
                    make_section("dve", by_eng["dve"])(vector)

            if "pe" in by_eng:
                @block.tensor
                def _(tensor):
                    make_section("pe", by_eng["pe"])(tensor)
        return


def mark_output_dmas(prog):
    pass





def build_l0ffn(pr, nc, h, name="f0", eps=1e-5, B0_D=2048, B0_I=8192, B0_T=4096):
    es = h["es"]
    sb = lambda nm, shp, dt: es.enter_context(nc.sbuf_tensor(f"{name}_{nm}", shp, dt))
    ps = lambda nm, shp, dt: es.enter_context(nc.psum_tensor(f"{name}_{nm}", shp, dt))
    NT = B0_I // 128
    KT = B0_D // 128
    OT = B0_D // 128
    TC = 512
    NCH = B0_T // TC

    x1, wg_t, wu_t, wd_t, sg_c, su_c, sd_c, ident, x2 = (
        h["x1"], h["wg_t"], h["wu_t"], h["wd_t"], h["sg_c"], h["su_c"],
        h["sd_c"], h["identf"], h["x2"])

    xa = sb("xa", [128, B0_D], F32)
    xb = sb("xb", [128, B0_D], F32)
    xTc = sb("xTc", [128, KT * TC], BF16)
    wg_sb = [sb(f"wg{i}", [128, KT * 128], BF16) for i in range(2)]
    wu_sb = [sb(f"wu{i}", [128, KT * 128], BF16) for i in range(2)]
    wd_sb = [sb(f"wd{i}", [128, NT * 128], BF16) for i in range(2)]
    tg = sb("tg", [128, TC], F32)
    tu = sb("tu", [128, TC], F32)
    fuse = sb("fuse", [128, NT * TC], BF16)
    yt = sb("yt", [128, TC], F32)
    ytok = [sb(f"ytok{i}", [128, B0_D], F32) for i in range(4)]
    acc = sb("acc", [128, B0_D], F32)
    sq = sb("sq", [128, B0_D], F32)
    st1 = sb("st1", [128, 8], F32)
    sgc = sb("sgc", [128, NT], F32)
    suc = sb("suc", [128, NT], F32)
    sdc = sb("sdc", [128, OT], F32)
    idf = sb("idf", [128, 128], F32)
    otile = sb("otile", [128, B0_D], F32)

    pg = ps("pg", [128, 512], F32)
    pu = ps("pu", [128, 512], F32)
    pd = ps("pd", [128, 512], F32)
    pt = [ps(f"pt{i}", [128, 128], F32) for i in range(4)]

    pr.dma("sp", sgc[:, :], sg_c[:, :], writes=("sgc",))
    pr.dma("sp", suc[:, :], su_c[:, :], writes=("suc",))
    pr.dma("sp", sdc[:, :], sd_c[:, :], writes=("sdc",))
    pr.dma("sp", idf[:, :], ident[:, :], writes=("idf",))

    for c in range(NCH):
        for tt in range(4):
            trow = c * TC + tt * 128
            pr.dma("sp", xa[:, :], x1[trow:trow + 128, :], writes=("xa",))
            for kt in range(KT):
                b = kt % 4
                pr.add("pe", (lambda xa=xa, kt=kt, b=b:
                              lambda e: e.transpose(pt[b][:, :], xa[:, kt * 128:(kt + 1) * 128], idf[:, :]))(),
                       reads=("xa", "idf"), writes=(f"pt{b}",))
                dst = xTc[:, kt * TC + tt * 128: kt * TC + (tt + 1) * 128]
                pr.add("act", (lambda dst=dst, b=b:
                               lambda e: e.activation(out=dst, in_=pt[b][:, :], func=AF.Copy))(),
                       reads=(f"pt{b}",), writes=("xTc",))
        for nt in range(NT):
            bw = nt % 2
            pr.dma("pool", wg_sb[bw][:, :], wg_t[nt], writes=(f"wg{bw}",))
            pr.dma("pool", wu_sb[bw][:, :], wu_t[nt], writes=(f"wu{bw}",))
            for kt in range(KT):
                pr.add("pe", (lambda bw=bw, kt=kt:
                              lambda e: e.matmul(pg[:, :], wg_sb[bw][:, kt * 128:(kt + 1) * 128],
                                                 xTc[:, kt * TC:(kt + 1) * TC],
                                                 start=(kt == 0), stop=(kt == KT - 1)))(),
                       reads=(f"wg{bw}", "xTc"), writes=("pg",))
            for kt in range(KT):
                pr.add("pe", (lambda bw=bw, kt=kt:
                              lambda e: e.matmul(pu[:, :], wu_sb[bw][:, kt * 128:(kt + 1) * 128],
                                                 xTc[:, kt * TC:(kt + 1) * TC],
                                                 start=(kt == 0), stop=(kt == KT - 1)))(),
                       reads=(f"wu{bw}", "xTc"), writes=("pu",))
            pr.add("act", (lambda nt=nt:
                           lambda e: e.activation(out=tg[:, :], in_=pg[:, :], func=AF.Silu,
                                                  scale=sgc[:, nt:nt + 1]))(),
                   reads=("pg", "sgc"), writes=("tg",))
            pr.add("act", (lambda nt=nt:
                           lambda e: e.activation(out=tu[:, :], in_=pu[:, :], func=AF.Copy,
                                                  scale=suc[:, nt:nt + 1]))(),
                   reads=("pu", "suc"), writes=("tu",))
            pr.add("dve", (lambda nt=nt:
                           lambda e: e.tensor_mul(fuse[:, nt * TC:(nt + 1) * TC], tg[:, :], tu[:, :]))(),
                   reads=("tg", "tu"), writes=("fuse",))
        for ot in range(OT):
            bw = ot % 2
            pr.dma("pool", wd_sb[bw][:, :], wd_t[ot], writes=(f"wd{bw}",))
            for k2 in range(NT):
                pr.add("pe", (lambda bw=bw, k2=k2:
                              lambda e: e.matmul(pd[:, :], wd_sb[bw][:, k2 * 128:(k2 + 1) * 128],
                                                 fuse[:, k2 * TC:(k2 + 1) * TC],
                                                 start=(k2 == 0), stop=(k2 == NT - 1)))(),
                       reads=(f"wd{bw}", "fuse"), writes=("pd",))
            pr.add("act", (lambda ot=ot:
                           lambda e: e.activation(out=yt[:, :], in_=pd[:, :], func=AF.Copy,
                                                  scale=sdc[:, ot:ot + 1]))(),
                   reads=("pd", "sdc"), writes=("yt",))
            for tt in range(4):
                b = tt % 4
                pr.add("pe", (lambda tt=tt, b=b:
                              lambda e: e.transpose(pt[b][:, :], yt[:, tt * 128:(tt + 1) * 128], idf[:, :]))(),
                       reads=("yt", "idf"), writes=(f"pt{b}",))
                dst = ytok[tt][:, ot * 128:(ot + 1) * 128]
                pr.add("act", (lambda dst=dst, b=b:
                               lambda e: e.activation(out=dst, in_=pt[b][:, :], func=AF.Copy))(),
                       reads=(f"pt{b}",), writes=(f"ytok{tt}",))
        for tt in range(4):
            trow = c * TC + tt * 128
            pr.dma("sp", xb[:, :], x1[trow:trow + 128, :], writes=("xb",))
            pr.add("dve", (lambda tt=tt: lambda e: e.tensor_add(acc[:, :], ytok[tt][:, :], xb[:, :]))(),
                   reads=(f"ytok{tt}", "xb"), writes=("acc",))
            emit_ln(pr, nc, acc, sq, st1, otile, eps, B0_D)
            pr.dma("pool", x2[trow:trow + 128, :], otile[:, :], reads=("otile",))
    return


def emit_ln(pr, nc, acc, sq, st1, otile, eps, B0_D):
    pr.add("act", lambda e: e.activation(out=sq[:, :], in_=acc[:, :], func=AF.Square),
           reads=("acc",), writes=("sq",))
    pr.add("dve", lambda e: e.reduce_sum(st1[:, 0:1], acc[:, :], axis=AX.X),
           reads=("acc",), writes=("st1",))
    pr.add("dve", lambda e: e.reduce_sum(st1[:, 1:2], sq[:, :], axis=AX.X),
           reads=("sq",), writes=("st1",))
    pr.add("dve", lambda e: e.tensor_scalar_mul(st1[:, 2:3], st1[:, 0:1], 1.0 / B0_D),
           reads=("st1",), writes=("st1",))
    pr.add("dve", lambda e: e.tensor_scalar_mul(st1[:, 3:4], st1[:, 1:2], 1.0 / B0_D),
           reads=("st1",), writes=("st1",))
    pr.add("dve", lambda e: e.scalar_tensor_tensor(st1[:, 4:5], st1[:, 2:3], -1.0,
                                                   st1[:, 2:3], ALU.mult, ALU.mult),
           reads=("st1",), writes=("st1",))
    pr.add("dve", lambda e: e.tensor_add(st1[:, 4:5], st1[:, 4:5], st1[:, 3:4]),
           reads=("st1",), writes=("st1",))
    pr.add("dve", lambda e: e.tensor_scalar_add(st1[:, 4:5], st1[:, 4:5], float(eps)),
           reads=("st1",), writes=("st1",))
    pr.add("act", lambda e: e.activation(out=st1[:, 5:6], in_=st1[:, 4:5], func=AF.Ln),
           reads=("st1",), writes=("st1",))
    pr.add("dve", lambda e: e.tensor_scalar_mul(st1[:, 5:6], st1[:, 5:6], -0.5),
           reads=("st1",), writes=("st1",))
    pr.add("act", lambda e: e.activation(out=st1[:, 6:7], in_=st1[:, 5:6], func=AF.Exp),
           reads=("st1",), writes=("st1",))
    pr.add("dve", lambda e: e.scalar_tensor_tensor(st1[:, 7:8], st1[:, 2:3], -1.0,
                                                   st1[:, 6:7], ALU.mult, ALU.mult),
           reads=("st1",), writes=("st1",))
    pr.add("act", lambda e: e.activation(out=otile[:, :], in_=acc[:, :], func=AF.Identity,
                                         scale=st1[:, 6:7], bias=st1[:, 7:8]),
           reads=("acc", "st1"), writes=("otile",))


def emit_ln2x(pr, nc, acc, acc_res, sq, st1, otile, ot_res, eps, B0_D, sq_res="sq"):
    return _emit_ln_core(pr, nc, acc, acc_res, sq, st1, otile, ot_res, eps, B0_D, sq_res)


def emit_ln2(pr, nc, acc, sq, st1, otile, ot_res, eps, B0_D):
    return _emit_ln_core(pr, nc, acc, "acc", sq, st1, otile, ot_res, eps, B0_D, "sq")


def _emit_ln_core(pr, nc, acc, acc_res, sq, st1, otile, ot_res, eps, B0_D, sq_res="sq"):
    pr.add("act", lambda e: e.activation(out=sq[:, :], in_=acc[:, :], func=AF.Square),
           reads=(acc_res,), writes=(sq_res,))
    pr.add("dve", lambda e: e.reduce_sum(st1[:, 0:1], acc[:, :], axis=AX.X),
           reads=(acc_res,), writes=("st1",))
    pr.add("dve", lambda e: e.reduce_sum(st1[:, 1:2], sq[:, :], axis=AX.X),
           reads=(sq_res,), writes=("st1",))
    pr.add("dve", lambda e: e.tensor_scalar_mul(st1[:, 2:3], st1[:, 0:1], 1.0 / B0_D),
           reads=("st1",), writes=("st1",))
    pr.add("dve", lambda e: e.tensor_scalar_mul(st1[:, 3:4], st1[:, 1:2], 1.0 / B0_D),
           reads=("st1",), writes=("st1",))
    pr.add("dve", lambda e: e.scalar_tensor_tensor(st1[:, 4:5], st1[:, 2:3], -1.0,
                                                   st1[:, 2:3], ALU.mult, ALU.mult),
           reads=("st1",), writes=("st1",))
    pr.add("dve", lambda e: e.tensor_add(st1[:, 4:5], st1[:, 4:5], st1[:, 3:4]),
           reads=("st1",), writes=("st1",))
    pr.add("dve", lambda e: e.tensor_scalar_add(st1[:, 4:5], st1[:, 4:5], float(eps)),
           reads=("st1",), writes=("st1",))
    pr.add("act", lambda e: e.activation(out=st1[:, 5:6], in_=st1[:, 4:5], func=AF.Ln),
           reads=("st1",), writes=("st1",))
    pr.add("dve", lambda e: e.tensor_scalar_mul(st1[:, 5:6], st1[:, 5:6], -0.5),
           reads=("st1",), writes=("st1",))
    pr.add("act", lambda e: e.activation(out=st1[:, 6:7], in_=st1[:, 5:6], func=AF.Exp),
           reads=("st1",), writes=("st1",))
    pr.add("dve", lambda e: e.scalar_tensor_tensor(st1[:, 7:8], st1[:, 2:3], -1.0,
                                                   st1[:, 6:7], ALU.mult, ALU.mult),
           reads=("st1",), writes=("st1",))
    pr.add("act", lambda e: e.activation(out=otile[:, :], in_=acc[:, :], func=AF.Identity,
                                         scale=st1[:, 6:7], bias=st1[:, 7:8]),
           reads=(acc_res, "st1"), writes=(ot_res,))




A0_H, A0_G, A0_DH, A0_D, A0_S = 16, 4, 128, 2048, 2048
A0_EPSR = 1e-6


def build_l0attn(pr, nc, h, name="a0"):
    es = h["es"]
    sb = lambda nm, shp, dt: es.enter_context(nc.sbuf_tensor(f"{name}_{nm}", shp, dt))
    ps = lambda nm, shp, dt: es.enter_context(nc.psum_tensor(f"{name}_{nm}", shp, dt))

    x0, x1 = h["x0"], h["x1"]
    KT = A0_D // 128

    xTb = sb("xTb", [128, KT * A0_S], BF16)
    attnT = sb("attnT", [128, A0_H * A0_S], BF16)
    wslab = [sb("wslab0", [128, A0_D], BF16)]
    xa = sb("xa", [128, A0_D], F32)
    qraw = sb("qraw", [128, A0_S], F32)
    rows = sb("rows", [1, A0_S], F32)
    b2hi = sb("b2hi", [1, A0_S], BF16)
    b2lo = sb("b2lo", [1, A0_S], BF16)
    qn = sb("qn", [128, A0_S], BF16)
    ktb = [sb(f"ktb{g}", [128, A0_S], BF16) for g in range(1)]
    vrow = sb("vrow", [128, A0_S], BF16)
    t1 = sb("t1", [128, 512], BF16)
    qnr = sb("qnr", [128, A0_S], BF16)
    probsT = sb("probsT", [128, A0_S], BF16)
    colv = sb("colv", [128, 8], F32)
    ysb = qraw
    st1 = sb("st1", [128, 8], F32)
    bohi = sb("bohi", [1, A0_D], BF16)
    bolo = sb("bolo", [1, A0_D], BF16)
    bqc = sb("bqc", [128, A0_H], F32)
    bkc = sb("bkc", [128, A0_G], F32)
    bvc = sb("bvc", [128, A0_G], F32)
    qnsc = sb("qnsc", [128, 1], F32)
    knsc = sb("knsc", [128, 1], F32)
    cost = sb("cost", [128, A0_S], BF16)
    sint = sb("sint", [128, A0_S], BF16)
    pswap = sb("pswap", [128, 128], BF16)
    cmsk = sb("cmsk", [128, 128], BF16)
    cmskT = sb("cmskT", [128, 4 * 512], BF16)
    idf = sb("idf", [128, 128], F32)
    onec = sb("onec", [128, 1], BF16)
    oner = sb("oner", [1, 128], F32)
    onerb = sb("onerb", [1, 128], BF16)

    pqkv = ps("pqkv", [128, 512], F32)
    pstat = ps("pstat", [128, 512], F32)
    pT = ps("pT", [128, 512], F32)
    pv = [ps(f"pv{i}", [128, 512], F32) for i in range(4)]
    psm = ps("psm", [128, 512], F32)

    for nm, src in [("bqc", h["bq_c"]), ("bkc", h["bk_c"]), ("bvc", h["bv_c"]),
                    ("qnsc", h["qns_c"]), ("knsc", h["kns_c"]),
                    ("cost", h["costab"]), ("sint", h["sintab"]), ("pswap", h["pswap"]),
                    ("cmsk", h["cmask"]), ("idf", h["identf"]),
                    ("onec", h["ones_col_b"]), ("oner", h["ones_row_f"]),
                    ("onerb", h["ones_row_b"])]:
        dst = locals()[nm]
        pr.dma("sp", dst[:, :], src[:, :], writes=(nm,))
    pr.dma("sp", cmskT[:, :], h["cmaskT"][:, :], writes=("cmskT",))

    pr.dma("sp", rows[0:1, :], h["bo_r"][0:1, :], writes=("rows",))
    pr.add("dve", lambda e: e.tensor_copy(bohi[0:1, :], rows[0:1, :]),
           reads=("rows",), writes=("bohi",))
    pr.add("dve", lambda e: e.tensor_sub(bolo[0:1, :], rows[0:1, :], bohi[0:1, :]),
           reads=("rows", "bohi"), writes=("bolo",))

    def transpose_to(dst_ap, src_sb_ap, src_res, dst_res, pbank, pres):
        pr.add("pe", lambda e: e.transpose(pbank[:, 0:128], src_sb_ap, idf[:, :]),
               reads=(src_res, "idf"), writes=(pres,))
        pr.add("act", lambda e: e.activation(out=dst_ap, in_=pbank[:, 0:128], func=AF.Copy),
               reads=(pres,), writes=(dst_res,))

    def project(dst_sb, dst_res, wt_ap, wt_res_idx, bias_col, bias_res, out_bf=False):
        pass

    for b in range(2):
        base = b * A0_S
        for tt in range(16):
            pr.dma("sp", xa[:, :], x0[base + tt * 128: base + (tt + 1) * 128, :], writes=("xa",))
            for kt in range(KT):
                transpose_to(xTb[:, kt * A0_S + tt * 128: kt * A0_S + (tt + 1) * 128],
                             xa[:, kt * 128:(kt + 1) * 128], "xa", "xTb", pstat, "pstat")

        for g in range(A0_G):
            gbuf = 0
            pr.dma("pool", wslab[gbuf][:, :], h["wk_t"][g], writes=(f"wslab{gbuf}",))
            _proj_norm_rope(pr, nc, locals(), gbuf, bkc[:, g:g + 1], knsc, ktb[0], "ktb0",
                            fold_hs=False, name=f"k{b}{g}")
            vbuf = 0
            pr.dma("pool", wslab[vbuf][:, :], h["wv_t"][g], writes=(f"wslab{vbuf}",))
            for ch in range(4):
                for kt in range(KT):
                    pr.add("pe", (lambda vbuf=vbuf, ch=ch, kt=kt:
                                  lambda e: e.matmul(pqkv[:, :], wslab[vbuf][:, kt * 128:(kt + 1) * 128],
                                                     xTb[:, kt * A0_S + ch * 512: kt * A0_S + (ch + 1) * 512],
                                                     start=(kt == 0), stop=(kt == KT - 1)))(),
                           reads=(f"wslab{vbuf}", "xTb"), writes=("pqkv",))
                pr.add("act", (lambda ch=ch, g=g:
                               lambda e: e.activation(out=qraw[:, ch * 512:(ch + 1) * 512], in_=pqkv[:, :],
                                                      func=AF.Identity, bias=bvc[:, g:g + 1]))(),
                       reads=("pqkv", "bvc"), writes=("qraw",))
            for st in range(16):
                transpose_to(vrow[:, st * 128:(st + 1) * 128], qraw[:, st * 128:(st + 1) * 128],
                             "qraw", "vrow", pstat, "pstat")

            for hh in range(4):
                head = g * 4 + hh
                qbuf = 0
                pr.dma("pool", wslab[qbuf][:, :], h["wq_t"][head], writes=(f"wslab{qbuf}",))
                _proj_norm_rope(pr, nc, locals(), qbuf, bqc[:, head:head + 1], qnsc, qn, "qn",
                                fold_hs=True, name=f"q{b}{head}")
                _attention(pr, nc, locals(), head)

        for kt in range(KT):
            pr.dma("pool", xTb[:, kt * A0_S:(kt + 1) * A0_S], h["wo_t"][kt], writes=("xTb",))
        for tt in range(16):
            pr.dma("sp", xa[:, :], x0[base + tt * 128: base + (tt + 1) * 128, :], writes=("xa",))
            for och in range(4):
                for kt in range(KT):
                    pr.add("pe", (lambda tt=tt, och=och, kt=kt:
                                  lambda e: e.matmul(pqkv[:, :],
                                                     attnT[:, kt * A0_S + tt * 128: kt * A0_S + (tt + 1) * 128],
                                                     xTb[:, kt * A0_S + och * 512: kt * A0_S + (och + 1) * 512],
                                                     start=(kt == 0), stop=False))(),
                           reads=("attnT", "xTb"), writes=("pqkv",))
                pr.add("pe", (lambda och=och:
                              lambda e: e.matmul(pqkv[:, :], onerb[0:1, :],
                                                 bohi[0:1, och * 512:(och + 1) * 512],
                                                 start=False, stop=False))(),
                       reads=("onerb", "bohi"), writes=("pqkv",))
                pr.add("pe", (lambda och=och:
                              lambda e: e.matmul(pqkv[:, :], onerb[0:1, :],
                                                 bolo[0:1, och * 512:(och + 1) * 512],
                                                 start=False, stop=True))(),
                       reads=("onerb", "bolo"), writes=("pqkv",))
                pr.add("act", (lambda och=och:
                               lambda e: e.activation(out=ysb[:, och * 512:(och + 1) * 512],
                                                      in_=pqkv[:, :], func=AF.Copy))(),
                       reads=("pqkv",), writes=("qraw",))
            pr.add("dve", lambda e: e.tensor_add(xa[:, :], ysb[:, :], xa[:, :]),
                   reads=("qraw", "xa"), writes=("xa",))
            emit_ln2x(pr, nc, xa, "xa", probsT, st1, ysb, "qraw", 1e-5, A0_D, sq_res="probsT")
            pr.dma("pool", x1[base + tt * 128: base + (tt + 1) * 128, :], ysb[:, :], reads=("qraw",))
    return


def _proj_norm_rope(pr, nc, L, wbuf, bias_ap, ns_col, dst, dst_res, fold_hs, name):
    pqkv, psm, wslab, xTb, qraw, rows, onec, oner, idf = (
        L["pqkv"], L["psm"], L["wslab"], L["xTb"], L["qraw"], L["rows"],
        L["onec"], L["oner"], L["idf"])
    cost, sint, t1 = L["cost"], L["sint"], L["t1"]
    qnr = L["qnr"]
    KT = 16
    S_ = A0_S
    nsres = "qnsc" if ns_col is L["qnsc"] else "knsc"
    bres = "bqc" if fold_hs else "bkc"
    for ch in range(4):
        for kt in range(KT):
            pr.add("pe", (lambda wbuf=wbuf, ch=ch, kt=kt:
                          lambda e: e.matmul(pqkv[:, :], wslab[wbuf][:, kt * 128:(kt + 1) * 128],
                                             xTb[:, kt * S_ + ch * 512: kt * S_ + (ch + 1) * 512],
                                             start=(kt == 0), stop=(kt == KT - 1)))(),
                   reads=(f"wslab{wbuf}", "xTb"), writes=("pqkv",))
        pr.add("act", (lambda ch=ch:
                       lambda e: e.activation(out=qraw[:, ch * 512:(ch + 1) * 512], in_=pqkv[:, :],
                                              func=AF.Identity, bias=bias_ap))(),
               reads=("pqkv", bres), writes=("qraw",))
    sqt = L["probsT"]
    pr.add("act", lambda e: e.activation(out=sqt[:, :], in_=qraw[:, :], func=AF.Square),
           reads=("qraw",), writes=("probsT",))
    for ch in range(4):
        pr.add("pe", (lambda ch=ch: lambda e: e.matmul(psm[0:1, :], onec[:, 0:1],
                                                       sqt[:, ch * 512:(ch + 1) * 512],
                                                       start=True, stop=True))(),
               reads=("onec", "probsT"), writes=("psm",))
        pr.add("act", (lambda ch=ch: lambda e: e.activation(out=rows[0:1, ch * 512:(ch + 1) * 512],
                                                            in_=psm[0:1, :], func=AF.Copy))(),
               reads=("psm",), writes=("rows",))
    pr.add("dve", lambda e: e.tensor_scalar(rows[0:1, 0:S_], rows[0:1, 0:S_],
                                            1.0 / 128, float(A0_EPSR), ALU.mult, ALU.add),
           reads=("rows",), writes=("rows",))
    pr.add("act", lambda e: e.activation(out=rows[0:1, 0:S_], in_=rows[0:1, 0:S_], func=AF.Ln),
           reads=("rows",), writes=("rows",))
    pr.add("dve", lambda e: e.tensor_scalar_mul(rows[0:1, 0:S_], rows[0:1, 0:S_], -0.5),
           reads=("rows",), writes=("rows",))
    pr.add("act", lambda e: e.activation(out=rows[0:1, 0:S_], in_=rows[0:1, 0:S_], func=AF.Exp),
           reads=("rows",), writes=("rows",))
    if fold_hs:
        pr.add("dve", lambda e: e.tensor_scalar_mul(rows[0:1, 0:S_], rows[0:1, 0:S_], 1.0 / 128),
               reads=("rows",), writes=("rows",))
    for ch in range(4):
        pr.add("pe", (lambda ch=ch: lambda e: e.matmul(psm[:, :], oner[0:1, :],
                                                       rows[0:1, ch * 512:(ch + 1) * 512],
                                                       start=True, stop=True))(),
               reads=("oner", "rows"), writes=("psm",))
        pr.add("dve", (lambda ch=ch: lambda e: e.scalar_tensor_tensor(
            qnr[:, ch * 512:(ch + 1) * 512], qraw[:, ch * 512:(ch + 1) * 512], ns_col[:, 0:1],
            psm[:, :], ALU.mult, ALU.mult))(),
               reads=("qraw", nsres, "psm"), writes=("qnr",))
    pswap_sb, qnr_l = L["pswap"], L["qnr"]
    for ch in range(4):
        sl = slice(ch * 512, (ch + 1) * 512)
        pr.add("pe", (lambda sl=sl: lambda e: e.matmul(psm[:, :], pswap_sb[:, :], qnr_l[:, sl],
                                                       start=True, stop=True))(),
               reads=("pswap", "qnr"), writes=("psm",))
        pr.add("dve", (lambda sl=sl: lambda e: e.tensor_mul(t1[:, :], psm[:, :], sint[:, sl]))(),
               reads=("psm", "sint"), writes=("t1",))
        pr.add("dve", (lambda sl=sl: lambda e: e.tensor_mul(dst[:, sl], qnr_l[:, sl], cost[:, sl]))(),
               reads=("qnr", "cost"), writes=(dst_res,))
        pr.add("dve", (lambda sl=sl: lambda e: e.tensor_add(dst[:, sl], dst[:, sl], t1[:, :]))(),
               reads=(dst_res, "t1"), writes=(dst_res,))


def _attention(pr, nc, L, head):
    (qn, ktb, vrow, lsb, probsT, expsc, colv, rows, b2hi, b2lo, cmsk, cmskT,
     pstat, pT, pv, psm, attnT, idf, onerb) = (
        L["qn"], L["ktb"][0], L["vrow"], L["qraw"], L["probsT"], L["probsT"],
        L["colv"], L["rows"], L["b2hi"], L["b2lo"], L["cmsk"], L["cmskT"],
        L["pstat"], L["pT"], L["pv"], L["psm"], L["attnT"], L["idf"], L["onerb"])
    S_ = A0_S
    for qt in range(16):
        cols = (qt + 1) * 128
        nch = (cols + 511) // 512
        for ch in range(nch):
            n = min(512, cols - ch * 512)
            pr.add("pe", (lambda qt=qt, ch=ch, n=n:
                          lambda e: e.matmul(pstat[:, 0:n], qn[:, qt * 128:(qt + 1) * 128],
                                             ktb[:, ch * 512: ch * 512 + n], start=True, stop=True))(),
                   reads=("qn", "ktb0"), writes=("pstat",))
            pr.add("act", (lambda qt=qt, ch=ch, n=n:
                           lambda e: e.activation(out=lsb[:, ch * 512: ch * 512 + n],
                                                  in_=pstat[:, 0:n], func=AF.Copy))(),
                   reads=("pstat",), writes=("qraw",))
        pr.add("dve", (lambda qt=qt: lambda e: e.tensor_add(lsb[:, qt * 128:(qt + 1) * 128],
                                                            lsb[:, qt * 128:(qt + 1) * 128], cmsk[:, :]))(),
               reads=("qraw", "cmsk"), writes=("qraw",))
        pr.add("dve", (lambda cols=cols: lambda e: e.reduce_max(colv[:, 0:1], lsb[:, 0:cols],
                                                                axis=AX.X, negate=True))(),
               reads=("qraw",), writes=("colv",))
        pr.add("act", (lambda cols=cols: lambda e: e.activation(out=expsc[:, 0:cols], in_=lsb[:, 0:cols],
                                                                func=AF.Exp, bias=colv[:, 0:1],
                                                                accum_out=colv[:, 1:2]))(),
               reads=("qraw", "colv"), writes=("probsT", "colv"))
        pr.add("act", lambda e: e.activation(out=colv[:, 2:3], in_=colv[:, 1:2], func=AF.Ln),
               reads=("colv",), writes=("colv",))
        pr.add("dve", lambda e: e.tensor_sub(colv[:, 3:4], colv[:, 0:1], colv[:, 2:3]),
               reads=("colv",), writes=("colv",))
        pr.add("pe", lambda e: e.transpose(pstat[0:1, 0:128], colv[:, 3:4], idf[:, :]),
               reads=("colv", "idf"), writes=("pstat",))
        pr.add("act", (lambda qt=qt: lambda e: e.activation(out=b2hi[0:1, qt * 128:(qt + 1) * 128],
                                                            in_=pstat[0:1, 0:128], func=AF.Copy))(),
               reads=("pstat",), writes=("b2hi",))
        pr.add("dve", (lambda qt=qt: lambda e: e.tensor_sub(b2lo[0:1, qt * 128:(qt + 1) * 128],
                                                            pstat[0:1, 0:128],
                                                            b2hi[0:1, qt * 128:(qt + 1) * 128]))(),
               reads=("pstat", "b2hi"), writes=("b2lo",))
    for st in range(16):
        qch0 = st // 4
        for qch in range(qch0, 4):
            pr.add("pe", (lambda st=st, qch=qch:
                          lambda e: e.matmul(pT[:, :], ktb[:, st * 128:(st + 1) * 128],
                                             qn[:, qch * 512:(qch + 1) * 512], start=True, stop=False))(),
                   reads=("ktb0", "qn"), writes=("pT",))
            pr.add("pe", (lambda qch=qch:
                          lambda e: e.matmul(pT[:, :], onerb[0:1, :], b2hi[0:1, qch * 512:(qch + 1) * 512],
                                             start=False, stop=False))(),
                   reads=("onerb", "b2hi"), writes=("pT",))
            pr.add("pe", (lambda qch=qch:
                          lambda e: e.matmul(pT[:, :], onerb[0:1, :], b2lo[0:1, qch * 512:(qch + 1) * 512],
                                             start=False, stop=True))(),
                   reads=("onerb", "b2lo"), writes=("pT",))
            if qch == qch0:
                k = st - 4 * qch
                pr.add("dve", (lambda k=k: lambda e: e.tensor_add(pT[:, :], pT[:, :],
                                                                  cmskT[:, k * 512:(k + 1) * 512]))(),
                       reads=("pT", "cmskT"), writes=("pT",))
            pr.add("act", (lambda st=st, qch=qch:
                           lambda e: e.activation(out=probsT[:, qch * 512:(qch + 1) * 512],
                                                  in_=pT[:, :], func=AF.Exp))(),
                   reads=("pT",), writes=("probsT",))
        for qch in range(qch0, 4):
            pr.add("pe", (lambda st=st, qch=qch:
                          lambda e: e.matmul(pv[qch][:, :], vrow[:, st * 128:(st + 1) * 128],
                                             probsT[:, qch * 512:(qch + 1) * 512],
                                             start=(st == 0), stop=(st == 4 * qch + 3)))(),
                   reads=("vrow", "probsT"), writes=(f"pv{qch}",))
    for qch in range(4):
        pr.add("act", (lambda qch=qch, head=head:
                       lambda e: e.activation(out=attnT[:, head * S_ + qch * 512: head * S_ + (qch + 1) * 512],
                                              in_=pv[qch][:, :], func=AF.Copy))(),
               reads=(f"pv{qch}",), writes=("attnT",))




D1_D = 2048
D1_H = 8
D1_DH = 128
D1_CT = 64
D1_C = 128
D1_NCHK = 32
D1_EPS_O = 1e-5


def build_l1attn(pr, nc, h, name="d1"):
    es = h["es"]
    sb = lambda nm, shp, dt: es.enter_context(nc.sbuf_tensor(f"{name}_{nm}", shp, dt))
    ps = lambda nm, shp, dt: es.enter_context(nc.psum_tensor(f"{name}_{nm}", shp, dt))
    S = 2048
    KT = 16

    x2, x3 = h["x2"], h["x3"]

    xTb = sb("xTb", [128, KT * S], BF16)
    wsl = sb("wsl", [128, D1_D], BF16)
    qraw = sb("qraw", [128, S], F32)
    cacc = sb("cacc", [128, S], F32)
    rowm = sb("rowm", [1, S], F32)
    qtf = sb("qtf", [128, S], BF16)
    ktf = sb("ktf", [128, 2 * S], BF16)
    vtf = sb("vtf", [128, 2 * S], BF16)
    wbs = sb("wbs", [128, 16 * 16], BF16)
    was = sb("was", [128, 16 * 8], BF16)
    btT = sb("btT", [16, S], F32)
    lbT = sb("lbT", [16, S], F32)
    DgT = sb("DgT", [8, S], F32)
    alog = sb("alog", [8, 1], F32)
    dtb = sb("dtb", [8, 1], F32)
    eA = sb("eA", [8, 1], F32)
    drow = sb("drow", [1, 2 * S], F32)
    brow = sb("brow", [1, 2 * S], BF16)
    lbrow = sb("lbrow", [1, 2 * S], BF16)
    dqrow = sb("dqrow", [1, D1_CT], F32)
    nd = sb("nd", [1, D1_C], F32)
    dlb = sb("dlb", [1, D1_C], F32)
    tiny = sb("tiny", [1, 8], F32)
    AAn = sb("AAn", [128, D1_C], F32)
    Dm2 = sb("Dm2", [128, D1_C], F32)
    XT_sb = sb("XT_sb", [128, D1_C], BF16)
    X_sb = sb("X_sb", [128, D1_C], BF16)
    X2a = sb("X2a", [128, D1_C], BF16)
    X2b = sb("X2b", [128, D1_C], BF16)
    Kc = sb("Kc", [128, D1_DH], BF16)
    khat = sb("khat", [128, D1_DH], BF16)
    rhs = sb("rhs", [128, 256], BF16)
    uf = sb("uf", [128, 256], F32)
    ub = sb("ub", [128, 256], BF16)
    bcol = sb("bcol", [128, 4], F32)
    btcol = sb("btcol", [128, 1], F32)
    GTs = sb("GTs", [128, D1_CT], F32)
    R2s = sb("R2s", [128, D1_CT], F32)
    WTs = sb("WTs", [128, D1_CT], BF16)
    ZTs = sb("ZTs", [128, D1_CT], BF16)
    tzs = sb("tzs", [128, D1_CT], BF16)
    bqr = sb("bqr", [1, D1_CT], F32)
    Msb = sb("Msb", [128, D1_DH], BF16)
    Psb = sb("Psb", [128, D1_DH], F32)
    Sf = sb("Sf", [128, D1_DH], F32)
    Sbf = sb("Sbf", [128, D1_DH], BF16)
    Icon = sb("Icon", [128, 128], BF16)
    otile = sb("otile", [128, D1_H * D1_DH], BF16)
    gtile = sb("gtile", [128, D1_H * D1_DH], BF16)
    oT_t = sb("oT_t", [128, D1_H * D1_DH], BF16)
    xr = sb("xr", [128, D1_D], F32)
    ysb = None
    sqb = sb("sqb", [128, D1_D], BF16)
    st1 = sb("st1", [128, 8], F32)
    rq = sb("rq", [128, 8], F32)
    nwc = sb("nwc", [128, 1], F32)
    onefb = sb("onefb", [1, 1], BF16)
    idf = sb("idf", [128, 128], F32)
    oner = sb("oner", [1, 128], F32)
    onef = sb("onef", [1, 1], F32)
    mx = sb("mx", [128, 128], F32)
    mq = sb("mq", [128, D1_CT], F32)
    cqc = sb("cqc", [128, 8 * 4], F32)
    ckc = sb("ckc", [128, 16 * 4], F32)
    cvc = sb("cvc", [128, 16 * 4], F32)

    pA = ps("pA", [128, 512], F32)
    pB = ps("pB", [128, 512], F32)
    pU = ps("pU", [128, 512], F32)
    pS = ps("pS", [128, 512], F32)
    pW = ps("pW", [128, 512], F32)
    pX2 = ps("pX2", [128, 256], F32)
    pXb = ps("pXb", [128, 128], BF16)

    for nm, src in [("idf", h["identf"]), ("oner", h["ones_row_f"]), ("onef", h["one_f"]),
                    ("mx", h["mx_c"]), ("mq", h["mq_c"]), ("alog", h["alog_c"]), ("dtb", h["dtb_c"]),
                    ("cqc", h["cq_c"]), ("ckc", h["ck_c"]), ("cvc", h["cv_c"]),
                    ("nwc", h["nw_c"]), ("Icon", h["ident_bf"]), ("onefb", h["one_b"])]:
        dst = locals()[nm]
        pr.dma("sp", dst[:, :], src[:, :], writes=(nm,))
    pr.dma("pool", wbs[:, :], h["wb_c"][:, :], writes=("wbs",))
    pr.dma("pool", was[:, :], h["wa_c"][:, :], writes=("was",))

    def proj_slab(wslab_src, wres, out_ps=True):
        pr.dma("pool", wsl[:, :], wslab_src, writes=("wsl",))
        for ch in range(4):
            for kt in range(KT):
                pr.add("pe", (lambda ch=ch, kt=kt: lambda e: e.matmul(
                    pA[:, :], wsl[:, kt * 128:(kt + 1) * 128],
                    xTb[:, kt * S + ch * 512: kt * S + (ch + 1) * 512],
                    start=(kt == 0), stop=(kt == KT - 1)))(),
                       reads=("wsl", "xTb"), writes=("pA",))
            pr.add("act", (lambda ch=ch: lambda e: e.activation(
                out=qraw[:, ch * 512:(ch + 1) * 512], in_=pA[:, :], func=AF.Copy))(),
                   reads=("pA",), writes=("qraw",))

    def conv_silu(cw, cw_res, idx):
        pr.add("dve", (lambda: lambda e: e.tensor_scalar_mul(
            cacc[:, :], qraw[:, :], cw[:, idx * 4 + 3: idx * 4 + 4]))(),
               reads=("qraw", cw_res), writes=("cacc",))
        for i in range(3):
            sh = 3 - i
            pr.add("dve", (lambda i=i, sh=sh: lambda e: e.scalar_tensor_tensor(
                cacc[:, sh:S], qraw[:, 0:S - sh], cw[:, idx * 4 + i: idx * 4 + i + 1],
                cacc[:, sh:S], ALU.mult, ALU.add))(),
                   reads=("qraw", cw_res, "cacc"), writes=("cacc",))
        pr.add("act", lambda e: e.activation(out=cacc[:, :], in_=cacc[:, :], func=AF.Silu),
               reads=("cacc",), writes=("cacc",))

    def l2norm_to(dst_ap, dst_res, scale):
        pr.add("act", lambda e: e.activation(out=sqb_[:, :], in_=cacc[:, :], func=AF.Square),
               reads=("cacc",), writes=("sqb",))
        for ch in range(4):
            pr.add("pe", (lambda ch=ch: lambda e: e.matmul(
                pW[0:1, :], onecb[:, 0:1], sqb_[:, ch * 512:(ch + 1) * 512],
                start=True, stop=True))(),
                   reads=("onecb", "sqb"), writes=("pW",))
            pr.add("act", (lambda ch=ch: lambda e: e.activation(
                out=rowm[0:1, ch * 512:(ch + 1) * 512], in_=pW[0:1, :], func=AF.Copy))(),
                   reads=("pW",), writes=("rowm",))
        pr.add("act", lambda e: e.activation(out=rowm[0:1, :], in_=rowm[0:1, :], func=AF.Sqrt),
               reads=("rowm",), writes=("rowm",))
        pr.add("dve", lambda e: e.tensor_scalar_max(rowm[0:1, :], rowm[0:1, :], 1e-12),
               reads=("rowm",), writes=("rowm",))
        pr.add("dve", lambda e: e.reciprocal(rowm[0:1, :], rowm[0:1, :]),
               reads=("rowm",), writes=("rowm",))
        if scale != 1.0:
            pr.add("dve", (lambda: lambda e: e.tensor_scalar_mul(rowm[0:1, :], rowm[0:1, :], scale))(),
                   reads=("rowm",), writes=("rowm",))
        for ch in range(4):
            pr.add("pe", (lambda ch=ch: lambda e: e.matmul(
                pW[:, :], oner[0:1, :], rowm[0:1, ch * 512:(ch + 1) * 512],
                start=True, stop=True))(),
                   reads=("oner", "rowm"), writes=("pW",))
            pr.add("dve", (lambda ch=ch, dst_ap=dst_ap: lambda e: e.tensor_mul(
                dst_ap[:, ch * 512:(ch + 1) * 512], cacc[:, ch * 512:(ch + 1) * 512], pW[:, :]))(),
                   reads=("cacc", "pW"), writes=(dst_res,))

    sqb_ = sqb
    onecb = sb("onecb", [128, 1], BF16)
    pr.dma("sp", onecb[:, :], h["ones_col_b"][:, :], writes=("onecb",))
    ktmp = sb("ktmp", [128, S], BF16)
    zrow = cacc[0:8, :]
    ysb = qraw
    dtmp = rowm
    wosl = xTb

    for b in range(2):
        base = b * S
        for tt in range(16):
            pr.dma("sp", xr[:, :], x2[base + tt * 128: base + (tt + 1) * 128, :], writes=("xr",))
            for kt in range(KT):
                pr.add("pe", (lambda kt=kt: lambda e: e.transpose(
                    pA[:, 0:128], xr[:, kt * 128:(kt + 1) * 128], idf[:, :]))(),
                       reads=("xr", "idf"), writes=("pA",))
                pr.add("act", (lambda kt=kt, tt=tt: lambda e: e.activation(
                    out=xTb[:, kt * S + tt * 128: kt * S + (tt + 1) * 128],
                    in_=pA[:, 0:128], func=AF.Copy))(),
                       reads=("pA",), writes=("xTb",))
        for sig, wofs, M in [("bt", 0, 16), ("ga", 0, 8)]:
            pass
        for ch in range(4):
            for kt in range(KT):
                pr.add("pe", (lambda ch=ch, kt=kt: lambda e: e.matmul(
                    pB[0:16, :], wbs[:, kt * 16:(kt + 1) * 16],
                    xTb[:, kt * S + ch * 512: kt * S + (ch + 1) * 512],
                    start=(kt == 0), stop=(kt == KT - 1)))(),
                       reads=("wbs", "xTb"), writes=("pB",))
            pr.add("act", (lambda ch=ch: lambda e: e.activation(
                out=btT[:, ch * 512:(ch + 1) * 512], in_=pB[0:16, :], func=AF.Sigmoid))(),
                   reads=("pB",), writes=("btT",))
        pr.add("dve", lambda e: e.tensor_scalar_mul(btT[:, :], btT[:, :], 2.0),
               reads=("btT",), writes=("btT",))
        pr.add("act", lambda e: e.activation(out=lbT[:, :], in_=btT[:, :], func=AF.Ln),
               reads=("btT",), writes=("lbT",))
        for ch in range(4):
            for kt in range(KT):
                pr.add("pe", (lambda ch=ch, kt=kt: lambda e: e.matmul(
                    pB[0:8, :], was[:, kt * 8:(kt + 1) * 8],
                    xTb[:, kt * S + ch * 512: kt * S + (ch + 1) * 512],
                    start=(kt == 0), stop=(kt == KT - 1)))(),
                       reads=("was", "xTb"), writes=("pB",))
            pr.add("act", (lambda ch=ch: lambda e: e.activation(
                out=DgT[:, ch * 512:(ch + 1) * 512], in_=pB[0:8, :], func=AF.Identity,
                bias=dtb[:, 0:1]))(),
                   reads=("pB", "dtb"), writes=("DgT",))
        pr.add("dve", lambda e: e.tensor_scalar_mul(zrow[:, :], DgT[:, :], -1.0),
               reads=("DgT",), writes=("cacc",))
        pr.add("dve", lambda e: e.tensor_tensor(zrow[:, :], zrow[:, :], DgT[:, :], ALU.min),
               reads=("cacc", "DgT"), writes=("cacc",))
        pr.add("act", lambda e: e.activation(out=zrow[:, :], in_=zrow[:, :], func=AF.Exp),
               reads=("cacc",), writes=("cacc",))
        pr.add("dve", lambda e: e.tensor_scalar_add(zrow[:, :], zrow[:, :], 1.0),
               reads=("cacc",), writes=("cacc",))
        pr.add("act", lambda e: e.activation(out=zrow[:, :], in_=zrow[:, :], func=AF.Ln),
               reads=("cacc",), writes=("cacc",))
        pr.add("dve", lambda e: e.tensor_scalar_max(DgT[:, :], DgT[:, :], 0.0),
               reads=("DgT",), writes=("DgT",))
        pr.add("dve", lambda e: e.tensor_add(DgT[:, :], DgT[:, :], zrow[:, :]),
               reads=("DgT", "cacc"), writes=("DgT",))
        pr.add("act", lambda e: e.activation(out=eA[:, 0:1], in_=alog[:, 0:1], func=AF.Exp),
               reads=("alog",), writes=("eA",))
        pr.add("dve", lambda e: e.tensor_scalar(DgT[:, :], DgT[:, :], eA[:, 0:1], -1.0,
                                                ALU.mult, ALU.mult),
               reads=("DgT", "eA"), writes=("DgT",))
        pr.add("dve", lambda e: e.memset(zrow[:, :], 0.0), reads=(), writes=("cacc",))
        pr.add("dve", lambda e: e.tensor_tensor_scan(DgT[:, :], DgT[:, :], zrow[:, :], 0.0,
                                                     ALU.add, ALU.add),
               reads=("DgT", "cacc"), writes=("DgT",))

        for hd in range(D1_H):
            proj_slab(h["wq_t"][hd], "wq")
            conv_silu(cqc, "cqc", hd)
            l2norm_to(qtf, "qtf", D1_DH ** -0.5)
            for jj in range(2):
                proj_slab(h["wk_t"][jj * 8 + hd], "wk")
                conv_silu(ckc, "ckc", jj * 8 + hd)
                l2norm_to(ktmp, "ktmp", 1.0)
                pr.add("dve", (lambda jj=jj: lambda e: e.tensor_copy(
                    ktf[:, jj:2 * S:2], ktmp[:, :]))(),
                       reads=("ktmp",), writes=("ktf",))
            for jj in range(2):
                proj_slab(h["wv_t"][jj * 8 + hd], "wv")
                conv_silu(cvc, "cvc", jj * 8 + hd)
                pr.add("dve", (lambda jj=jj: lambda e: e.tensor_copy(
                    vtf[:, jj:2 * S:2], cacc[:, :]))(),
                       reads=("cacc",), writes=("vtf",))
            proj_slab(h["wg_t"][hd], "wg")
            pr.add("act", lambda e: e.activation(out=cacc[:, :], in_=qraw[:, :], func=AF.Silu),
                   reads=("qraw",), writes=("cacc",))
            for tt in range(16):
                pr.add("pe", (lambda tt=tt: lambda e: e.transpose(
                    pA[:, 0:128], cacc[:, tt * 128:(tt + 1) * 128], idf[:, :]))(),
                       reads=("cacc", "idf"), writes=("pA",))
                pr.add("act", (lambda tt=tt: lambda e: e.activation(
                    out=gtile[:, 0:128], in_=pA[:, 0:128], func=AF.Copy))(),
                       reads=("pA",), writes=("gtile",))
                pr.dma("pool", h["g_s"][b, tt * 128:(tt + 1) * 128, hd * 128:(hd + 1) * 128],
                       gtile[:, 0:128], reads=("gtile",), writes=(f"gs{b}",))
            pr.dma("pool", dtmp[0:1, :], DgT[hd:hd + 1, :], reads=("DgT",), writes=("rowm",))
            pr.add("dve", lambda e: e.tensor_copy(drow[0:1, 0:2 * S:2], dtmp[0:1, :]),
                   reads=("rowm",), writes=("drow",))
            pr.add("dve", lambda e: e.tensor_copy(drow[0:1, 1:2 * S:2], dtmp[0:1, :]),
                   reads=("rowm",), writes=("drow",))
            for jj in range(2):
                pr.dma("pool", dtmp[0:1, :], btT[jj * 8 + hd: jj * 8 + hd + 1, :],
                       reads=("btT",), writes=("rowm",))
                pr.add("dve", (lambda jj=jj: lambda e: e.tensor_copy(
                    brow[0:1, jj:2 * S:2], dtmp[0:1, :]))(),
                       reads=("rowm",), writes=("brow",))
                pr.dma("pool", dtmp[0:1, :], lbT[jj * 8 + hd: jj * 8 + hd + 1, :],
                       reads=("lbT",), writes=("rowm",))
                pr.add("dve", (lambda jj=jj: lambda e: e.tensor_copy(
                    lbrow[0:1, jj:2 * S:2], dtmp[0:1, :]))(),
                       reads=("rowm",), writes=("lbrow",))
            pr.add("dve", lambda e: e.memset(Sf[:, :], 0.0), reads=(), writes=("Sf",))
            pr.add("dve", lambda e: e.memset(Sbf[:, :], 0.0), reads=(), writes=("Sbf",))
            for c in range(D1_NCHK):
                _chunk(pr, nc, locals(), b, hd, c)
        _endphase(pr, nc, locals(), b)
    return


def _chunk(pr, nc, L, b, hd, c):
    (ktf, vtf, qtf, drow, brow, lbrow, nd, dlb, tiny, AAn, Dm2, XT_sb, X_sb, X2a, X2b,
     Kc, khat, rhs, uf, ub, bcol, GTs, R2s, WTs, ZTs, tzs, bqr, Msb, Psb, Sf, Sbf,
     dqrow, mx, mq, oner, onef, Icon, idf, otile) = (
        L["ktf"], L["vtf"], L["qtf"], L["drow"], L["brow"], L["lbrow"], L["nd"], L["dlb"],
        L["tiny"], L["AAn"], L["Dm2"], L["XT_sb"], L["X_sb"], L["X2a"], L["X2b"],
        L["Kc"], L["khat"], L["rhs"], L["uf"], L["ub"], L["bcol"], L["GTs"], L["R2s"],
        L["WTs"], L["ZTs"], L["tzs"], L["bqr"], L["Msb"], L["Psb"], L["Sf"], L["Sbf"],
        L["dqrow"], L["mx"], L["mq"], L["oner"], L["onef"], L["Icon"], L["idf"], L["otile"])
    pA, pB, pU, pS, pW, pX2, pXb = (L["pA"], L["pB"], L["pU"], L["pS"], L["pW"],
                                    L["pX2"], L["pXb"])
    h = L["h"]
    c0 = c * D1_C
    KTc = ktf[:, c0:c0 + D1_C]
    VTc = vtf[:, c0:c0 + D1_C]
    QTc = qtf[:, c * D1_CT:(c + 1) * D1_CT]
    pr.add("act", (lambda: lambda e: e.activation(out=nd[0:1, :], in_=drow[0:1, c0:c0 + D1_C],
                                                  func=AF.Copy, scale=-1))(),
           reads=("drow",), writes=("nd",))
    pr.add("dve", (lambda: lambda e: e.tensor_add(dlb[0:1, :], drow[0:1, c0:c0 + D1_C],
                                                  lbrow[0:1, c0:c0 + D1_C]))(),
           reads=("drow", "lbrow"), writes=("dlb",))
    pr.add("dve", (lambda: lambda e: e.tensor_copy(dqrow[0:1, :], drow[0:1, c0 + 1:c0 + D1_C:2]))(),
           reads=("drow",), writes=("dqrow",))
    if c > 0:
        pr.add("act", (lambda: lambda e: e.activation(out=tiny[0:1, 0:1],
                                                      in_=drow[0:1, c0 - 1:c0],
                                                      func=AF.Copy, scale=-1))(),
               reads=("drow",), writes=("tiny",))
    pr.add("pe", lambda e: e.matmul(pA[:, 0:D1_C], KTc, KTc, start=True, stop=True),
           reads=("ktf",), writes=("pA",))
    pr.add("act", lambda e: e.activation(out=AAn[:, :], in_=pA[:, 0:D1_C], func=AF.Copy, scale=-1),
           reads=("pA",), writes=("AAn",))
    pr.add("pe", lambda e: e.matmul(pB[:, 0:D1_C], nd[0:1, :], oner[0:1, :], start=True, stop=False),
           reads=("nd", "oner"), writes=("pB",))
    pr.add("pe", lambda e: e.matmul(pB[:, 0:D1_C], oner[0:1, :], dlb[0:1, :], start=False, stop=True),
           reads=("oner", "dlb"), writes=("pB",))
    pr.add("dve", lambda e: e.tensor_add(Dm2[:, :], pB[:, 0:D1_C], mx[:, :]),
           reads=("pB", "mx"), writes=("Dm2",))
    pr.add("act", lambda e: e.activation(out=Dm2[:, :], in_=Dm2[:, :], func=AF.Exp),
           reads=("Dm2",), writes=("Dm2",))
    pr.add("dve", lambda e: e.tensor_mul(XT_sb[:, :], Dm2[:, :], AAn[:, :]),
           reads=("Dm2", "AAn"), writes=("XT_sb",))
    pr.add("pe", lambda e: e.transpose(pXb[:, :], XT_sb[:, :], Icon[:, :]),
           reads=("XT_sb", "Icon"), writes=("pXb",))
    pr.add("act", lambda e: e.activation(out=X_sb[:, :], in_=pXb[:, :], func=AF.Copy),
           reads=("pXb",), writes=("X_sb",))
    pr.add("pe", (lambda: lambda e: e.matmul(pW[:, 0:1], drow[0:1, c0:c0 + D1_C], onef[0:1, 0:1],
                                             start=True, stop=(c == 0)))(),
           reads=("drow", "onef"), writes=("pW",))
    if c > 0:
        pr.add("pe", lambda e: e.matmul(pW[:, 0:1], oner[0:1, :], tiny[0:1, 0:1],
                                        start=False, stop=True),
               reads=("oner", "tiny"), writes=("pW",))
    pr.add("act", lambda e: e.activation(out=bcol[:, 0:1], in_=pW[:, 0:1], func=AF.Exp),
           reads=("pW",), writes=("bcol",))
    pr.add("pe", lambda e: e.matmul(pW[:, 1:2], nd[0:1, :], onef[0:1, 0:1], start=True, stop=False),
           reads=("nd", "onef"), writes=("pW",))
    pr.add("pe", (lambda: lambda e: e.matmul(pW[:, 1:2], oner[0:1, :],
                                             drow[0:1, c0 + D1_C - 1:c0 + D1_C], start=False, stop=True))(),
           reads=("oner", "drow"), writes=("pW",))
    pr.add("act", lambda e: e.activation(out=bcol[:, 1:2], in_=pW[:, 1:2], func=AF.Exp),
           reads=("pW",), writes=("bcol",))
    onefb_l, btcol_l = L["onefb"], L["btcol"]
    pr.add("pe", (lambda: lambda e: e.matmul(pW[:, 2:3], brow[0:1, c0:c0 + D1_C],
                                             onefb_l[0:1, 0:1], start=True, stop=True))(),
           reads=("brow", "onefb"), writes=("pW",))
    pr.add("act", lambda e: e.activation(out=btcol_l[:, 0:1], in_=pW[:, 2:3], func=AF.Copy),
           reads=("pW",), writes=("btcol",))
    pr.add("pe", (lambda: lambda e: e.matmul(pW[:, 3:4], oner[0:1, :],
                                             drow[0:1, c0 + D1_C - 1:c0 + D1_C],
                                             start=True, stop=(c == 0)))(),
           reads=("oner", "drow"), writes=("pW",))
    if c > 0:
        pr.add("pe", lambda e: e.matmul(pW[:, 3:4], oner[0:1, :], tiny[0:1, 0:1],
                                        start=False, stop=True),
               reads=("oner", "tiny"), writes=("pW",))
    pr.add("act", lambda e: e.activation(out=bcol[:, 2:3], in_=pW[:, 3:4], func=AF.Exp),
           reads=("pW",), writes=("bcol",))
    pr.add("pe", lambda e: e.transpose(pXb[:, :], KTc, Icon[:, :]),
           reads=("ktf", "Icon"), writes=("pXb",))
    pr.add("act", lambda e: e.activation(out=Kc[:, :], in_=pXb[:, :], func=AF.Copy),
           reads=("pXb",), writes=("Kc",))
    pr.add("pe", lambda e: e.transpose(pXb[:, :], VTc, Icon[:, :]),
           reads=("vtf", "Icon"), writes=("pXb",))
    pr.add("dve", lambda e: e.tensor_scalar_mul(rhs[:, 0:128], pXb[:, :], btcol_l[:, 0:1]),
           reads=("pXb", "btcol"), writes=("rhs",))
    pr.add("dve", lambda e: e.tensor_mul(bcol[:, 3:4], btcol_l[:, 0:1], bcol[:, 0:1]),
           reads=("btcol", "bcol"), writes=("bcol",))
    pr.add("dve", lambda e: e.tensor_scalar_mul(rhs[:, 128:256], Kc[:, :], bcol[:, 3:4]),
           reads=("Kc", "bcol"), writes=("rhs",))
    pr.add("dve", lambda e: e.tensor_scalar_mul(khat[:, :], Kc[:, :], bcol[:, 1:2]),
           reads=("Kc", "bcol"), writes=("khat",))
    pr.add("dve", lambda e: e.tensor_copy(uf[:, :], rhs[:, :]),
           reads=("rhs",), writes=("uf",))
    pr.add("dve", lambda e: e.tensor_copy(ub[:, :], rhs[:, :]),
           reads=("rhs",), writes=("ub",))
    Xc, XcT, Xn, XnT = X_sb, XT_sb, X2a, X2b
    xres = {id(X_sb): "X_sb", id(XT_sb): "XT_sb", id(X2a): "X2a", id(X2b): "X2b"}
    for lev in range(7):
        pr.add("pe", (lambda XcT=XcT: lambda e: e.matmul(pU[:, 0:256], XcT[:, :], ub[:, :],
                                                         start=True, stop=True))(),
               reads=(xres[id(XcT)], "ub"), writes=("pU",))
        pr.add("dve", lambda e: e.tensor_add(uf[:, :], uf[:, :], pU[:, 0:256]),
               reads=("uf", "pU"), writes=("uf",))
        pr.add("act", lambda e: e.activation(out=ub[:, :], in_=uf[:, :], func=AF.Copy),
               reads=("uf",), writes=("ub",))
        if lev < 6:
            pr.add("pe", (lambda Xc=Xc, XcT=XcT: lambda e: e.matmul(
                pX2[:, 0:128], XcT[:, :], Xc[:, :], start=True, stop=True))(),
                   reads=(xres[id(XcT)], xres[id(Xc)]), writes=("pX2",))
            pr.add("pe", (lambda Xc=Xc, XcT=XcT: lambda e: e.matmul(
                pX2[:, 128:256], Xc[:, :], XcT[:, :], start=True, stop=True))(),
                   reads=(xres[id(Xc)], xres[id(XcT)]), writes=("pX2",))
            pr.add("act", (lambda Xn=Xn: lambda e: e.activation(
                out=Xn[:, :], in_=pX2[:, 0:128], func=AF.Copy))(),
                   reads=("pX2",), writes=(xres[id(Xn)],))
            pr.add("act", (lambda XnT=XnT: lambda e: e.activation(
                out=XnT[:, :], in_=pX2[:, 128:256], func=AF.Copy))(),
                   reads=("pX2",), writes=(xres[id(XnT)],))
            Xc, XcT, Xn, XnT = Xn, XnT, Xc, XcT
    pr.add("pe", lambda e: e.matmul(pA[:, 0:D1_CT], KTc, QTc, start=True, stop=True),
           reads=("ktf", "qtf"), writes=("pA",))
    pr.add("act", lambda e: e.activation(out=GTs[:, :], in_=pA[:, 0:D1_CT], func=AF.Copy),
           reads=("pA",), writes=("GTs",))
    pr.add("pe", lambda e: e.matmul(pB[:, 0:D1_CT], nd[0:1, :], oner[0:1, 0:D1_CT], start=True, stop=False),
           reads=("nd", "oner"), writes=("pB",))
    pr.add("pe", lambda e: e.matmul(pB[:, 0:D1_CT], oner[0:1, :], dqrow[0:1, :], start=False, stop=True),
           reads=("oner", "dqrow"), writes=("pB",))
    pr.add("dve", lambda e: e.tensor_add(R2s[:, :], pB[:, 0:D1_CT], mq[:, :]),
           reads=("pB", "mq"), writes=("R2s",))
    pr.add("act", lambda e: e.activation(out=R2s[:, :], in_=R2s[:, :], func=AF.Exp),
           reads=("R2s",), writes=("R2s",))
    pr.add("dve", lambda e: e.tensor_mul(WTs[:, :], R2s[:, :], GTs[:, :]),
           reads=("R2s", "GTs"), writes=("WTs",))
    if c > 0:
        pr.add("dve", (lambda: lambda e: e.tensor_single_scalar(
            bqr[0:1, :], dqrow[0:1, :], tiny[0:1, 0:1], ALU.add))(),
               reads=("dqrow", "tiny"), writes=("bqr",))
    else:
        pr.add("dve", lambda e: e.tensor_copy(bqr[0:1, :], dqrow[0:1, :]),
               reads=("dqrow",), writes=("bqr",))
    pr.add("act", lambda e: e.activation(out=bqr[0:1, :], in_=bqr[0:1, :], func=AF.Exp),
           reads=("bqr",), writes=("bqr",))
    pr.add("pe", lambda e: e.matmul(pB[:, 0:D1_CT], oner[0:1, :], bqr[0:1, :], start=True, stop=True),
           reads=("oner", "bqr"), writes=("pB",))
    pr.add("dve", lambda e: e.tensor_mul(tzs[:, :], QTc, pB[:, 0:D1_CT]),
           reads=("qtf", "pB"), writes=("tzs",))
    pr.add("pe", lambda e: e.matmul(pS[:, 0:D1_CT], ub[:, 128:256], WTs[:, :], start=True, stop=True),
           reads=("ub", "WTs"), writes=("pS",))
    pr.add("dve", lambda e: e.tensor_sub(ZTs[:, :], tzs[:, :], pS[:, 0:D1_CT]),
           reads=("tzs", "pS"), writes=("ZTs",))
    pr.add("pe", lambda e: e.matmul(pU[0:64, 0:128], WTs[:, :], ub[:, 0:128], start=True, stop=False),
           reads=("WTs", "ub"), writes=("pU",))
    pr.add("pe", lambda e: e.matmul(pU[0:64, 0:128], ZTs[:, :], Sbf[:, :], start=False, stop=True),
           reads=("ZTs", "Sbf"), writes=("pU",))
    tile = c // 2
    off = (c % 2) * 64
    pr.add("act", (lambda tile=tile, off=off: lambda e: e.activation(
        out=otile[off:off + 64, hd * 128:(hd + 1) * 128], in_=pU[0:64, 0:128], func=AF.Copy))(),
           reads=("pU",), writes=("otile",))
    if c % 2 == 1:
        pr.dma("pool", h["o_s"][b, tile * 128:(tile + 1) * 128, hd * 128:(hd + 1) * 128],
               otile[:, hd * 128:(hd + 1) * 128], reads=("otile",), writes=(f"os{b}",))
    pr.add("pe", lambda e: e.matmul(pS[:, 0:128], khat[:, :], ub[:, 0:128], start=True, stop=True),
           reads=("khat", "ub"), writes=("pS",))
    pr.add("act", lambda e: e.activation(out=Psb[:, :], in_=pS[:, 0:128], func=AF.Copy),
           reads=("pS",), writes=("Psb",))
    pr.add("pe", lambda e: e.matmul(pS[:, 0:128], ub[:, 128:256], khat[:, :], start=True, stop=True),
           reads=("ub", "khat"), writes=("pS",))
    pr.add("dve", lambda e: e.scalar_tensor_tensor(Msb[:, :], Icon[:, :], bcol[:, 2:3],
                                                   pS[:, 0:128], ALU.mult, ALU.subtract),
           reads=("Icon", "bcol", "pS"), writes=("Msb",))
    pr.add("pe", lambda e: e.matmul(pS[:, 0:128], Msb[:, :], Sbf[:, :], start=True, stop=True),
           reads=("Msb", "Sbf"), writes=("pS",))
    pr.add("dve", lambda e: e.tensor_add(Sf[:, :], pS[:, 0:128], Psb[:, :]),
           reads=("pS", "Psb"), writes=("Sf",))
    pr.add("act", lambda e: e.activation(out=Sbf[:, :], in_=Sf[:, :], func=AF.Copy),
           reads=("Sf",), writes=("Sbf",))


def _endphase(pr, nc, L, b):
    (otile, gtile, oT_t, xr, ysb, sqb, st1, rq, nwc, idf, oner, Icon, wsl) = (
        L["otile"], L["gtile"], L["oT_t"], L["xr"], L["ysb"], L["sqb"], L["st1"],
        L["rq"], L["nwc"], L["idf"], L["oner"], L["Icon"], L["wsl"])
    mybir_local = None
    pA, pB = L["pA"], L["pB"]
    h = L["h"]
    x2, x3 = h["x2"], h["x3"]
    base = b * 2048
    wosl = L["wosl"]
    for kt in range(8):
        pr.dma("pool", wosl[:, kt * 2048:(kt + 1) * 2048], h["wo_t"][kt], writes=("xTb",))
        pr.add("dve", (lambda kt=kt: lambda e: e.tensor_scalar_mul(
            wosl[:, kt * 2048:(kt + 1) * 2048], wosl[:, kt * 2048:(kt + 1) * 2048],
            nwc[:, 0:1]))(),
               reads=("xTb", "nwc"), writes=("xTb",))
    for tt in range(16):
        pr.dma("sp", otile[:, :], h["o_s"][b, tt * 128:(tt + 1) * 128, :],
               reads=(f"os{b}",), writes=("otile",))
        pr.dma("sp", gtile[:, :], h["g_s"][b, tt * 128:(tt + 1) * 128, :],
               reads=(f"gs{b}",), writes=("gtile",))
        pr.dma("sp", xr[:, :], x2[base + tt * 128: base + (tt + 1) * 128, :], writes=("xr",))
        pr.add("act", lambda e: e.activation(out=sqb[:, 0:1024], in_=otile[:, :], func=AF.Square),
               reads=("otile",), writes=("sqb",))
        for hd in range(8):
            pr.add("dve", (lambda hd=hd: lambda e: e.reduce_sum(
                rq[:, hd:hd + 1], sqb[:, hd * 128:(hd + 1) * 128], axis=AX.X))(),
                   reads=("sqb",), writes=("rq",))
        pr.add("dve", lambda e: e.tensor_scalar(rq[:, :], rq[:, :], 1.0 / 128, float(D1_EPS_O),
                                                ALU.mult, ALU.add),
               reads=("rq",), writes=("rq",))
        pr.add("act", lambda e: e.activation(out=rq[:, :], in_=rq[:, :], func=AF.Ln),
               reads=("rq",), writes=("rq",))
        pr.add("dve", lambda e: e.tensor_scalar_mul(rq[:, :], rq[:, :], -0.5),
               reads=("rq",), writes=("rq",))
        pr.add("act", lambda e: e.activation(out=rq[:, :], in_=rq[:, :], func=AF.Exp),
               reads=("rq",), writes=("rq",))
        for hd in range(8):
            pr.add("dve", (lambda hd=hd: lambda e: e.tensor_scalar_mul(
                otile[:, hd * 128:(hd + 1) * 128], otile[:, hd * 128:(hd + 1) * 128],
                rq[:, hd:hd + 1]))(),
                   reads=("otile", "rq"), writes=("otile",))
        pr.add("dve", lambda e: e.tensor_mul(otile[:, :], otile[:, :], gtile[:, :]),
               reads=("otile", "gtile"), writes=("otile",))
        for hd in range(8):
            pr.add("pe", (lambda hd=hd: lambda e: e.transpose(
                L["pXb"][:, :], otile[:, hd * 128:(hd + 1) * 128], Icon[:, :]))(),
                   reads=("otile", "Icon"), writes=("pXb",))
            pr.add("act", (lambda hd=hd: lambda e: e.activation(
                out=oT_t[:, hd * 128:(hd + 1) * 128], in_=L["pXb"][:, :], func=AF.Copy))(),
                   reads=("pXb",), writes=("oT_t",))
        for och in range(4):
            for kt in range(8):
                pr.add("pe", (lambda och=och, kt=kt: lambda e: e.matmul(
                    pA[:, :], oT_t[:, kt * 128:(kt + 1) * 128],
                    wosl[:, kt * 2048 + och * 512: kt * 2048 + (och + 1) * 512],
                    start=(kt == 0), stop=(kt == 7)))(),
                       reads=("oT_t", "xTb"), writes=("pA",))
            pr.add("act", (lambda och=och: lambda e: e.activation(
                out=ysb[:, och * 512:(och + 1) * 512], in_=pA[:, :], func=AF.Copy))(),
                   reads=("pA",), writes=("qraw",))
        pr.add("dve", lambda e: e.tensor_add(xr[:, :], ysb[:, :], xr[:, :]),
               reads=("qraw", "xr"), writes=("xr",))
        emit_ln2x(pr, nc, xr, "xr", sqb, st1, ysb, "qraw", 1e-5, 2048, sq_res="sqb")
        pr.dma("pool", x3[base + tt * 128: base + (tt + 1) * 128, :], ysb[:, :], reads=("qraw",))




F1_D = 2048
F1_EPS_RMS = 1e-6
F1_EPS_LN = 1e-5


def build_l1ffn(pr, nc, h, name="f1", T=4096, skip_dequant=False):
    es = h["es"]
    sb = lambda nm, shp, dt: es.enter_context(nc.sbuf_tensor(f"{name}_{nm}", shp, dt))
    ps = lambda nm, shp, dt: es.enter_context(nc.psum_tensor(f"{name}_{nm}", shp, dt))
    TC = 512
    NCH = T // TC
    NGU = 88
    NDT = 44
    KT = 16

    x3, xout = h["x3"], h["xout"]

    pk = sb("pk", [128, 704], I32)
    shi = sb("shi", [128, 704], I32)
    mi = sb("mi", [128, 704], I32)
    d8 = sb("d8", [128, 704], I32)
    mh = sb("mh", [128, 704], F32)
    sgn = sb("sgn", [128, 704], F32)
    r1 = sb("r1", [128, 704], F32)
    r2 = sb("r2", [128, 704], F32)
    tsum = sb("tsum", [128, 704], F32)
    slab = sb("slab", [128, 5632], F32)
    slabT = sb("slabT", [128, F1_D], BF16)
    sgu = sb("sgu", [128, NGU * 16], F32)
    sdw = sb("sdw", [128, 16 * NDT], F32)
    idf = sb("idf", [128, 128], F32)

    xa = [sb(f"xa{i}", [128, F1_D], F32) for i in range(2)]
    xTc = sb("xTc", [128, KT * TC], BF16)
    wg_sb = [sb(f"wg{i}", [128, KT * 128], BF16) for i in range(2)]
    wd_sb = [sb(f"wd{i}", [128, 1024], BF16) for i in range(2)]
    tg = sb("tg", [128, TC], F32)
    tu = sb("tu", [128, TC], F32)
    fuse = sb("fuse", [128, NDT * TC], BF16)
    ysb = [sb(f"ysb{i}", [128, F1_D], F32) for i in range(2)]
    st1 = [sb(f"st1{i}", [128, 8], F32) for i in range(2)]
    sqb = sb("sqb", [128, F1_D], BF16)

    pg = ps("pg", [128, 512], F32)
    pu = ps("pu", [128, 512], F32)
    pdb = [ps(f"pdb{i}", [128, 512], F32) for i in range(4)]
    ptr = ps("ptr", [128, 512], F32)

    pr.dma("sp", sgu[:, :], h["sgu_c"][:, :], writes=("sgu",))
    pr.dma("sp", sdw[:, :], h["sdw_c"][:, :], writes=("sdw",))
    pr.dma("sp", idf[:, :], h["identf"][:, :], writes=("idf",))

    def dequant(W):
        for s in range(8):
            pr.add("dve", (lambda s=s, W=W: lambda e: e.tensor_single_scalar(
                shi[:, 0:W], pk[:, 0:W], s * 4, ALU.logical_shift_right))(),
                   reads=("pk",), writes=("shi",))
            pr.add("dve", (lambda W=W: lambda e: e.tensor_single_scalar(
                mi[:, 0:W], shi[:, 0:W], 7, ALU.bitwise_and))(),
                   reads=("shi",), writes=("mi",))
            pr.add("dve", (lambda W=W: lambda e: e.tensor_single_scalar(
                d8[:, 0:W], shi[:, 0:W], 8, ALU.bitwise_and))(),
                   reads=("shi",), writes=("d8",))
            pr.add("dve", (lambda W=W: lambda e: e.tensor_scalar_mul(
                mh[:, 0:W], mi[:, 0:W], 0.5))(),
                   reads=("mi",), writes=("mh",))
            pr.add("dve", (lambda W=W: lambda e: e.tensor_scalar(
                sgn[:, 0:W], d8[:, 0:W], -0.25, 1.0, ALU.mult, ALU.add))(),
                   reads=("d8",), writes=("sgn",))
            pr.add("dve", (lambda W=W: lambda e: e.tensor_scalar(
                r1[:, 0:W], mh[:, 0:W], -2.0, 0.0, ALU.add, ALU.max))(),
                   reads=("mh",), writes=("r1",))
            pr.add("dve", (lambda W=W: lambda e: e.tensor_scalar(
                r2[:, 0:W], mh[:, 0:W], -3.0, 0.0, ALU.add, ALU.max))(),
                   reads=("mh",), writes=("r2",))
            pr.add("dve", (lambda W=W: lambda e: e.scalar_tensor_tensor(
                tsum[:, 0:W], r2[:, 0:W], 2.0, r1[:, 0:W], ALU.mult, ALU.add))(),
                   reads=("r1", "r2"), writes=("tsum",))
            pr.add("dve", (lambda W=W: lambda e: e.tensor_add(
                tsum[:, 0:W], tsum[:, 0:W], mh[:, 0:W]))(),
                   reads=("tsum", "mh"), writes=("tsum",))
            pr.add("dve", (lambda s=s, W=W: lambda e: e.tensor_mul(
                slab[:, s:8 * W:8], tsum[:, 0:W], sgn[:, 0:W]))(),
                   reads=("tsum", "sgn"), writes=("slab",))

    for ot in range(NGU if not skip_dequant else 0):
        pr.dma("sp", pk[:, 0:256], h["gup_t"][ot], writes=("pk",))
        dequant(256)
        for kt in range(KT):
            pr.add("dve", (lambda ot=ot, kt=kt: lambda e: e.tensor_scalar_mul(
                slab[:, kt * 128:(kt + 1) * 128], slab[:, kt * 128:(kt + 1) * 128],
                sgu[:, ot * 16 + kt: ot * 16 + kt + 1]))(),
                   reads=("slab", "sgu"), writes=("slab",))
        for kt in range(KT):
            pr.add("pe", (lambda kt=kt: lambda e: e.transpose(
                ptr[:, 0:128], slab[:, kt * 128:(kt + 1) * 128], idf[:, :]))(),
                   reads=("slab", "idf"), writes=("ptr",))
            pr.add("act", (lambda kt=kt: lambda e: e.activation(
                out=slabT[:, kt * 128:(kt + 1) * 128], in_=ptr[:, 0:128], func=AF.Copy))(),
                   reads=("ptr",), writes=("slabT",))
        pr.dma("pool", h["wgu_s"][ot], slabT[:, :], reads=("slabT",), writes=(f"wgu{ot}",))

    for ot in range(16 if not skip_dequant else 0):
        pr.dma("sp", pk[:, 0:704], h["wdp_t"][ot], writes=("pk",))
        dequant(704)
        for g in range(NDT):
            pr.add("dve", (lambda ot=ot, g=g: lambda e: e.tensor_scalar_mul(
                slab[:, g * 128:(g + 1) * 128], slab[:, g * 128:(g + 1) * 128],
                sdw[:, ot * NDT + g: ot * NDT + g + 1]))(),
                   reads=("slab", "sdw"), writes=("slab",))
        for g in range(NDT):
            pr.add("pe", (lambda g=g: lambda e: e.transpose(
                ptr[:, 0:128], slab[:, g * 128:(g + 1) * 128], idf[:, :]))(),
                   reads=("slab", "idf"), writes=("ptr",))
            pr.add("act", (lambda g=g: lambda e: e.activation(
                out=slabT[:, 0:128], in_=ptr[:, 0:128], func=AF.Copy))(),
                   reads=("ptr",), writes=("slabT",))
            pr.dma("pool", h["wdt_s"][g][:, ot * 128:(ot + 1) * 128], slabT[:, 0:128],
                   reads=("slabT",), writes=(f"wdt{g}",))

    for c in range(NCH):
        for tt in range(4):
            trow = c * TC + tt * 128
            pr.dma("sp", xa[0][:, :], x3[trow:trow + 128, :], writes=("xa0",))
            for kt in range(KT):
                pr.add("pe", (lambda kt=kt: lambda e: e.transpose(
                    ptr[:, 0:128], xa[0][:, kt * 128:(kt + 1) * 128], idf[:, :]))(),
                       reads=("xa0", "idf"), writes=("ptr",))
                pr.add("act", (lambda kt=kt, tt=tt: lambda e: e.activation(
                    out=xTc[:, kt * TC + tt * 128: kt * TC + (tt + 1) * 128],
                    in_=ptr[:, 0:128], func=AF.Copy))(),
                       reads=("ptr",), writes=("xTc",))
        for j in range(NDT):
            bw = j % 2
            pr.dma("sp", wg_sb[bw][:, :], h["wgu_s"][j], reads=(f"wgu{j}",), writes=(f"wg{bw}",))
            for kt in range(KT):
                pr.add("pe", (lambda bw=bw, kt=kt: lambda e: e.matmul(
                    pg[:, :], wg_sb[bw][:, kt * 128:(kt + 1) * 128],
                    xTc[:, kt * TC:(kt + 1) * TC],
                    start=(kt == 0), stop=(kt == KT - 1)))(),
                       reads=(f"wg{bw}", "xTc"), writes=("pg",))
            pr.dma("sp", wg_sb[bw][:, :], h["wgu_s"][j + NDT], reads=(f"wgu{j + NDT}",), writes=(f"wg{bw}",))
            for kt in range(KT):
                pr.add("pe", (lambda bw=bw, kt=kt: lambda e: e.matmul(
                    pu[:, :], wg_sb[bw][:, kt * 128:(kt + 1) * 128],
                    xTc[:, kt * TC:(kt + 1) * TC],
                    start=(kt == 0), stop=(kt == KT - 1)))(),
                       reads=(f"wg{bw}", "xTc"), writes=("pu",))
            pr.add("act", lambda e: e.activation(out=tg[:, :], in_=pg[:, :], func=AF.Silu),
                   reads=("pg",), writes=("tg",))
            pr.add("act", lambda e: e.activation(out=tu[:, :], in_=pu[:, :], func=AF.Copy),
                   reads=("pu",), writes=("tu",))
            pr.add("dve", (lambda j=j: lambda e: e.tensor_mul(
                fuse[:, j * TC:(j + 1) * TC], tg[:, :], tu[:, :]))(),
                   reads=("tg", "tu"), writes=("fuse",))
        for ttp in range(2):
            xr0 = slab[:, 2048:4096]
            for tti in range(2):
                trow = c * TC + (ttp * 2 + tti) * 128
                if tti == 0:
                    pr.dma("sp", xr0[:, :], x3[trow:trow + 128, :], writes=("slab",))
                else:
                    pr.dma("sp", xa[tti][:, :], x3[trow:trow + 128, :], writes=(f"xa{tti}",))
            for ochp in range(2):
                for it in range(NDT):
                    bw = it % 2
                    pr.dma("pool", wd_sb[bw][:, :],
                           h["wdt_s"][it][:, ochp * 1024:(ochp + 1) * 1024],
                           reads=(f"wdt{it}",), writes=(f"wd{bw}",))
                    for tti in range(2):
                        tt = ttp * 2 + tti
                        for oi in range(2):
                            pr.add("pe", (lambda bw=bw, it=it, tt=tt, tti=tti, oi=oi:
                                          lambda e: e.matmul(
                                pdb[tti * 2 + oi][:, :],
                                fuse[:, it * TC + tt * 128: it * TC + (tt + 1) * 128],
                                wd_sb[bw][:, oi * 512:(oi + 1) * 512],
                                start=(it == 0), stop=(it == NDT - 1)))(),
                                   reads=(f"wd{bw}", "fuse"), writes=(f"pdb{tti * 2 + oi}",))
                for tti in range(2):
                    for oi in range(2):
                        och = ochp * 2 + oi
                        pr.add("act", (lambda tti=tti, oi=oi, och=och: lambda e: e.activation(
                            out=ysb[tti][:, och * 512:(och + 1) * 512],
                            in_=pdb[tti * 2 + oi][:, :], func=AF.Copy))(),
                               reads=(f"pdb{tti * 2 + oi}",), writes=(f"ysb{tti}",))
            for tti in range(2):
                tt = ttp * 2 + tti
                trow = c * TC + tt * 128
                y, xr, stt = ysb[tti], (xr0 if tti == 0 else xa[tti]), st1[tti]
                yres, xres = f"ysb{tti}", ("slab" if tti == 0 else f"xa{tti}")
                pr.add("act", (lambda y=y: lambda e: e.activation(out=sqb[:, :], in_=y[:, :], func=AF.Square))(),
                       reads=(yres,), writes=("sqb",))
                pr.add("dve", (lambda stt=stt: lambda e: e.reduce_sum(stt[:, 0:1], sqb[:, :], axis=AX.X))(),
                       reads=("sqb",), writes=(f"st1{tti}",))
                pr.add("dve", (lambda stt=stt: lambda e: e.tensor_scalar(
                    stt[:, 1:2], stt[:, 0:1], 1.0 / F1_D, float(F1_EPS_RMS), ALU.mult, ALU.add))(),
                       reads=(f"st1{tti}",), writes=(f"st1{tti}",))
                pr.add("act", (lambda stt=stt: lambda e: e.activation(out=stt[:, 2:3], in_=stt[:, 1:2], func=AF.Ln))(),
                       reads=(f"st1{tti}",), writes=(f"st1{tti}",))
                pr.add("dve", (lambda stt=stt: lambda e: e.tensor_scalar_mul(stt[:, 2:3], stt[:, 2:3], -0.5))(),
                       reads=(f"st1{tti}",), writes=(f"st1{tti}",))
                pr.add("act", (lambda stt=stt: lambda e: e.activation(out=stt[:, 3:4], in_=stt[:, 2:3], func=AF.Exp))(),
                       reads=(f"st1{tti}",), writes=(f"st1{tti}",))
                pr.add("dve", (lambda y=y, xr=xr, stt=stt: lambda e: e.scalar_tensor_tensor(
                    xr[:, :], y[:, :], stt[:, 3:4], xr[:, :], ALU.mult, ALU.add))(),
                       reads=(yres, f"st1{tti}", xres), writes=(xres,))
                emit_ln2x(pr, nc, xr, xres, sqb, stt, y, yres, F1_EPS_LN, F1_D, sq_res="sqb")
                pr.dma("pool", xout[trow:trow + 128, :], y[:, :], reads=(yres,))
    return


import numpy as np
import ml_dtypes as _mld
from concourse.bass_utils import run_bass_kernel_spmd

_BF = _mld.bfloat16
_NCORES = 8


def _consts_l0attn():
    DH, S = 128, 2048
    inv_freq = (1.0 / 10000.0 ** (np.arange(0, DH, 2) / DH)).astype(np.float32)
    ang = np.outer(np.arange(S), inv_freq)
    cosT = np.ascontiguousarray(np.cos(ang).T).astype(np.float32)
    sinT = np.ascontiguousarray(np.sin(ang).T).astype(np.float32)
    costab = np.concatenate([cosT, cosT], axis=0).astype(_BF)
    sintab = np.concatenate([-sinT, sinT], axis=0).astype(_BF)
    pswap = np.zeros((128, 128), np.float32)
    for kk in range(128):
        pswap[kk, (kk + 64) % 128] = 1.0
    r = np.arange(128)
    cmask = np.where(r[None, :] <= r[:, None], 0.0, -1e30).astype(np.float32).astype(_BF)
    cmaskT = np.zeros((4, 128, 512), np.float32)
    for kk in range(4):
        cmaskT[kk, :, :kk * 128] = -1e30
        blk = np.where(r[:, None] <= r[None, :], 0.0, -1e30)
        cmaskT[kk, :, kk * 128:(kk + 1) * 128] = blk
    cmaskT = np.ascontiguousarray(cmaskT.transpose(1, 0, 2).reshape(128, 2048)).astype(_BF)
    return dict(costab=costab, sintab=sintab, pswap=pswap.astype(_BF),
                cmask=cmask, cmaskT=cmaskT,
                identf=np.eye(128, dtype=np.float32),
                ones_col_f=np.ones((128, 1), np.float32),
                ones_col_b=np.ones((128, 1), dtype=_BF),
                ones_row_f=np.ones((1, 128), np.float32),
                ones_row_b=np.ones((1, 128), dtype=_BF))


def kernel(**inp):
    x = np.ascontiguousarray(np.asarray(inp["L0_attn_x"], dtype=np.float32))
    B, S, D = x.shape
    shards = [np.ascontiguousarray(x[2 * c:2 * c + 2].reshape(2 * S, D)) for c in range(_NCORES)]
    f32 = np.float32
    asf = lambda a: np.ascontiguousarray(np.asarray(a, dtype=f32))

    cA = _consts_l0attn()
    wq = asf(inp["L0_attn_wq"]); wk = asf(inp["L0_attn_wk"])
    wv = asf(inp["L0_attn_wv"]); wo = asf(inp["L0_attn_wo"])
    packA = dict(
        wq_t=np.ascontiguousarray(wq.reshape(16, 128, 16, 128).transpose(2, 1, 0, 3).reshape(16, 128, 2048)),
        wk_t=np.ascontiguousarray(wk.reshape(16, 128, 4, 128).transpose(2, 1, 0, 3).reshape(4, 128, 2048)),
        wv_t=np.ascontiguousarray(wv.reshape(16, 128, 4, 128).transpose(2, 1, 0, 3).reshape(4, 128, 2048)),
        wo_t=np.ascontiguousarray(wo.reshape(16, 128, 2048)),
        bq_c=np.ascontiguousarray(asf(inp["L0_attn_bq"]).reshape(16, 128).T),
        bk_c=np.ascontiguousarray(asf(inp["L0_attn_bk"]).reshape(4, 128).T),
        bv_c=np.ascontiguousarray(asf(inp["L0_attn_bv"]).reshape(4, 128).T),
        bo_r=asf(inp["L0_attn_bo"]).reshape(1, 2048),
        qns_c=asf(inp["L0_attn_q_norm_scale"]).reshape(128, 1),
        kns_c=asf(inp["L0_attn_k_norm_scale"]).reshape(128, 1),
        **cA)
    nc = bass.Bass("TRN2", target_bir_lowering=False)
    hh = {}
    hh["x0"] = nc.dram_tensor("x0", [2 * S, D], F32, kind="ExternalInput").ap()
    for nm, shp, dt in [("wq_t", [16, 128, 2048], F32), ("wk_t", [4, 128, 2048], F32),
                        ("wv_t", [4, 128, 2048], F32), ("wo_t", [16, 128, 2048], F32),
                        ("bq_c", [128, 16], F32), ("bk_c", [128, 4], F32), ("bv_c", [128, 4], F32),
                        ("bo_r", [1, 2048], F32), ("qns_c", [128, 1], F32), ("kns_c", [128, 1], F32),
                        ("costab", [128, 2048], BF16), ("sintab", [128, 2048], BF16),
                        ("pswap", [128, 128], BF16), ("cmask", [128, 128], BF16),
                        ("cmaskT", [128, 2048], BF16), ("identf", [128, 128], F32),
                        ("ones_col_f", [128, 1], F32), ("ones_col_b", [128, 1], BF16),
                        ("ones_row_f", [1, 128], F32), ("ones_row_b", [1, 128], BF16)]:
        hh[nm] = nc.dram_tensor(nm, shp, dt, kind="ExternalInput").ap()
    hh["x1"] = nc.dram_tensor("x1", [2 * S, D], F32, kind="ExternalOutput").ap()
    import contextlib
    with contextlib.ExitStack() as es:
        hh["es"] = es
        pr = Prog(nc)
        build_l0attn(pr, nc, hh)
        pr.build_and_emit()
    in_maps = [dict(x0=shards[c], **packA) for c in range(_NCORES)]
    res = run_bass_kernel_spmd(nc, in_maps, core_ids=list(range(_NCORES)))
    x1s = [res.results[c]["x1"] for c in range(_NCORES)]

    w_gate = np.asarray(inp["L0_ffn_w_gate"])
    w_up = np.asarray(inp["L0_ffn_w_up"])
    w_down = np.asarray(inp["L0_ffn_w_down"])
    packB = dict(
        wg_t=np.ascontiguousarray(w_gate.reshape(16, 128, 64, 128).transpose(2, 1, 0, 3).reshape(64, 128, 2048)),
        wu_t=np.ascontiguousarray(w_up.reshape(16, 128, 64, 128).transpose(2, 1, 0, 3).reshape(64, 128, 2048)),
        wd_t=np.ascontiguousarray(w_down.reshape(64, 128, 16, 128).transpose(2, 1, 0, 3).reshape(16, 128, 8192)),
        sg_c=np.ascontiguousarray(asf(inp["L0_ffn_s_gate"]).reshape(64, 128).T),
        su_c=np.ascontiguousarray(asf(inp["L0_ffn_s_up"]).reshape(64, 128).T),
        sd_c=np.ascontiguousarray(asf(inp["L0_ffn_s_down"]).reshape(16, 128).T),
        identf=np.eye(128, dtype=np.float32))
    nc = bass.Bass("TRN2", target_bir_lowering=False)
    hh = {}
    hh["x1"] = nc.dram_tensor("x1", [2 * S, D], F32, kind="ExternalInput").ap()
    for nm, shp, dt in [("wg_t", [64, 128, 2048], I8), ("wu_t", [64, 128, 2048], I8),
                        ("wd_t", [16, 128, 8192], I8), ("sg_c", [128, 64], F32),
                        ("su_c", [128, 64], F32), ("sd_c", [128, 16], F32),
                        ("identf", [128, 128], F32)]:
        hh[nm] = nc.dram_tensor(nm, shp, dt, kind="ExternalInput").ap()
    hh["x2"] = nc.dram_tensor("x2", [2 * S, D], F32, kind="ExternalOutput").ap()
    with contextlib.ExitStack() as es:
        hh["es"] = es
        hh["x2_out"] = hh["x2"]
        pr = Prog(nc)
        h2 = dict(hh)
        h2["x2"] = hh["x2"]
        build_l0ffn(pr, nc, dict(hh, x1=hh["x1"], x2=hh["x2"]))
        pr.build_and_emit()
    in_maps = [dict(x1=x1s[c], **packB) for c in range(_NCORES)]
    res = run_bass_kernel_spmd(nc, in_maps, core_ids=list(range(_NCORES)))
    x2s = [res.results[c]["x2"] for c in range(_NCORES)]

    Wq = asf(inp["L1_attn_Wq"]); Wk = asf(inp["L1_attn_Wk"])
    Wv = asf(inp["L1_attn_Wv"]); Wg = asf(inp["L1_attn_Wg"])
    Wb = asf(inp["L1_attn_Wb"]); Wa = asf(inp["L1_attn_Wa"])
    Wo2 = asf(inp["L1_attn_Wo"])
    cq = asf(inp["L1_attn_conv_q_w"]); ck = asf(inp["L1_attn_conv_k_w"]); cv = asf(inp["L1_attn_conv_v_w"])
    r = np.arange(128)
    sarr = np.arange(64)
    packC = dict(
        wq_t=np.ascontiguousarray(Wq.reshape(16, 128, 8, 128).transpose(2, 1, 0, 3).reshape(8, 128, 2048)),
        wk_t=np.ascontiguousarray(Wk.reshape(16, 128, 16, 128).transpose(2, 1, 0, 3).reshape(16, 128, 2048)),
        wv_t=np.ascontiguousarray(Wv.reshape(16, 128, 16, 128).transpose(2, 1, 0, 3).reshape(16, 128, 2048)),
        wg_t=np.ascontiguousarray(Wg.reshape(16, 128, 8, 128).transpose(2, 1, 0, 3).reshape(8, 128, 2048)),
        wb_c=np.ascontiguousarray(Wb.reshape(16, 128, 16).transpose(1, 0, 2).reshape(128, 256)),
        wa_c=np.ascontiguousarray(Wa.reshape(16, 128, 8).transpose(1, 0, 2).reshape(128, 128)),
        wo_t=np.ascontiguousarray(Wo2.reshape(8, 128, 2048)),
        nw_c=asf(inp["L1_attn_norm_w"]).reshape(128, 1),
        alog_c=asf(inp["L1_attn_A_log"]).reshape(8, 1),
        dtb_c=asf(inp["L1_attn_dt_bias"]).reshape(8, 1),
        cq_c=np.ascontiguousarray(cq[:, 0, :].reshape(8, 128, 4).transpose(1, 0, 2).reshape(128, 32)),
        ck_c=np.ascontiguousarray(ck[:, 0, :].reshape(16, 128, 4).transpose(1, 0, 2).reshape(128, 64)),
        cv_c=np.ascontiguousarray(cv[:, 0, :].reshape(16, 128, 4).transpose(1, 0, 2).reshape(128, 64)),
        identf=np.eye(128, dtype=np.float32),
        ones_row_f=np.ones((1, 128), np.float32), one_f=np.ones((1, 1), np.float32),
        mx_c=np.where(r[:, None] < r[None, :], 0.0, -1e30).astype(np.float32),
        mq_c=np.where(r[:, None] <= (2 * sarr + 1)[None, :], 0.0, -1e30).astype(np.float32),
        ident_bf=np.eye(128, dtype=_BF), one_b=np.ones((1, 1), dtype=_BF),
        ones_col_b=np.ones((128, 1), dtype=_BF))
    nc = bass.Bass("TRN2", target_bir_lowering=False)
    hh = {}
    hh["x2"] = nc.dram_tensor("x2", [2 * S, D], F32, kind="ExternalInput").ap()
    for nm, shp, dt in [("wq_t", [8, 128, 2048], F32), ("wk_t", [16, 128, 2048], F32),
                        ("wv_t", [16, 128, 2048], F32), ("wg_t", [8, 128, 2048], F32),
                        ("wb_c", [128, 256], F32), ("wa_c", [128, 128], F32),
                        ("wo_t", [8, 128, 2048], F32), ("nw_c", [128, 1], F32),
                        ("alog_c", [8, 1], F32), ("dtb_c", [8, 1], F32),
                        ("cq_c", [128, 32], F32), ("ck_c", [128, 64], F32), ("cv_c", [128, 64], F32),
                        ("identf", [128, 128], F32), ("ones_row_f", [1, 128], F32),
                        ("one_f", [1, 1], F32), ("mx_c", [128, 128], F32), ("mq_c", [128, 64], F32),
                        ("ident_bf", [128, 128], BF16), ("one_b", [1, 1], BF16),
                        ("ones_col_b", [128, 1], BF16)]:
        hh[nm] = nc.dram_tensor(nm, shp, dt, kind="ExternalInput").ap()
    hh["o_s"] = nc.dram_tensor("o_s", [2, 2048, 1024], BF16, kind="Internal").ap()
    hh["g_s"] = nc.dram_tensor("g_s", [2, 2048, 1024], BF16, kind="Internal").ap()
    hh["x3"] = nc.dram_tensor("x3", [2 * S, D], F32, kind="ExternalOutput").ap()
    with contextlib.ExitStack() as es:
        hh["es"] = es
        pr = Prog(nc)
        build_l1attn(pr, nc, hh)
        pr.build_and_emit()
    in_maps = [dict(x2=x2s[c], **packC) for c in range(_NCORES)]
    res = run_bass_kernel_spmd(nc, in_maps, core_ids=list(range(_NCORES)))
    x3s = [res.results[c]["x3"] for c in range(_NCORES)]

    gup = np.asarray(inp["L1_ffn_gate_up_packed"])
    gus = np.asarray(inp["L1_ffn_gate_up_scales"])
    dnp_ = np.asarray(inp["L1_ffn_down_packed"])
    dns = np.asarray(inp["L1_ffn_down_scales"])
    packD = dict(
        gup_t=np.ascontiguousarray(gup.reshape(88, 128, 256)),
        sgu_c=np.ascontiguousarray(gus.astype(np.float32).T.reshape(88, 128, 16).transpose(1, 0, 2).reshape(128, 88 * 16)),
        wdp_t=np.ascontiguousarray(dnp_.reshape(16, 128, 704)),
        sdw_c=np.ascontiguousarray(dns.astype(np.float32).T.reshape(16, 128, 44).transpose(1, 0, 2).reshape(128, 16 * 44)),
        identf=np.eye(128, dtype=np.float32))
    nc = bass.Bass("TRN2", target_bir_lowering=False)
    hh = {}
    hh["x3"] = nc.dram_tensor("x3", [2 * S, D], F32, kind="ExternalInput").ap()
    for nm, shp, dt in [("gup_t", [88, 128, 256], I32), ("sgu_c", [128, 88 * 16], F32),
                        ("wdp_t", [16, 128, 704], I32), ("sdw_c", [128, 16 * 44], F32),
                        ("identf", [128, 128], F32)]:
        hh[nm] = nc.dram_tensor(nm, shp, dt, kind="ExternalInput").ap()
    hh["wgu_s"] = nc.dram_tensor("wgu_s", [88, 128, 2048], BF16, kind="Internal").ap()
    hh["wdt_s"] = nc.dram_tensor("wdt_s", [44, 128, 2048], BF16, kind="Internal").ap()
    hh["xout"] = nc.dram_tensor("xout", [2 * S, D], F32, kind="ExternalOutput").ap()
    with contextlib.ExitStack() as es:
        hh["es"] = es
        pr = Prog(nc)
        build_l1ffn(pr, nc, hh)
        pr.build_and_emit()
    in_maps = [dict(x3=x3s[c], **packD) for c in range(_NCORES)]
    res = run_bass_kernel_spmd(nc, in_maps, core_ids=list(range(_NCORES)))
    outs = [res.results[c]["xout"] for c in range(_NCORES)]

    out = np.concatenate([o.reshape(2, S, D) for o in outs], axis=0)
    return np.ascontiguousarray(out.astype(np.float32))
